# version 07
# baseline (speedup 1.0000x reference)
import numpy as np
import ml_dtypes
from contextlib import ExitStack

import concourse.bass as bass
import concourse.mybir as mybir
from concourse.bass_utils import run_bass_kernel_spmd

F32 = mybir.dt.float32
BF16 = mybir.dt.bfloat16
AX = mybir.AxisListType.X
ALU = mybir.AluOpType
AF = mybir.ActivationFunctionType
BF = ml_dtypes.bfloat16

D = 2048
TI = 8
DC = 16
NH = 16
DH = 128
KVLEN = 4096
FH = 8192
NHC = 64
SCALE = 1.0 / np.sqrt(128.0)
EPS_LN = 1e-5
NEG = -1.0e30

ENGS = ("sync", "tensor", "vector", "scalar", "gpsimd")


def default_cfg():
    return dict(
        n_heads=NH,
        n_tiles=TI,
        n_hc=NHC,
        sublayers=(1,),
        zero_arenas=False,
    )


class Em:

    def __init__(self, me, eng, sems):
        self.me = me
        self.eng = eng
        self.S = sems
        self.C = {k: 0 for k in sems}

    def w(self, owner, sem, val):
        if val > 0 and self.me == owner:
            self.eng.wait_ge(self.S[sem], val)

    def do(self, owner, sem, f, n=1):
        if sem is not None:
            self.C[sem] += n
        if self.me == owner:
            inst = f(self.eng)
            if sem is not None:
                inst.then_inc(self.S[sem], n)
        return self.C[sem] if sem is not None else None

    def do0(self, owner, f):
        if self.me == owner:
            f(self.eng)


def build_program(cfg):
    nc = bass.Bass("TRN2", target_bir_lowering=False, debug=False)

    dX = nc.dram_tensor("x0", [TI * 128, D], F32, kind="ExternalInput").ap()
    dKT = nc.dram_tensor("kvkT", [D, KVLEN], F32, kind="ExternalInput").ap()
    dVB = nc.dram_tensor("kvv_b", [NH * 128, KVLEN], F32, kind="ExternalInput").ap()
    dWQA = nc.dram_tensor("wqaT", [D, D], F32, kind="ExternalInput").ap()
    dWOA = nc.dram_tensor("woaT", [D, D], F32, kind="ExternalInput").ap()
    dWQB = nc.dram_tensor("wqb", [D, D], F32, kind="ExternalInput").ap()
    dWKB = nc.dram_tensor("wkb", [D, D], F32, kind="ExternalInput").ap()
    dWVB = nc.dram_tensor("wvb", [D, D], F32, kind="ExternalInput").ap()
    dWOB = nc.dram_tensor("wob", [D, D], F32, kind="ExternalInput").ap()
    dBQ = nc.dram_tensor("bqb", [128, 16], F32, kind="ExternalInput").ap()
    dBK = nc.dram_tensor("bkb", [128, 16], F32, kind="ExternalInput").ap()
    dBV = nc.dram_tensor("bvb", [128, 16], F32, kind="ExternalInput").ap()
    dBO = nc.dram_tensor("bob", [1, D], F32, kind="ExternalInput").ap()
    dLNG = nc.dram_tensor("lng", [128, 16], F32, kind="ExternalInput").ap()
    dLNB = nc.dram_tensor("lnb", [128, 16], F32, kind="ExternalInput").ap()
    dWG = nc.dram_tensor("wgT_r", [FH, D], F32, kind="ExternalInput").ap()
    dWU = nc.dram_tensor("wuT_r", [FH, D], F32, kind="ExternalInput").ap()
    dWD = nc.dram_tensor("wdT_r", [FH, D], F32, kind="ExternalInput").ap()
    dMA = nc.dram_tensor("maskA", [128, 128], F32, kind="ExternalInput").ap()
    dMB = nc.dram_tensor("maskB", [128, 128], F32, kind="ExternalInput").ap()
    dID = nc.dram_tensor("ident", [128, 128], BF16, kind="ExternalInput").ap()
    dONES = nc.dram_tensor("ones1", [1, 128], BF16, kind="ExternalInput").ap()
    dY = nc.dram_tensor("y", [TI * 128, D], F32, kind="ExternalOutput").ap()

    dKSEND = nc.dram_tensor("ksend", [D, 1024], BF16).ap()
    dKRECV = nc.dram_tensor("krecv", [2 * D, 1024], BF16).ap()
    dVSEND = nc.dram_tensor("vsend", [1024, D], BF16).ap()
    dVRECV = nc.dram_tensor("vrecv", [2 * 1024, D], BF16).ap()
    dWG16 = nc.dram_tensor("wg16", [FH, D], BF16).ap()
    dWU16 = nc.dram_tensor("wu16", [FH, D], BF16).ap()
    dWD16 = nc.dram_tensor("wd16", [FH, D], BF16).ap()

    stack = ExitStack()
    e = stack.enter_context

    x_sb = e(nc.sbuf_tensor("x_sb", [128, TI * D], F32))
    ar1 = e(nc.sbuf_tensor("ar1", [128, 32768], mybir.dt.uint8))
    ar2 = e(nc.sbuf_tensor("ar2", [128, 32768], mybir.dt.uint8))
    ar3 = e(nc.sbuf_tensor("ar3", [128, 32768], mybir.dt.uint8))
    ar4 = e(nc.sbuf_tensor("ar4", [128, 32768], mybir.dt.uint8))
    scr = e(nc.sbuf_tensor("scr", [128, 10240], mybir.dt.uint8))
    cst = e(nc.sbuf_tensor("cst", [128, 2048], mybir.dt.uint8))
    sta = e(nc.sbuf_tensor("sta", [128, 2048], mybir.dt.uint8))

    def u8_bf(t, off, n):
        return t[:, off:off + 2 * n].bitcast(BF16)

    def u8_f32(t, off, n):
        return t[:, off:off + 4 * n].bitcast(F32)

    lnT = u8_bf(ar1, 0, 16384)
    pt = u8_bf(ar1, 0, 8192)
    ptm0 = u8_bf(ar1, 16384, 4096)
    ptm1 = u8_bf(ar1, 24576, 4096)
    av_m = u8_bf(ar1, 0, 16384)
    bo_row = u8_bf(ar1, 32768 - 4096, 2048)

    qa = u8_bf(ar2, 0, 16384)
    gated = u8_bf(ar2, 0, 8192)
    sg0 = u8_bf(ar2, 16384, 1024)
    sg1 = u8_bf(ar2, 18432, 1024)
    ffw0 = u8_f32(ar2, 20480, 1024)
    ffw1 = u8_f32(ar2, 24576, 1024)
    kb_own = u8_bf(ar2, 0, 16384)

    av = u8_bf(ar3, 0, 16384)
    wd16 = u8_bf(ar3, 0, 16384)
    w16a = u8_bf(ar3, 0, 16384)

    w16b = u8_bf(ar4, 0, 16384)
    kh0 = u8_bf(ar4, 0, 4096)
    kh1 = u8_bf(ar4, 8192, 4096)
    vh0 = u8_bf(ar4, 16384, 4096)
    vh1 = u8_bf(ar4, 24576, 4096)
    wg16s = [u8_bf(ar4, 0, 2048), u8_bf(ar4, 4096, 2048)]
    wu16s = [u8_bf(ar4, 8192, 2048), u8_bf(ar4, 12288, 2048)]
    ffw2 = u8_f32(ar4, 16384, 1024)
    ffw3 = u8_f32(ar4, 20480, 1024)
    wo16s = [u8_bf(ar4, 0, 2048), u8_bf(ar4, 4096, 2048)]

    wst0 = u8_f32(scr, 0, 1024)
    wst1 = u8_f32(scr, 4096, 1024)
    ln_tm0 = u8_bf(ar2, 0, 2048)
    ln_tm1 = u8_bf(ar2, 4096, 2048)
    sqscr = u8_f32(scr, 8192, 512)

    maskA = u8_f32(cst, 0, 128)
    maskB = u8_f32(cst, 512, 128)
    ident = u8_bf(cst, 1024, 128)
    ones1 = u8_bf(cst, 1280, 128)
    bq_c = u8_f32(cst, 1536, 16)
    bk_c = u8_f32(cst, 1600, 16)
    bv_c = u8_f32(cst, 1664, 16)
    lng_c = u8_f32(cst, 1728, 16)
    lnb_c = u8_f32(cst, 1792, 16)
    eps_c = u8_f32(cst, 1856, 1)
    bqs_c = u8_f32(sta, 1680, 16)

    ssum = u8_f32(sta, 0, 16)
    sqac = u8_f32(sta, 64, 16)
    mu_s = u8_f32(sta, 128, 16)
    var_s = u8_f32(sta, 192, 16)
    rstd_s = u8_f32(sta, 256, 16)
    t0_s = u8_f32(sta, 320, 16)
    t1_s = u8_f32(sta, 384, 16)
    mbuf = u8_f32(sta, 448, 8)
    bias_b = u8_f32(sta, 480, 8)
    acc_b = u8_f32(sta, 512, 8)
    fv_b = u8_f32(sta, 544, 8)
    accf_b = u8_f32(sta, 576, 8)
    tot_b = u8_f32(sta, 608, 2)
    rinv = u8_f32(sta, 640, 128)
    rawm = u8_f32(sta, 1152, 4)
    avst0 = u8_bf(sta, 1168, 128)
    avst1 = u8_bf(sta, 1424, 128)

    ps = e(nc.psum_tensor("ps", [128, 4096], F32))

    def bank(k, n=1):
        return ps[:, 512 * k:512 * (k + n)]

    def bank_bf(k, n=1):
        return ps[:, 512 * k:512 * (k + n)].bitcast(BF16)

    sems = {}
    for name in ("qX", "qS0", "qS1", "qA0", "qA1", "qF0", "qF1", "qF2", "qF3",
                 "qW", "qC", "qK", "qY", "cc0", "cc1", "pe", "dv", "ac", "po"):
        sems[name] = e(nc.semaphore(name))

    block = e(nc.Block())

    n_heads = cfg["n_heads"]
    n_tiles = cfg["n_tiles"]
    n_hc = cfg["n_hc"]
    sublayers = cfg["sublayers"]

    def emit(me, eng):
        E = Em(me, eng, sems)
        w, do, do0, C = E.w, E.do, E.do0, E.C
        def scope(name):
            return nc.named_scope(name)

        xw = {}
        for i in range(2):
            do("sync", "qX", lambda e, i=i: e.dma_start(
                out=x_sb[:, i * D:(i + 1) * D], in_=dX[i * 128:(i + 1) * 128, :]), 16)
        xw[0] = C["qX"]
        for i in range(2, TI):
            do("sync", "qY", lambda e, i=i: e.dma_start(
                out=x_sb[:, i * D:(i + 1) * D], in_=dX[i * 128:(i + 1) * 128, :]), 16)
        xw[1] = C["qY"]
        for (dst, src) in ((maskA, dMA), (maskB, dMB), (ident, dID),
                           (bq_c, dBQ), (bk_c, dBK), (bv_c, dBV),
                           (lng_c, dLNG), (lnb_c, dLNB)):
            do("sync", "qK", lambda e, d=dst, s=src: e.dma_start(out=d, in_=s), 16)
        do("sync", "qK", lambda e: e.dma_start(out=ones1[0:1, :], in_=dONES), 16)
        consts_ok = C["qK"]
        x_loaded = [("qX", xw[0]) if i < 2 else ("qY", xw[1]) for i in range(TI)]
        if cfg.get("zero_arenas"):
            for t in (ar1, ar2, ar3, ar4):
                do("vector", "dv", lambda e, t=t: e.memset(t[:, :], 0))
        zero_dv = C["dv"]
        eps_dv = do("vector", "dv", lambda e: e.memset(eps_c, EPS_LN))
        w("vector", "qK", consts_ok)
        bqs_dv = do("vector", "dv", lambda e: e.tensor_scalar_mul(
            out=bqs_c[:], in0=bq_c[:], scalar1=SCALE))

        state = dict(tb=0, avb=0)

        def layer_norm(double_ln=False, affine=False, x_ready=0, arena_pe=0,
                       scratch_po=0):
            for i in range(n_tiles):
                sl = i % 2
                ln_tm = ln_tm0 if sl == 0 else ln_tm1
                xi = x_sb[:, i * D:(i + 1) * D]
                w("vector", x_loaded[i][0], x_loaded[i][1])
                w("vector", "dv", x_ready)
                v_sum = do("vector", "dv", lambda e, xi=xi, i=i: e.reduce_sum(
                    out=ssum[:, i:i + 1], in_=xi, axis=AX))
                w("scalar", x_loaded[i][0], x_loaded[i][1])
                w("scalar", "dv", x_ready)
                a_sq0 = do("scalar", "ac", lambda e, xi=xi, i=i: e.activation(
                    out=sqscr[:], in_=xi[:, 0:512], func=AF.Square,
                    accum_out=sqac[:, 2 * (i % 8):2 * (i % 8) + 1]))
                w("scalar", "ac", a_sq0)
                a_sq1 = do("scalar", "ac", lambda e, xi=xi, i=i: e.activation(
                    out=sqscr[:], in_=xi[:, 512:1024], func=AF.Square,
                    accum_out=t0_s[:, i:i + 1]))
                w("scalar", "ac", a_sq1)
                a_sq2 = do("scalar", "ac", lambda e, xi=xi, i=i: e.activation(
                    out=sqscr[:], in_=xi[:, 1024:1536], func=AF.Square,
                    accum_out=t1_s[:, i:i + 1]))
                w("scalar", "ac", a_sq2)
                a_sq3 = do("scalar", "ac", lambda e, xi=xi, i=i: e.activation(
                    out=sqscr[:], in_=xi[:, 1536:2048], func=AF.Square,
                    accum_out=sqac[:, 2 * (i % 8) + 1:2 * (i % 8) + 2]))

                def stats_chain(i, v_sum, a_sq3, pfx_mu, pfx_var, pfx_rstd):
                    w("vector", "dv", v_sum)
                    v_mu = do("vector", "dv", lambda e, i=i: e.tensor_scalar_mul(
                        out=mu_s[:, i:i + 1], in0=ssum[:, i:i + 1], scalar1=1.0 / D))
                    w("vector", "ac", a_sq3)
                    v_s2 = do("vector", "dv", lambda e, i=i: e.tensor_add(
                        out=sqac[:, 2 * (i % 8):2 * (i % 8) + 1],
                        in0=sqac[:, 2 * (i % 8):2 * (i % 8) + 1],
                        in1=t0_s[:, i:i + 1]))
                    w("vector", "dv", v_s2)
                    v_s2b = do("vector", "dv", lambda e, i=i: e.tensor_add(
                        out=t1_s[:, i:i + 1], in0=t1_s[:, i:i + 1],
                        in1=sqac[:, 2 * (i % 8) + 1:2 * (i % 8) + 2]))
                    w("vector", "dv", v_s2b)
                    v_s2c = do("vector", "dv", lambda e, i=i: e.tensor_add(
                        out=sqac[:, 2 * (i % 8):2 * (i % 8) + 1],
                        in0=sqac[:, 2 * (i % 8):2 * (i % 8) + 1],
                        in1=t1_s[:, i:i + 1]))
                    w("vector", "dv", max(v_mu, v_s2c))
                    v_mu2 = do("vector", "dv", lambda e, i=i: e.tensor_mul(
                        out=t0_s[:, i:i + 1], in0=mu_s[:, i:i + 1], in1=mu_s[:, i:i + 1]))
                    w("vector", "dv", v_mu2)
                    v_var = do("vector", "dv", lambda e, i=i: e.scalar_tensor_tensor(
                        out=var_s[:, i:i + 1], in0=sqac[:, 2 * (i % 8):2 * (i % 8) + 1],
                        scalar=1.0 / D, in1=t0_s[:, i:i + 1],
                        op0=ALU.mult, op1=ALU.subtract))
                    w("scalar", "dv", v_var)
                    w("scalar", "dv", eps_dv)
                    a_ln = do("scalar", "ac", lambda e, i=i: e.activation(
                        out=t1_s[:, i:i + 1], in_=var_s[:, i:i + 1], func=AF.Ln,
                        bias=eps_c[:, 0:1]))
                    w("scalar", "ac", a_ln)
                    a_rs = do("scalar", "ac", lambda e, i=i: e.activation(
                        out=rstd_s[:, i:i + 1], in_=t1_s[:, i:i + 1], func=AF.Exp,
                        scale=-0.5))
                    return a_rs

                a_rs = stats_chain(i, v_sum, a_sq3, mu_s, var_s, rstd_s)

                if not double_ln:
                    w("vector", "ac", a_rs)
                    if i < 2:
                        w("vector", "pe", arena_pe)
                    v_ln = do("vector", "dv", lambda e, xi=xi, i=i, lt=ln_tm: e.tensor_scalar(
                        out=lt[:], in0=xi, scalar1=mu_s[:, i:i + 1],
                        scalar2=rstd_s[:, i:i + 1], op0=ALU.subtract, op1=ALU.mult))
                else:
                    y32 = ar2[:, 8192:16384].bitcast(F32)
                    w("vector", "ac", a_rs)
                    if i < 2:
                        w("vector", "pe", arena_pe)
                    v_y = do("vector", "dv", lambda e, xi=xi, i=i, y=y32: e.tensor_scalar(
                        out=y[:], in0=xi, scalar1=mu_s[:, i:i + 1],
                        scalar2=rstd_s[:, i:i + 1], op0=ALU.subtract, op1=ALU.mult))
                    w("vector", "dv", v_y)
                    v_sum2 = do("vector", "dv", lambda e, y=y32, i=i: e.reduce_sum(
                        out=ssum[:, 8 + (i % 8):9 + (i % 8)], in_=y, axis=AX))
                    w("scalar", "dv", v_y)
                    a_q0 = do("scalar", "ac", lambda e, y=y32, i=i: e.activation(
                        out=sqscr[:], in_=y[:, 0:512], func=AF.Square,
                        accum_out=sqac[:, 2 * (i % 8):2 * (i % 8) + 1]))
                    w("scalar", "ac", a_q0)
                    a_q1 = do("scalar", "ac", lambda e, y=y32, i=i: e.activation(
                        out=sqscr[:], in_=y[:, 512:1024], func=AF.Square,
                        accum_out=t0_s[:, i:i + 1]))
                    w("scalar", "ac", a_q1)
                    a_q2 = do("scalar", "ac", lambda e, y=y32, i=i: e.activation(
                        out=sqscr[:], in_=y[:, 1024:1536], func=AF.Square,
                        accum_out=t1_s[:, i:i + 1]))
                    w("scalar", "ac", a_q2)
                    a_q3 = do("scalar", "ac", lambda e, y=y32, i=i: e.activation(
                        out=sqscr[:], in_=y[:, 1536:2048], func=AF.Square,
                        accum_out=sqac[:, 2 * (i % 8) + 1:2 * (i % 8) + 2]))
                    w("vector", "dv", v_sum2)
                    v_mu = do("vector", "dv", lambda e, i=i: e.tensor_scalar_mul(
                        out=mu_s[:, 8 + (i % 8):9 + (i % 8)],
                        in0=ssum[:, 8 + (i % 8):9 + (i % 8)], scalar1=1.0 / D))
                    w("vector", "ac", a_q3)
                    v_a = do("vector", "dv", lambda e, i=i: e.tensor_add(
                        out=sqac[:, 2 * (i % 8):2 * (i % 8) + 1],
                        in0=sqac[:, 2 * (i % 8):2 * (i % 8) + 1], in1=t0_s[:, i:i + 1]))
                    w("vector", "dv", v_a)
                    v_b = do("vector", "dv", lambda e, i=i: e.tensor_add(
                        out=t1_s[:, i:i + 1], in0=t1_s[:, i:i + 1],
                        in1=sqac[:, 2 * (i % 8) + 1:2 * (i % 8) + 2]))
                    w("vector", "dv", v_b)
                    v_c = do("vector", "dv", lambda e, i=i: e.tensor_add(
                        out=sqac[:, 2 * (i % 8):2 * (i % 8) + 1],
                        in0=sqac[:, 2 * (i % 8):2 * (i % 8) + 1], in1=t1_s[:, i:i + 1]))
                    w("vector", "dv", max(v_mu, v_c))
                    v_mu2 = do("vector", "dv", lambda e, i=i: e.tensor_mul(
                        out=t0_s[:, i:i + 1], in0=mu_s[:, 8 + (i % 8):9 + (i % 8)],
                        in1=mu_s[:, 8 + (i % 8):9 + (i % 8)]))
                    w("vector", "dv", v_mu2)
                    v_var = do("vector", "dv", lambda e, i=i: e.scalar_tensor_tensor(
                        out=var_s[:, 8 + (i % 8):9 + (i % 8)],
                        in0=sqac[:, 2 * (i % 8):2 * (i % 8) + 1],
                        scalar=1.0 / D, in1=t0_s[:, i:i + 1],
                        op0=ALU.mult, op1=ALU.subtract))
                    w("scalar", "dv", v_var)
                    w("scalar", "dv", eps_dv)
                    a_l2 = do("scalar", "ac", lambda e, i=i: e.activation(
                        out=t1_s[:, i:i + 1], in_=var_s[:, 8 + (i % 8):9 + (i % 8)],
                        func=AF.Ln, bias=eps_c[:, 0:1]))
                    w("scalar", "ac", a_l2)
                    a_rs2 = do("scalar", "ac", lambda e, i=i: e.activation(
                        out=rstd_s[:, 8 + (i % 8):9 + (i % 8)], in_=t1_s[:, i:i + 1],
                        func=AF.Exp, scale=-0.5))
                    w("vector", "ac", a_rs2)
                    v_ln = do("vector", "dv", lambda e, y=y32, i=i, lt=ln_tm: e.tensor_scalar(
                        out=lt[:], in0=y, scalar1=mu_s[:, 8 + (i % 8):9 + (i % 8)],
                        scalar2=rstd_s[:, 8 + (i % 8):9 + (i % 8)],
                        op0=ALU.subtract, op1=ALU.mult))

                w("tensor", "qK", consts_ok)
                w("tensor", "dv", zero_dv)
                w("tensor", "pe", arena_pe)
                for g in range(4):
                    tb = state["tb"]
                    state["tb"] ^= 1
                    tbank = 4 + tb
                    w("tensor", "dv", state.get(("tev", tb), 0))
                    w("tensor", "dv", v_ln)
                    for j in range(4):
                        dc = g * 4 + j
                        pe_v = do("tensor", "pe", lambda e, lt=ln_tm, dc=dc, j=j, tbank=tbank: e.transpose(
                            out=bank_bf(tbank)[:, j * 128:(j + 1) * 128],
                            in_=lt[:, dc * 128:(dc + 1) * 128], identity=ident))
                    w("vector", "pe", pe_v)
                    if affine:
                        for j in range(4):
                            dc = g * 4 + j
                            ev = do("vector", "dv", lambda e, dc=dc, j=j, i=i, tbank=tbank: e.tensor_scalar(
                                out=lnT[:, dc * 1024 + i * 128: dc * 1024 + (i + 1) * 128],
                                in0=bank_bf(tbank)[:, j * 128:(j + 1) * 128],
                                scalar1=lng_c[:, dc:dc + 1], scalar2=lnb_c[:, dc:dc + 1],
                                op0=ALU.mult, op1=ALU.add))
                    else:
                        ev = do("vector", "dv", lambda e, g=g, i=i, tbank=tbank: e.tensor_copy(
                            out=lnT.rearrange("p (c n) -> p c n", n=1024)[
                                :, g * 4:(g + 1) * 4, i * 128:(i + 1) * 128],
                            in_=bank_bf(tbank).rearrange("p (c n) -> p c n", n=128)[:, 0:4, :]))
                    state[("tev", tb)] = ev
            state["scr_free_pe"] = C["pe"]
            return C["dv"], C["ac"]

        def w16_tile(dcc):
            if dcc < 8:
                return w16a[:, dcc * 2048:(dcc + 1) * 2048]
            return w16b[:, (dcc - 8) * 2048:(dcc - 8 + 1) * 2048]

        CAST_SEM = {"gpsimd": "po", "vector": "dv", "scalar": "ac"}

        def cast_to(engx, dst, srcap, waits=()):
            for (s, v) in waits:
                w(engx, s, v)
            sem = CAST_SEM[engx]
            if engx == "scalar":
                v = do(engx, sem, lambda e, dst=dst, srcap=srcap: e.copy(
                    out=dst, in_=srcap))
            else:
                v = do(engx, sem, lambda e, dst=dst, srcap=srcap: e.tensor_copy(
                    out=dst, in_=srcap))
            return (sem, v)

        def stream_w_resident(src, gate_sem=None, gate_val=0, dst_tile=None,
                              cast_engs=("gpsimd",)):
            if dst_tile is None:
                dst_tile = w16_tile
            last = None
            for dcc in range(DC):
                for hhalf in range(2):
                    k = dcc * 2 + hhalf
                    p = k % 2
                    st = wst0 if p == 0 else wst1
                    if gate_sem is not None and dcc == 0:
                        w("sync", gate_sem, gate_val)
                    ssem, sval = state.get(("stage", p), ("po", 0))
                    w("sync", ssem, sval)
                    qsem = "qS%d" % p
                    qv = do("sync", qsem, lambda e, st=st, dcc=dcc, hh=hhalf, src=src: e.dma_start(
                        out=st, in_=src[dcc * 128:(dcc + 1) * 128, hh * 1024:(hh + 1) * 1024]), 16)
                    engx = cast_engs[k % len(cast_engs)]
                    last = cast_to(engx, dst_tile(dcc)[:, hhalf * 1024:(hhalf + 1) * 1024],
                                   st, waits=[(qsem, qv)])
                    state[("stage", p)] = last
            return dict(po=C["po"], dv=C["dv"], ac=C["ac"])

        def wait_ready(owner, r):
            if isinstance(r, dict):
                w(owner, "po", r["po"])
                w(owner, "dv", r["dv"])
                w(owner, "ac", r["ac"])
            else:
                w(owner, "po", r)

        def projection(dst, bias_col, lnT_ready_dv, w_po, extra_pe_wait=0, n_oc=DC,
                       scale_imm=None):
            for oc in range(n_oc):
                for th in range(2):
                    pbk = (2 * oc + th) % 4
                    psem, pval = state.get(("pjev", pbk), ("ac", 0))
                    w("tensor", psem, pval)
                    w("tensor", "dv", lnT_ready_dv)
                    wait_ready("tensor", w_po)
                    w("tensor", "pe", extra_pe_wait)
                    for dcc in range(DC):
                        pe_v = do("tensor", "pe", lambda e, oc=oc, th=th, dcc=dcc, pbk=pbk: e.matmul(
                            out=bank(pbk),
                            lhsT=w16_tile(dcc)[:, oc * 128:(oc + 1) * 128],
                            rhs=lnT[:, dcc * 1024 + th * 512: dcc * 1024 + (th + 1) * 512],
                            start=(dcc == 0), stop=(dcc == DC - 1)))
                    if scale_imm is not None and bias_col is not None:
                        w("vector", "pe", pe_v)
                        evd = do("vector", "dv", lambda e, oc=oc, th=th, pbk=pbk, dst=dst, b=bias_col, s=scale_imm: e.tensor_scalar(
                            out=dst[:, oc * 1024 + th * 512: oc * 1024 + (th + 1) * 512],
                            in0=bank(pbk), scalar1=s, scalar2=b[:, oc:oc + 1],
                            op0=ALU.mult, op1=ALU.add))
                        state[("pjev", pbk)] = ("dv", evd)
                        continue
                    w("scalar", "pe", pe_v)
                    if scale_imm is not None:
                        ev = do("scalar", "ac", lambda e, oc=oc, th=th, pbk=pbk, dst=dst, s=scale_imm: e.activation(
                            out=dst[:, oc * 1024 + th * 512: oc * 1024 + (th + 1) * 512],
                            in_=bank(pbk), func=AF.Copy, scale=s, bias=0.0))
                    elif bias_col is None:
                        ev = do("scalar", "ac", lambda e, oc=oc, th=th, pbk=pbk, dst=dst: e.copy(
                            out=dst[:, oc * 1024 + th * 512: oc * 1024 + (th + 1) * 512],
                            in_=bank(pbk)))
                    else:
                        ev = do("scalar", "ac", lambda e, oc=oc, th=th, pbk=pbk, dst=dst, b=bias_col: e.add(
                            out=dst[:, oc * 1024 + th * 512: oc * 1024 + (th + 1) * 512],
                            in_=bank(pbk),
                            add=b[:, oc:oc + 1]))
                    state[("pjev", pbk)] = ("ac", ev)
            return (C["ac"], C["dv"])

        def attn_unit(h, i, u, q_lhsT, kT, nk, ptm, masks, slot, weave=()):
            CN = 1024
            nch = (nk + CN - 1) // CN
            exp_acs = []
            first_dve = [True]
            wv = list(weave)

            def unit_dve_gate():
                if first_dve[0]:
                    w("vector", "dv", state.get("unit_dv", 0))
                    first_dve[0] = False

            v_m = None
            for c in range(nch):
                k0 = c * CN
                k1 = min(nk, k0 + CN)
                par = c % 2
                sb = 2 * par
                w("tensor", "ac", state.get(("sxp", par), 0))
                n0 = k1 - k0
                pe_v = None
                for j in range(0, n0, 512):
                    nn = min(512, n0 - j)
                    pe_v = do("tensor", "pe", lambda e, sb=sb, j=j, nn=nn, k0=k0, q=q_lhsT, kT=kT: e.matmul(
                        out=bank(sb, 2)[:, j:j + nn], lhsT=q,
                        rhs=kT[:, k0 + j:k0 + j + nn], start=True, stop=True))
                if wv:
                    wv.pop(0)()
                mv = None
                for (bi, m_ap) in masks:
                    b0 = bi * 128
                    if b0 >= k0 and b0 < k1:
                        w("vector", "pe", pe_v)
                        unit_dve_gate()
                        mv = do("vector", "dv", lambda e, sb=sb, b0=b0, k0=k0, m=m_ap: e.tensor_add(
                            out=bank(sb, 2)[:, b0 - k0:b0 - k0 + 128],
                            in0=bank(sb, 2)[:, b0 - k0:b0 - k0 + 128], in1=m))
                w("vector", "pe", pe_v)
                unit_dve_gate()
                if mv is not None:
                    w("vector", "dv", mv)
                if c == 0:
                    v_m = do("vector", "dv", lambda e, sb=sb, n0=n0: e.reduce_max(
                        out=mbuf[:, 0:1], in_=bank(sb, 2)[:, 0:n0], axis=AX, negate=True))
                else:
                    w("vector", "dv", v_m)
                    v_r = do("vector", "dv", lambda e, sb=sb, n0=n0: e.reduce_max(
                        out=rawm[:, 0:1], in_=bank(sb, 2)[:, 0:n0], axis=AX, negate=True))
                    w("vector", "dv", v_r)
                    v_m = do("vector", "dv", lambda e, c=c: e.tensor_scalar(
                        out=mbuf[:, c:c + 1], in0=rawm[:, 0:1],
                        scalar1=mbuf[:, c - 1:c], scalar2=None, op0=ALU.min))
                w("scalar", "dv", v_m)
                if c == 0:
                    w("scalar", "pe", state.get(("ptm_pe", slot), 0))
                a_e = do("scalar", "ac", lambda e, sb=sb, c=c, k0=k0, n0=n0, ptm=ptm: e.activation(
                    out=ptm[:, k0:k0 + n0], in_=bank(sb, 2)[:, 0:n0], func=AF.Exp,
                    bias=mbuf[:, c:c + 1], accum_out=acc_b[:, c:c + 1]))
                state[("sxp", par)] = a_e
                exp_acs.append(a_e)
            for f in wv:
                f()
            last = nch - 1
            w("scalar", "dv", v_m)
            w("scalar", "ac", exp_acs[-1])
            a_f = do("scalar", "ac", lambda e, nch=nch, last=last: e.activation(
                out=fv_b[:, 0:nch], in_=mbuf[:, 0:nch], func=AF.Exp, scale=-1.0,
                bias=mbuf[:, last:last + 1]))
            w("vector", "ac", a_f)
            v_t = do("vector", "dv", lambda e, nch=nch: e.scalar_tensor_tensor(
                out=accf_b[:, 0:nch], in0=acc_b[:, 0:nch], scalar=1.0,
                in1=fv_b[:, 0:nch], op0=ALU.mult, op1=ALU.mult,
                accum_out=tot_b[:, 0:1]))
            w("vector", "dv", v_t)
            v_ri = do("vector", "dv", lambda e: e.reciprocal(
                out=tot_b[:, 1:2], in_=tot_b[:, 0:1]))
            w("vector", "dv", v_ri)
            corr_dv = C["dv"]
            for c in range(nch):
                k0 = c * CN
                k1 = min(nk, k0 + CN)
                corr_dv = do("vector", "dv", lambda e, c=c, k0=k0, k1=k1, ptm=ptm: e.tensor_scalar(
                    out=ptm[:, k0:k1], in0=ptm[:, k0:k1], scalar1=fv_b[:, c:c + 1],
                    scalar2=tot_b[:, 1:2], op0=ALU.mult, op1=ALU.mult))
            state["unit_dv"] = corr_dv
            return dict(exp_ac=exp_acs[-1], corr_dv=corr_dv)

        def make_T_groups(h, i, u, unit, ptm, nk, slot):
            nkc = nk // 128
            ip = i % 2
            ngr = (nkc + 7) // 8
            closures = []

            def grp(g):
                tb = state["tb"]
                state["tb"] ^= 1
                tbank = 4 + tb
                w("tensor", "dv", state.get(("tev", tb), 0))
                w("tensor", "ac", state.get(("tev_ac", tb), 0))
                w("tensor", "ac", unit["exp_ac"])
                w("tensor", "dv", unit["corr_dv"])
                ntr = min(8, nkc - g * 8)
                pe_v = None
                for j in range(ntr):
                    kc = g * 8 + j
                    pe_v = do("tensor", "pe", lambda e, kc=kc, j=j, tbank=tbank, ptm=ptm: e.transpose(
                        out=bank_bf(tbank)[:, j * 128:(j + 1) * 128],
                        in_=ptm[:, kc * 128:(kc + 1) * 128], identity=ident))
                out_ap = pt.rearrange("p (c n) -> p c n", n=256)[
                    :, g * 8:g * 8 + ntr, ip * 128:(ip + 1) * 128]
                in_ap = bank_bf(tbank).rearrange("p (c n) -> p c n", n=128)[:, 0:ntr, :]
                if g % 2 == 0:
                    w("vector", "pe", pe_v)
                    if g == 0:
                        w("vector", "pe", state.get("pt_free_pe", 0))
                    ev = do("vector", "dv", lambda e, o=out_ap, i_=in_ap: e.tensor_copy(
                        out=o, in_=i_))
                    state[("tev", tb)] = ev
                else:
                    w("scalar", "pe", pe_v)
                    if g == 1:
                        w("scalar", "pe", state.get("pt_free_pe", 0))
                    ev_ac = do("scalar", "ac", lambda e, o=out_ap, i_=in_ap: e.copy(
                        out=o, in_=i_))
                    state[("tev_ac", tb)] = ev_ac
                if g == ngr - 1:
                    state[("ptm_pe", slot)] = C["pe"]
                    state[("pt_done_dv", ip)] = C["dv"]
                    state[("pt_done_ac", ip)] = C["ac"]

            for g in range(ngr):
                closures.append(lambda g=g: grp(g))
            return closures

        def emit_av_pair(h, i_odd, vh, nk_odd, nk_even, av_dst_col):
            nkc = nk_odd // 128
            nkc_e = nk_even // 128
            ab = state["avb"]
            state["avb"] ^= 1
            avbank = 6 + ab
            w("tensor", "dv", state[("pt_done_dv", 0)])
            w("tensor", "ac", state[("pt_done_ac", 0)])
            w("tensor", "dv", state[("pt_done_dv", 1)])
            w("tensor", "ac", state[("pt_done_ac", 1)])
            w("tensor", "dv", state.get(("avev", ab), 0))
            pe_av = None
            for kc in range(nkc):
                if kc < nkc_e:
                    o0, nn = 0, 256
                else:
                    o0, nn = 128, 128
                pe_av = do("tensor", "pe", lambda e, kc=kc, avbank=avbank, vh=vh, o0=o0, nn=nn: e.matmul(
                    out=bank(avbank)[:, o0:o0 + nn], lhsT=vh[:, kc * 128:(kc + 1) * 128],
                    rhs=pt[:, kc * 256 + o0: kc * 256 + o0 + nn],
                    start=(kc == 0), stop=(kc == nkc - 1)))
            state["pt_free_pe"] = C["pe"]
            w("vector", "pe", pe_av)
            v_c = do("vector", "dv", lambda e, avbank=avbank, col=av_dst_col: e.tensor_copy(
                out=av[:, col - 128:col + 128], in_=bank(avbank)[:, 0:256]))
            state[("avev", ab)] = v_c
            return v_c

        def ffn_sublayer(x_ready, arena_pe, scratch_po):
            ln_dv, _ = layer_norm(x_ready=x_ready, arena_pe=arena_pe,
                                  scratch_po=scratch_po)
            state[("w16free", 0)] = arena_pe
            state[("w16free", 1)] = arena_pe
            state["ffn_down_pe"] = arena_pe
            state["ffn_dn_ev01"] = x_ready
            state["ffn_tt_dv"] = x_ready
            state[("sgfree", 0)] = x_ready
            state[("sgfree", 1)] = x_ready
            nblk = max(1, n_hc // 8)
            for blk in range(nblk):
                for k in range(8):
                    hc = blk * 8 + k
                    par = hc % 2
                    gu_marks = []
                    for (srcw, stg, stsem, dsts, cengs) in (
                        (dWG, ffw0, "qF0", lambda h=hc: wg16s[h % 2], ("vector", "scalar")),
                        (dWU, ffw1, "qF1", lambda h=hc: wu16s[h % 2], ("scalar", "vector")),
                    ):
                        for half in range(2):
                            ssem, sval = state.get(("fst", stsem), ("po", 0))
                            w("sync", ssem, sval)
                            if blk == 0 and k == 0 and half == 0:
                                w("sync", "pe", arena_pe)
                                w("sync", "dv", ln_dv)
                            qv = do("sync", stsem, lambda e, st=stg, srcw=srcw, hc=hc, half=half: e.dma_start(
                                out=st, in_=srcw[hc * 128:(hc + 1) * 128,
                                                 half * 1024:(half + 1) * 1024]), 16)
                            engx = cengs[half]
                            mk = cast_to(engx,
                                         dsts()[:, half * 1024:(half + 1) * 1024], stg,
                                         waits=[(stsem, qv),
                                                ("pe", state.get(("w16free", par), 0))])
                            state[("fst", stsem)] = mk
                            gu_marks.append(mk)
                    wd_marks = []
                    for half in range(2):
                        stsem = "qF2" if half == 0 else "qF3"
                        stg = ffw2 if half == 0 else ffw3
                        ssem, sval = state.get(("fst", stsem), ("po", 0))
                        w("sync", ssem, sval)
                        if blk == 0 and k == 0 and half == 0:
                            w("sync", "pe", arena_pe)
                        qv = do("sync", stsem, lambda e, st=stg, hc=hc, half=half: e.dma_start(
                            out=st, in_=dWD[hc * 128:(hc + 1) * 128,
                                            half * 1024:(half + 1) * 1024]), 16)
                        engx = ("vector", "scalar")[(k + half) % 2]
                        wts = [(stsem, qv), ("pe", state.get("ffn_down_pe", 0))]
                        if blk == 0 and k == 0:
                            wts.append(("dv", x_ready))
                        mk = cast_to(engx,
                                     wd16[:, k * 2048 + half * 1024: k * 2048 + (half + 1) * 1024],
                                     stg, waits=wts)
                        state[("fst", stsem)] = mk
                        wd_marks.append(mk)
                    state[("wd_marks", k)] = list(wd_marks)
                    for (s_, v_) in gu_marks:
                        w("tensor", s_, v_)
                    w("tensor", "dv", ln_dv)
                    w("tensor", "ac", state.get("ffn_silu_ac", 0))
                    w("tensor", "dv", state.get("ffn_dn_ev01", 0))
                    for dcc in range(DC):
                        for th in range(2):
                            pe_g = do("tensor", "pe", lambda e, dcc=dcc, th=th, hc=hc: e.matmul(
                                out=bank(th), lhsT=wg16s[hc % 2][:, dcc * 128:(dcc + 1) * 128],
                                rhs=lnT[:, dcc * 1024 + th * 512: dcc * 1024 + (th + 1) * 512],
                                start=(dcc == 0), stop=(dcc == DC - 1)))
                    w("scalar", "pe", pe_g)
                    w("scalar", "dv", state.get(("sgfree", hc % 2), 0))
                    a_si = do("scalar", "ac", lambda e, hc=hc: e.activation(
                        out=(sg0 if hc % 2 == 0 else sg1)[:], in_=bank(0, 2), func=AF.Silu))
                    state["ffn_silu_ac"] = a_si
                    w("tensor", "dv", state.get("ffn_tt_dv", 0))
                    for dcc in range(DC):
                        for th in range(2):
                            pe_u = do("tensor", "pe", lambda e, dcc=dcc, th=th, hc=hc: e.matmul(
                                out=bank(2 + th), lhsT=wu16s[hc % 2][:, dcc * 128:(dcc + 1) * 128],
                                rhs=lnT[:, dcc * 1024 + th * 512: dcc * 1024 + (th + 1) * 512],
                                start=(dcc == 0), stop=(dcc == DC - 1)))
                    state[("w16free", par)] = C["pe"]
                    w("vector", "pe", pe_u)
                    w("vector", "ac", a_si)
                    w("vector", "pe", state.get("ffn_down_pe", 0))
                    v_tt = do("vector", "dv", lambda e, hc=hc, k=k: e.tensor_mul(
                        out=gated[:, k * 1024:(k + 1) * 1024], in0=bank(2, 2),
                        in1=(sg0 if hc % 2 == 0 else sg1)[:]))
                    state["ffn_tt_dv"] = v_tt
                    state[("sgfree", hc % 2)] = v_tt
                    state[("ffn_gated", k)] = v_tt
                for tt in range(n_tiles):
                    pb = 4 * (tt % 2)
                    w("tensor", "dv", state[("ffn_gated", 7)])
                    if tt == 0:
                        for kk in range(8):
                            for (s_, v_) in state[("wd_marks", kk)]:
                                w("tensor", s_, v_)
                    w("tensor", "ac", state.get("ffn_silu_ac", 0))
                    w("tensor", "dv", state.get(("ffn_dn_ev", tt % 2), 0))
                    for hcc in range(8):
                        for j in range(4):
                            pe_d = do("tensor", "pe", lambda e, hcc=hcc, j=j, tt=tt, pb=pb: e.matmul(
                                out=bank(pb + j),
                                lhsT=gated[:, hcc * 1024 + tt * 128: hcc * 1024 + (tt + 1) * 128],
                                rhs=wd16[:, hcc * 2048 + j * 512: hcc * 2048 + (j + 1) * 512],
                                start=(hcc == 0), stop=(hcc == 7)))
                    w("vector", "pe", pe_d)
                    w("vector", "dv", state.get(("x_guard", tt), 0))
                    ev = do("vector", "dv", lambda e, tt=tt, pb=pb: e.tensor_add(
                        out=x_sb[:, tt * D:(tt + 1) * D],
                        in0=x_sb[:, tt * D:(tt + 1) * D], in1=bank(pb, 4)))
                    state[("ffn_dn_ev", tt % 2)] = ev
                    state[("x_guard", tt)] = ev
                    if tt % 2 == 0:
                        state["ffn_dn_ev01"] = ev
                state["ffn_down_pe"] = C["pe"]
            return C["dv"]
        if 1 in sublayers:
            with scope("s1_ln"):
                ln_dv, ln_ac = layer_norm(double_ln=False, affine=False)

            wq_po = stream_w_resident(
                dWQA, gate_sem=("dv" if cfg.get("zero_arenas") else None),
                gate_val=zero_dv if cfg.get("zero_arenas") else 0,
                cast_engs=("vector", "scalar"))
            q_ac, _ = projection(qa, None, ln_dv, wq_po, n_oc=n_heads, scale_imm=SCALE)
            state["kv_gate_pe"] = C["pe"]

            kv_po = {}
            def stream_head_kv2(h):
                slot = h % 2
                kh = kh0 if slot == 0 else kh1
                vh = vh0 if slot == 0 else vh1
                for part in range(8):
                    is_v = part >= 4
                    src = dVB if is_v else dKT
                    dst = vh if is_v else kh
                    col0 = (part % 4) * 1024
                    stp = part % 2
                    st = wst0 if stp == 0 else wst1
                    ssem, sval = state.get(("stage", stp), ("po", 0))
                    w("sync", ssem, sval)
                    w("sync", "dv", state.get(("kvdone", slot), 0))
                    qsem = "qS%d" % stp
                    qv = do("sync", qsem, lambda e, st=st, src=src, h=h, col0=col0: e.dma_start(
                        out=st, in_=src[h * 128:(h + 1) * 128, col0:col0 + 1024]), 16)
                    w("gpsimd", qsem, qv)
                    w("gpsimd", "pe", state.get("kv_gate_pe", 0))
                    pv = do("gpsimd", "po", lambda e, st=st, dst=dst, col0=col0: e.tensor_copy(
                        out=dst[:, col0:col0 + 1024], in_=st))
                    state[("stage", stp)] = ("po", pv)
                kv_po[h] = C["po"]

            stream_head_kv2(0)
            if n_heads > 1:
                stream_head_kv2(1)

            ctx_h = scope("s1_heads")
            ctx_h.__enter__()
            pend_weave = []
            pend_av = None

            def mk_av_a(h, i, vh):
                def go(h=h, i=i, vh=vh):
                    res = emit_av_pair(h, i, vh, KVLEN, KVLEN,
                                       av_dst_col=h * 1024 + i * 128)
                    if i == n_tiles - 1:
                        state[("kvdone", h % 2)] = res
                        if h + 2 < n_heads:
                            stream_head_kv2(h + 2)
                return go
            for h in range(n_heads):
                slot = h % 2
                kh = kh0 if slot == 0 else kh1
                vh = vh0 if slot == 0 else vh1
                for i in range(n_tiles):
                    u = h * 8 + i
                    ptm = ptm0 if i % 2 == 0 else ptm1
                    q_lhsT = qa[:, h * 1024 + i * 128: h * 1024 + (i + 1) * 128]
                    w("tensor", "ac", q_ac)
                    w("tensor", "po", kv_po[h])
                    weave = pend_weave
                    if pend_av is not None:
                        weave = weave + [pend_av]
                        pend_av = None
                    unit = attn_unit(h, i, u, q_lhsT, kh, KVLEN, ptm, masks=(),
                                     slot=i % 2, weave=weave)
                    pend_weave = make_T_groups(h, i, u, unit, ptm, KVLEN, slot=i % 2)
                    if i % 2 == 1:
                        pend_av = mk_av_a(h, i, vh)
            for f in pend_weave:
                f()
            if pend_av is not None:
                pend_av()
            pend_weave, pend_av = [], None

            ctx_h.__exit__(None, None, None)
            av_done = C["dv"]
            mv = None
            for q in range(4):
                w("vector", "dv", av_done)
                mv = do("vector", "dv", lambda e, q=q: e.tensor_copy(
                    out=av_m[:, q * 4096:(q + 1) * 4096], in_=av[:, q * 4096:(q + 1) * 4096]))
            av_mv = mv
            w("sync", "dv", av_mv)
            wo_po = stream_w_resident(dWOA, cast_engs=("vector", "scalar"))

            for tt in range(n_tiles):
                pb = 4 * (tt % 2)
                w("tensor", "dv", state.get(("opev", tt % 2), 0))
                w("tensor", "dv", av_mv)
                wait_ready("tensor", wo_po)
                for hc in range(n_heads):
                    for j in range(4):
                        pe_v = do("tensor", "pe", lambda e, hc=hc, j=j, tt=tt, pb=pb: e.matmul(
                            out=bank(pb, 4)[:, j * 512:(j + 1) * 512],
                            lhsT=av_m[:, hc * 1024 + tt * 128: hc * 1024 + (tt + 1) * 128],
                            rhs=w16_tile(hc)[:, j * 512:(j + 1) * 512],
                            start=(hc == 0), stop=(hc == n_heads - 1)))
                w("vector", "pe", pe_v)
                ev = do("vector", "dv", lambda e, tt=tt, pb=pb: e.tensor_add(
                    out=x_sb[:, tt * D:(tt + 1) * D],
                    in0=x_sb[:, tt * D:(tt + 1) * D], in1=bank(pb, 4)))
                state[("opev", tt % 2)] = ev
                state[("x_guard", tt)] = ev
            state["sub_done_dv"] = C["dv"]
            state["sub_done_pe"] = C["pe"]
            state["sub_done_po"] = C["po"]

        if 2 in sublayers:
            with scope("ffn1"):
                ffn_sublayer(x_ready=state.get("sub_done_dv", 0),
                             arena_pe=state.get("sub_done_pe", 0),
                             scratch_po=state.get("sub_done_po", 0))
            state["sub_done_dv"] = C["dv"]
            state["sub_done_pe"] = C["pe"]
            state["sub_done_po"] = C["po"]

        if 3 in sublayers:
            sd, sp, spo = (state.get("sub_done_dv", 0), state.get("sub_done_pe", 0),
                           state.get("sub_done_po", 0))
            ln3_dv, _ = layer_norm(double_ln=True, affine=True, x_ready=sd,
                                   arena_pe=sp, scratch_po=spo)
            wk_po = stream_w_resident(dWKB, gate_sem="pe",
                                      gate_val=state["scr_free_pe"])
            k_ac, _ = projection(kb_own, bk_c, ln3_dv, wk_po)
            kproj_pe = C["pe"]
            for oc in range(DC):
                w("gpsimd", "ac", k_ac)
                do("gpsimd", "qC", lambda e, oc=oc: e.dma_start(
                    out=dKSEND[oc * 128:(oc + 1) * 128, :],
                    in_=kb_own[:, oc * 1024:(oc + 1) * 1024]), 16)
            ksend_q = C["qC"]
            w("gpsimd", "qC", ksend_q)
            if cfg.get("self_pair"):
                do("gpsimd", "cc0", lambda e: e.dma_start(
                    out=dKRECV[0:D, :], in_=dKSEND), 16)
                do("gpsimd", "cc0", lambda e: e.dma_start(
                    out=dKRECV[D:2 * D, :], in_=dKSEND), 16)
                cc0_done = 32
            else:
                do("gpsimd", "cc0", lambda e: e.collective_compute(
                    "AllGather", ALU.bypass,
                    replica_groups=[[0, 1], [2, 3], [4, 5], [6, 7]],
                    ins=[dKSEND.opt()], outs=[dKRECV.opt()]), 1)
                cc0_done = 1
            w("sync", "pe", kproj_pe)
            wv_po = stream_w_resident(dWVB)
            w("scalar", "qC", ksend_q)
            v_ac, _ = projection(kb_own, bv_c, ln3_dv, wv_po, extra_pe_wait=kproj_pe)
            vstg = scr[:, 0:4096].bitcast(BF16)
            for i in range(TI):
                for g in range(4):
                    tb = state["tb"]
                    state["tb"] ^= 1
                    tbank = 4 + tb
                    w("tensor", "dv", state.get(("tev", tb), 0))
                    w("tensor", "ac", v_ac)
                    for jj in range(4):
                        oc = g * 4 + jj
                        pe_v = do("tensor", "pe", lambda e, oc=oc, jj=jj, i=i, tbank=tbank: e.transpose(
                            out=bank_bf(tbank)[:, jj * 128:(jj + 1) * 128],
                            in_=kb_own[:, oc * 1024 + i * 128: oc * 1024 + (i + 1) * 128],
                            identity=ident))
                    w("vector", "pe", pe_v)
                    if i == 0 and g == 0:
                        w("vector", "po", C["po"])
                        w("vector", "qC", ksend_q)
                    w("vector", "qC", state.get("vstg_q", 0))
                    ev = do("vector", "dv", lambda e, g=g, tbank=tbank: e.tensor_copy(
                        out=vstg[:, g * 512:(g + 1) * 512],
                        in_=bank_bf(tbank)[:, 0:512]))
                    state[("tev", tb)] = ev
                w("gpsimd", "dv", C["dv"])
                qv = do("gpsimd", "qC", lambda e, i=i: e.dma_start(
                    out=dVSEND[i * 128:(i + 1) * 128, :], in_=vstg[:]), 16)
                state["vstg_q"] = qv
            vsend_q = C["qC"]
            w("gpsimd", "qC", vsend_q)
            if cfg.get("self_pair"):
                do("gpsimd", "cc1", lambda e: e.dma_start(
                    out=dVRECV[0:1024, :], in_=dVSEND), 16)
                do("gpsimd", "cc1", lambda e: e.dma_start(
                    out=dVRECV[1024:2048, :], in_=dVSEND), 16)
                cc1_done = 32
            else:
                do("gpsimd", "cc1", lambda e: e.collective_compute(
                    "AllGather", ALU.bypass,
                    replica_groups=[[0, 1], [2, 3], [4, 5], [6, 7]],
                    ins=[dVSEND.opt()], outs=[dVRECV.opt()]), 1)
                cc1_done = 1
            wq_po2 = stream_w_resident(dWQB, gate_sem="qC", gate_val=vsend_q)
            w("scalar", "qC", vsend_q)
            w("scalar", "dv", bqs_dv)
            q_ac2, q_dv2 = projection(qa, bqs_c, ln3_dv, wq_po2, extra_pe_wait=kproj_pe,
                                      scale_imm=SCALE)
            qproj_pe = C["pe"]
            state[("ptm_pe", 0)] = qproj_pe
            state[("ptm_pe", 1)] = qproj_pe
            state["pt_free_pe"] = qproj_pe
            kb_h = [u8_bf(ar4, 0, 2048), u8_bf(ar4, 4096, 2048)]
            vb_h = [u8_bf(ar4, 8192, 2048), u8_bf(ar4, 16384, 2048)]
            kv_q = {}

            def assemble_head(h):
                slot = h % 2
                qsem = "qA%d" % slot
                w("scalar", "cc0", cc0_done)
                w("scalar", "cc1", cc1_done)
                w("scalar", "pe", qproj_pe)
                w("scalar", "dv", state.get(("kvdone_b", slot), 0))
                for j in range(16):
                    r, it = j % 2, j // 2
                    do("scalar", qsem, lambda e, h=h, j=j, r=r, it=it, slot=slot: e.dma_start(
                        out=kb_h[slot][:, j * 128:(j + 1) * 128],
                        in_=dKRECV[r * D + h * 128: r * D + (h + 1) * 128,
                                   it * 128:(it + 1) * 128]), 16)
                for j in range(16):
                    r, it = j % 2, j // 2
                    do("scalar", qsem, lambda e, h=h, j=j, r=r, it=it, slot=slot: e.dma_start(
                        out=vb_h[slot][:, j * 128:(j + 1) * 128],
                        in_=dVRECV[r * 1024 + it * 128: r * 1024 + (it + 1) * 128,
                                   h * 128:(h + 1) * 128]), 16)
                kv_q[h] = C[qsem]

            assemble_head(0)
            if n_heads > 1:
                assemble_head(1)
            ctx_h3 = scope("s3_heads")
            ctx_h3.__enter__()
            pend_weave = []
            pend_av = None

            def mk_av_b(h, i, slot):
                def go(h=h, i=i, slot=slot):
                    res = emit_av_pair(h, i, vb_h[slot], (2 * i + 2) * 128,
                                       (2 * i) * 128,
                                       av_dst_col=h * 1024 + i * 128)
                    if i == n_tiles - 1:
                        state[("kvdone_b", h % 2)] = res
                        if h + 2 < n_heads:
                            assemble_head(h + 2)
                return go
            for h in range(n_heads):
                slot = h % 2
                for i in range(n_tiles):
                    u = h * 8 + i
                    ptm = ptm0 if i % 2 == 0 else ptm1
                    nk_pad = (2 * i + 2) * 128
                    q_lhsT = qa[:, h * 1024 + i * 128: h * 1024 + (i + 1) * 128]
                    w("tensor", "ac", q_ac2)
                    w("tensor", "dv", q_dv2)
                    w("tensor", "qA%d" % slot, kv_q[h])
                    masks = ((2 * i, maskA), (2 * i + 1, maskB))
                    weave = pend_weave
                    if pend_av is not None:
                        weave = weave + [pend_av]
                        pend_av = None
                    unit = attn_unit(h, i, u, q_lhsT, kb_h[slot], nk_pad, ptm,
                                     masks=masks, slot=i % 2, weave=weave)
                    pend_weave = make_T_groups(h, i, u, unit, ptm, nk_pad,
                                               slot=i % 2)
                    if i % 2 == 1:
                        pend_av = mk_av_b(h, i, slot)
            for f in pend_weave:
                f()
            if pend_av is not None:
                pend_av()
            pend_weave, pend_av = [], None
            ctx_h3.__exit__(None, None, None)
            av_done = C["dv"]
            mv = None
            for qq in range(4):
                w("vector", "dv", av_done)
                mv = do("vector", "dv", lambda e, qq=qq: e.tensor_copy(
                    out=av_m[:, qq * 4096:(qq + 1) * 4096],
                    in_=av[:, qq * 4096:(qq + 1) * 4096]))
            av_mv = mv
            w("sync", "dv", av_mv)
            wo_po2 = stream_w_resident(dWOB)
            bo_row2 = u8_bf(ar2, 0, 2048)
            bo32 = u8_f32(ar2, 8192, 2048)
            headloop_pe = C["pe"]
            w("sync", "pe", headloop_pe)
            bo_q = do("sync", "qX", lambda e: e.dma_start(out=bo32[0:1, :], in_=dBO), 16)
            w("gpsimd", "qX", bo_q)
            w("gpsimd", "pe", headloop_pe)
            bo_po = do("gpsimd", "po", lambda e: e.tensor_copy(
                out=bo_row2[0:1, :], in_=bo32[0:1, :]))
            for tt in range(n_tiles):
                pb = 4 * (tt % 2)
                w("tensor", "dv", state.get(("opev", tt % 2), 0))
                w("tensor", "dv", av_mv)
                wait_ready("tensor", wo_po2)
                for hc in range(n_heads):
                    for j in range(4):
                        pe_v = do("tensor", "pe", lambda e, hc=hc, j=j, tt=tt, pb=pb: e.matmul(
                            out=bank(pb + j),
                            lhsT=av_m[:, hc * 1024 + tt * 128: hc * 1024 + (tt + 1) * 128],
                            rhs=w16_tile(hc)[:, j * 512:(j + 1) * 512],
                            start=(hc == 0), stop=False))
                w("tensor", "po", bo_po)
                w("tensor", "qX", C["qX"])
                for j in range(4):
                    pe_v = do("tensor", "pe", lambda e, j=j, pb=pb: e.matmul(
                        out=bank(pb + j), lhsT=ones1[0:1, :],
                        rhs=bo_row2[0:1, j * 512:(j + 1) * 512],
                        start=False, stop=True))
                w("vector", "pe", pe_v)
                w("vector", "dv", state.get(("x_guard", tt), 0))
                ev = do("vector", "dv", lambda e, tt=tt, pb=pb: e.tensor_add(
                    out=x_sb[:, tt * D:(tt + 1) * D],
                    in0=x_sb[:, tt * D:(tt + 1) * D], in1=bank(pb, 4)))
                state[("opev", tt % 2)] = ev
                state[("x_guard", tt)] = ev
            state["sub_done_dv"] = C["dv"]
            state["sub_done_pe"] = C["pe"]
            state["sub_done_po"] = C["po"]

        if 4 in sublayers:
            with scope("ffn2"):
                ffn_sublayer(x_ready=state.get("sub_done_dv", 0),
                             arena_pe=state.get("sub_done_pe", 0),
                             scratch_po=state.get("sub_done_po", 0))
            state["sub_done_dv"] = C["dv"]
            state["sub_done_pe"] = C["pe"]
            state["sub_done_po"] = C["po"]

        xfin = C["dv"]
        for i in range(TI):
            w("sync", "dv", xfin)
            do("sync", "qY", lambda e, i=i: e.dma_start(
                out=dY[i * 128:(i + 1) * 128, :], in_=x_sb[:, i * D:(i + 1) * D]), 16)
        w("sync", "qY", C["qY"])

    def mk(name):
        def f(eng):
            emit(name, eng)
        return f

    block.sync(mk("sync"))
    block.tensor(mk("tensor"))
    block.vector(mk("vector"))
    block.scalar(mk("scalar"))
    block.gpsimd(mk("gpsimd"))

    stack.close()
    return nc


def prep_inputs(inputs, pc_list=None):
    q = np.asarray(inputs["L0_attn_a_query"], dtype=np.float32)
    kvk = np.asarray(inputs["L0_attn_a_kv_k"], dtype=np.float32).reshape(KVLEN, D)
    kvv = np.asarray(inputs["L0_attn_a_kv_v"], dtype=np.float32).reshape(KVLEN, D)
    kvkT = np.ascontiguousarray(kvk.T)
    kvv_b = np.ascontiguousarray(
        kvv.reshape(32, 128, 16, 128).transpose(2, 1, 0, 3).reshape(NH * 128, KVLEN))
    wqaT = np.ascontiguousarray(np.asarray(inputs["L0_attn_a_Wq"], np.float32).T)
    woaT = np.ascontiguousarray(np.asarray(inputs["L0_attn_a_Wo"], np.float32).T)
    wqb = np.ascontiguousarray(np.asarray(inputs["L0_attn_b_Wq"], np.float32))
    wkb = np.ascontiguousarray(np.asarray(inputs["L0_attn_b_Wk"], np.float32))
    wvb = np.ascontiguousarray(np.asarray(inputs["L0_attn_b_Wv"], np.float32))
    wob = np.ascontiguousarray(np.asarray(inputs["L0_attn_b_Wo"], np.float32))
    bq = np.ascontiguousarray(np.asarray(inputs["L0_attn_b_bq"], np.float32).reshape(16, 128).T)
    bk = np.ascontiguousarray(np.asarray(inputs["L0_attn_b_bk"], np.float32).reshape(16, 128).T)
    bv = np.ascontiguousarray(np.asarray(inputs["L0_attn_b_bv"], np.float32).reshape(16, 128).T)
    bo = np.asarray(inputs["L0_attn_b_bo"], np.float32).reshape(1, D)
    lng = np.ascontiguousarray(np.asarray(inputs["L0_attn_b_ln_g"], np.float32).reshape(16, 128).T)
    lnb = np.ascontiguousarray(np.asarray(inputs["L0_attn_b_ln_b"], np.float32).reshape(16, 128).T)
    wg = np.asarray(inputs["L0_ffn_w_gate"], np.float32)
    wu = np.asarray(inputs["L0_ffn_w_up"], np.float32)
    wd = np.asarray(inputs["L0_ffn_w_down"], np.float32)
    wgT_r = np.ascontiguousarray(
        wg.reshape(64, 128, 16, 128).transpose(0, 3, 2, 1).reshape(FH, D))
    wuT_r = np.ascontiguousarray(
        wu.reshape(64, 128, 16, 128).transpose(0, 3, 2, 1).reshape(FH, D))
    wdT_r = np.ascontiguousarray(wd.T)
    ident = np.eye(128, dtype=BF)
    ones1 = np.ones((1, 128), dtype=BF)
    tril = np.where(np.tril(np.ones((128, 128), bool)), 0.0, NEG).astype(np.float32)
    full = np.full((128, 128), NEG, dtype=np.float32)
    zero = np.zeros((128, 128), dtype=np.float32)

    in_maps = []
    for c in range(8):
        b, pc = c // 2, c % 2
        x0 = np.ascontiguousarray(
            q[b].reshape(16, 128, D)[pc::2].reshape(TI * 128, D))
        maskA = tril if pc == 0 else zero
        maskB = full if pc == 0 else tril
        in_maps.append(dict(
            x0=x0, kvkT=kvkT, kvv_b=kvv_b, wqaT=wqaT, woaT=woaT,
            wqb=wqb, wkb=wkb, wvb=wvb, wob=wob,
            bqb=bq, bkb=bk, bvb=bv, bob=bo, lng=lng, lnb=lnb,
            wgT_r=wgT_r, wuT_r=wuT_r, wdT_r=wdT_r,
            maskA=maskA, maskB=maskB, ident=ident, ones1=ones1,
        ))
    return in_maps


def kernel(**inputs):
    cfg = default_cfg()
    cfg["sublayers"] = (1, 2, 3, 4)
    nc = build_program(cfg)
    in_maps = prep_inputs(inputs)
    res = run_bass_kernel_spmd(nc, in_maps, core_ids=list(range(8)))
    out = np.zeros((4, 2048, D), dtype=np.float32)
    for c in range(8):
        b, pc = c // 2, c % 2
        y = res.results[c]["y"].reshape(TI, 128, D)
        out[b].reshape(16, 128, D)[pc::2] = y
    return out


# version 08
# speedup vs baseline: 1.0876x; 1.0876x over previous
import numpy as np
import ml_dtypes
from contextlib import ExitStack

import concourse.bass as bass
import concourse.mybir as mybir
from concourse.bass_utils import run_bass_kernel_spmd

F32 = mybir.dt.float32
BF16 = mybir.dt.bfloat16
AX = mybir.AxisListType.X
ALU = mybir.AluOpType
AF = mybir.ActivationFunctionType
BF = ml_dtypes.bfloat16

D = 2048
TI = 8
DC = 16
NH = 16
DH = 128
KVLEN = 4096
FH = 8192
NHC = 64
SCALE = 1.0 / np.sqrt(128.0)
EPS_LN = 1e-5
NEG = -1.0e30

ENGS = ("sync", "tensor", "vector", "scalar", "gpsimd")


def default_cfg():
    return dict(
        n_heads=NH,
        n_tiles=TI,
        n_hc=NHC,
        sublayers=(1,),
        zero_arenas=False,
    )


class Em:

    def __init__(self, me, eng, sems):
        self.me = me
        self.eng = eng
        self.S = sems
        self.C = {k: 0 for k in sems}

    def w(self, owner, sem, val):
        if val > 0 and self.me == owner:
            self.eng.wait_ge(self.S[sem], val)

    def do(self, owner, sem, f, n=1):
        if sem is not None:
            self.C[sem] += n
        if self.me == owner:
            inst = f(self.eng)
            if sem is not None:
                inst.then_inc(self.S[sem], n)
        return self.C[sem] if sem is not None else None

    def do0(self, owner, f):
        if self.me == owner:
            f(self.eng)


def build_program(cfg):
    nc = bass.Bass("TRN2", target_bir_lowering=False, debug=False)

    dX = nc.dram_tensor("x0", [TI * 128, D], F32, kind="ExternalInput").ap()
    dKT = nc.dram_tensor("kvkT", [D, KVLEN], F32, kind="ExternalInput").ap()
    dVB = nc.dram_tensor("kvv_b", [NH * 128, KVLEN], F32, kind="ExternalInput").ap()
    dWQA = nc.dram_tensor("wqaT", [D, D], F32, kind="ExternalInput").ap()
    dWOA = nc.dram_tensor("woaT", [D, D], F32, kind="ExternalInput").ap()
    dWQB = nc.dram_tensor("wqb", [D, D], F32, kind="ExternalInput").ap()
    dWKB = nc.dram_tensor("wkb", [D, D], F32, kind="ExternalInput").ap()
    dWVB = nc.dram_tensor("wvb", [D, D], F32, kind="ExternalInput").ap()
    dWOB = nc.dram_tensor("wob", [D, D], F32, kind="ExternalInput").ap()
    dBQ = nc.dram_tensor("bqb", [128, 16], F32, kind="ExternalInput").ap()
    dBK = nc.dram_tensor("bkb", [128, 16], F32, kind="ExternalInput").ap()
    dBV = nc.dram_tensor("bvb", [128, 16], F32, kind="ExternalInput").ap()
    dBO = nc.dram_tensor("bob", [1, D], F32, kind="ExternalInput").ap()
    dLNG = nc.dram_tensor("lng", [128, 16], F32, kind="ExternalInput").ap()
    dLNB = nc.dram_tensor("lnb", [128, 16], F32, kind="ExternalInput").ap()
    dWG = nc.dram_tensor("wgT_r", [FH, D], F32, kind="ExternalInput").ap()
    dWU = nc.dram_tensor("wuT_r", [FH, D], F32, kind="ExternalInput").ap()
    dWD = nc.dram_tensor("wdT_r", [FH, D], F32, kind="ExternalInput").ap()
    dMA = nc.dram_tensor("maskA", [128, 128], F32, kind="ExternalInput").ap()
    dMB = nc.dram_tensor("maskB", [128, 128], F32, kind="ExternalInput").ap()
    dID = nc.dram_tensor("ident", [128, 128], BF16, kind="ExternalInput").ap()
    dONES = nc.dram_tensor("ones1", [1, 128], BF16, kind="ExternalInput").ap()
    dY = nc.dram_tensor("y", [TI * 128, D], F32, kind="ExternalOutput").ap()

    dKSEND = nc.dram_tensor("ksend", [D, 1024], BF16).ap()
    dKRECV = nc.dram_tensor("krecv", [2 * D, 1024], BF16).ap()
    dVSEND = nc.dram_tensor("vsend", [1024, D], BF16).ap()
    dVRECV = nc.dram_tensor("vrecv", [2 * 1024, D], BF16).ap()
    dWG16 = nc.dram_tensor("wg16", [FH, D], BF16).ap()
    dWU16 = nc.dram_tensor("wu16", [FH, D], BF16).ap()
    dWD16 = nc.dram_tensor("wd16", [FH, D], BF16).ap()

    stack = ExitStack()
    e = stack.enter_context

    x_sb = e(nc.sbuf_tensor("x_sb", [128, TI * D], F32))
    ar1 = e(nc.sbuf_tensor("ar1", [128, 32768], mybir.dt.uint8))
    ar2 = e(nc.sbuf_tensor("ar2", [128, 32768], mybir.dt.uint8))
    ar3 = e(nc.sbuf_tensor("ar3", [128, 32768], mybir.dt.uint8))
    ar4 = e(nc.sbuf_tensor("ar4", [128, 32768], mybir.dt.uint8))
    scr = e(nc.sbuf_tensor("scr", [128, 10240], mybir.dt.uint8))
    cst = e(nc.sbuf_tensor("cst", [128, 2048], mybir.dt.uint8))
    sta = e(nc.sbuf_tensor("sta", [128, 2048], mybir.dt.uint8))

    def u8_bf(t, off, n):
        return t[:, off:off + 2 * n].bitcast(BF16)

    def u8_f32(t, off, n):
        return t[:, off:off + 4 * n].bitcast(F32)

    lnT = u8_bf(ar1, 0, 16384)
    pt = u8_bf(ar1, 0, 8192)
    ptm0 = u8_bf(ar1, 16384, 4096)
    ptm1 = u8_bf(ar1, 24576, 4096)
    av_m = u8_bf(ar1, 0, 16384)
    bo_row = u8_bf(ar1, 32768 - 4096, 2048)

    qa = u8_bf(ar2, 0, 16384)
    gated = u8_bf(ar2, 0, 8192)
    sg0 = u8_bf(ar2, 16384, 1024)
    sg1 = u8_bf(ar2, 18432, 1024)
    ffw0 = u8_f32(ar2, 20480, 1024)
    ffw1 = u8_f32(ar2, 24576, 1024)
    kb_own = u8_bf(ar2, 0, 16384)

    av = u8_bf(ar3, 0, 16384)
    wd16 = u8_bf(ar3, 0, 16384)
    w16a = u8_bf(ar3, 0, 16384)

    w16b = u8_bf(ar4, 0, 16384)
    kh0 = u8_bf(ar4, 0, 4096)
    kh1 = u8_bf(ar4, 8192, 4096)
    vh0 = u8_bf(ar4, 16384, 4096)
    vh1 = u8_bf(ar4, 24576, 4096)
    wg16s = [u8_bf(ar4, 0, 2048), u8_bf(ar4, 4096, 2048)]
    wu16s = [u8_bf(ar4, 8192, 2048), u8_bf(ar4, 12288, 2048)]
    ffw2 = u8_f32(ar4, 16384, 1024)
    ffw3 = u8_f32(ar4, 20480, 1024)
    wo16s = [u8_bf(ar4, 0, 2048), u8_bf(ar4, 4096, 2048)]

    wst0 = u8_f32(scr, 0, 1024)
    wst1 = u8_f32(scr, 4096, 1024)
    ln_tm0 = u8_bf(ar2, 0, 2048)
    ln_tm1 = u8_bf(ar2, 4096, 2048)
    sqscr = u8_f32(scr, 8192, 512)

    maskA = u8_f32(cst, 0, 128)
    maskB = u8_f32(cst, 512, 128)
    ident = u8_bf(cst, 1024, 128)
    ones1 = u8_bf(cst, 1280, 128)
    bq_c = u8_f32(cst, 1536, 16)
    bk_c = u8_f32(cst, 1600, 16)
    bv_c = u8_f32(cst, 1664, 16)
    lng_c = u8_f32(cst, 1728, 16)
    lnb_c = u8_f32(cst, 1792, 16)
    eps_c = u8_f32(cst, 1856, 1)
    bqs_c = u8_f32(sta, 1680, 16)

    ssum = u8_f32(sta, 0, 16)
    sqac = u8_f32(sta, 64, 16)
    mu_s = u8_f32(sta, 128, 16)
    var_s = u8_f32(sta, 192, 16)
    rstd_s = u8_f32(sta, 256, 16)
    t0_s = u8_f32(sta, 320, 16)
    t1_s = u8_f32(sta, 384, 16)
    mbuf = u8_f32(sta, 448, 8)
    bias_b = u8_f32(sta, 480, 8)
    acc_b = u8_f32(sta, 512, 8)
    fv_b = u8_f32(sta, 544, 8)
    accf_b = u8_f32(sta, 576, 8)
    tot_b = u8_f32(sta, 608, 2)
    rinv = u8_f32(sta, 640, 128)
    rawm = u8_f32(sta, 1152, 4)
    avst0 = u8_bf(sta, 1168, 128)
    avst1 = u8_bf(sta, 1424, 128)

    ps = e(nc.psum_tensor("ps", [128, 4096], F32))

    def bank(k, n=1):
        return ps[:, 512 * k:512 * (k + n)]

    def bank_bf(k, n=1):
        return ps[:, 512 * k:512 * (k + n)].bitcast(BF16)

    sems = {}
    for name in ("qX", "qS0", "qS1", "qA0", "qA1", "qF0", "qF1", "qF2", "qF3",
                 "qW", "qC", "qK", "qY", "cc0", "cc1", "pe", "dv", "ac", "po"):
        sems[name] = e(nc.semaphore(name))

    block = e(nc.Block())

    n_heads = cfg["n_heads"]
    n_tiles = cfg["n_tiles"]
    n_hc = cfg["n_hc"]
    sublayers = cfg["sublayers"]

    def emit(me, eng):
        E = Em(me, eng, sems)
        w, do, do0, C = E.w, E.do, E.do0, E.C
        def scope(name):
            return nc.named_scope(name)

        xw = {}
        for i in range(2):
            do("sync", "qX", lambda e, i=i: e.dma_start(
                out=x_sb[:, i * D:(i + 1) * D], in_=dX[i * 128:(i + 1) * 128, :]), 16)
        xw[0] = C["qX"]
        for i in range(2, TI):
            do("sync", "qY", lambda e, i=i: e.dma_start(
                out=x_sb[:, i * D:(i + 1) * D], in_=dX[i * 128:(i + 1) * 128, :]), 16)
        xw[1] = C["qY"]
        for (dst, src) in ((maskA, dMA), (maskB, dMB), (ident, dID),
                           (bq_c, dBQ), (bk_c, dBK), (bv_c, dBV),
                           (lng_c, dLNG), (lnb_c, dLNB)):
            do("sync", "qK", lambda e, d=dst, s=src: e.dma_start(out=d, in_=s), 16)
        do("sync", "qK", lambda e: e.dma_start(out=ones1[0:1, :], in_=dONES), 16)
        consts_ok = C["qK"]
        x_loaded = [("qX", xw[0]) if i < 2 else ("qY", xw[1]) for i in range(TI)]
        if cfg.get("zero_arenas"):
            for t in (ar1, ar2, ar3, ar4):
                do("vector", "dv", lambda e, t=t: e.memset(t[:, :], 0))
        zero_dv = C["dv"]
        eps_dv = do("vector", "dv", lambda e: e.memset(eps_c, EPS_LN))
        w("vector", "qK", consts_ok)
        bqs_dv = do("vector", "dv", lambda e: e.tensor_scalar_mul(
            out=bqs_c[:], in0=bq_c[:], scalar1=SCALE))

        state = dict(tb=0, avb=0)

        def layer_norm(double_ln=False, affine=False, x_ready=0, arena_pe=0,
                       scratch_po=0):
            for i in range(n_tiles):
                sl = i % 2
                ln_tm = ln_tm0 if sl == 0 else ln_tm1
                xi = x_sb[:, i * D:(i + 1) * D]
                w("vector", x_loaded[i][0], x_loaded[i][1])
                w("vector", "dv", x_ready)
                v_sum = do("vector", "dv", lambda e, xi=xi, i=i: e.reduce_sum(
                    out=ssum[:, i:i + 1], in_=xi, axis=AX))
                w("scalar", x_loaded[i][0], x_loaded[i][1])
                w("scalar", "dv", x_ready)
                a_sq0 = do("scalar", "ac", lambda e, xi=xi, i=i: e.activation(
                    out=sqscr[:], in_=xi[:, 0:512], func=AF.Square,
                    accum_out=sqac[:, 2 * (i % 8):2 * (i % 8) + 1]))
                w("scalar", "ac", a_sq0)
                a_sq1 = do("scalar", "ac", lambda e, xi=xi, i=i: e.activation(
                    out=sqscr[:], in_=xi[:, 512:1024], func=AF.Square,
                    accum_out=t0_s[:, i:i + 1]))
                w("scalar", "ac", a_sq1)
                a_sq2 = do("scalar", "ac", lambda e, xi=xi, i=i: e.activation(
                    out=sqscr[:], in_=xi[:, 1024:1536], func=AF.Square,
                    accum_out=t1_s[:, i:i + 1]))
                w("scalar", "ac", a_sq2)
                a_sq3 = do("scalar", "ac", lambda e, xi=xi, i=i: e.activation(
                    out=sqscr[:], in_=xi[:, 1536:2048], func=AF.Square,
                    accum_out=sqac[:, 2 * (i % 8) + 1:2 * (i % 8) + 2]))

                def stats_chain(i, v_sum, a_sq3, pfx_mu, pfx_var, pfx_rstd):
                    w("vector", "dv", v_sum)
                    v_mu = do("vector", "dv", lambda e, i=i: e.tensor_scalar_mul(
                        out=mu_s[:, i:i + 1], in0=ssum[:, i:i + 1], scalar1=1.0 / D))
                    w("vector", "ac", a_sq3)
                    v_s2 = do("vector", "dv", lambda e, i=i: e.tensor_add(
                        out=sqac[:, 2 * (i % 8):2 * (i % 8) + 1],
                        in0=sqac[:, 2 * (i % 8):2 * (i % 8) + 1],
                        in1=t0_s[:, i:i + 1]))
                    w("vector", "dv", v_s2)
                    v_s2b = do("vector", "dv", lambda e, i=i: e.tensor_add(
                        out=t1_s[:, i:i + 1], in0=t1_s[:, i:i + 1],
                        in1=sqac[:, 2 * (i % 8) + 1:2 * (i % 8) + 2]))
                    w("vector", "dv", v_s2b)
                    v_s2c = do("vector", "dv", lambda e, i=i: e.tensor_add(
                        out=sqac[:, 2 * (i % 8):2 * (i % 8) + 1],
                        in0=sqac[:, 2 * (i % 8):2 * (i % 8) + 1],
                        in1=t1_s[:, i:i + 1]))
                    w("vector", "dv", max(v_mu, v_s2c))
                    v_mu2 = do("vector", "dv", lambda e, i=i: e.tensor_mul(
                        out=t0_s[:, i:i + 1], in0=mu_s[:, i:i + 1], in1=mu_s[:, i:i + 1]))
                    w("vector", "dv", v_mu2)
                    v_var = do("vector", "dv", lambda e, i=i: e.scalar_tensor_tensor(
                        out=var_s[:, i:i + 1], in0=sqac[:, 2 * (i % 8):2 * (i % 8) + 1],
                        scalar=1.0 / D, in1=t0_s[:, i:i + 1],
                        op0=ALU.mult, op1=ALU.subtract))
                    w("scalar", "dv", v_var)
                    w("scalar", "dv", eps_dv)
                    a_ln = do("scalar", "ac", lambda e, i=i: e.activation(
                        out=t1_s[:, i:i + 1], in_=var_s[:, i:i + 1], func=AF.Ln,
                        bias=eps_c[:, 0:1]))
                    w("scalar", "ac", a_ln)
                    a_rs = do("scalar", "ac", lambda e, i=i: e.activation(
                        out=rstd_s[:, i:i + 1], in_=t1_s[:, i:i + 1], func=AF.Exp,
                        scale=-0.5))
                    return a_rs

                a_rs = stats_chain(i, v_sum, a_sq3, mu_s, var_s, rstd_s)

                if not double_ln:
                    w("vector", "ac", a_rs)
                    if i < 2:
                        w("vector", "pe", arena_pe)
                    v_ln = do("vector", "dv", lambda e, xi=xi, i=i, lt=ln_tm: e.tensor_scalar(
                        out=lt[:], in0=xi, scalar1=mu_s[:, i:i + 1],
                        scalar2=rstd_s[:, i:i + 1], op0=ALU.subtract, op1=ALU.mult))
                else:
                    y32 = ar2[:, 8192:16384].bitcast(F32)
                    w("vector", "ac", a_rs)
                    if i < 2:
                        w("vector", "pe", arena_pe)
                    v_y = do("vector", "dv", lambda e, xi=xi, i=i, y=y32: e.tensor_scalar(
                        out=y[:], in0=xi, scalar1=mu_s[:, i:i + 1],
                        scalar2=rstd_s[:, i:i + 1], op0=ALU.subtract, op1=ALU.mult))
                    w("vector", "dv", v_y)
                    v_sum2 = do("vector", "dv", lambda e, y=y32, i=i: e.reduce_sum(
                        out=ssum[:, 8 + (i % 8):9 + (i % 8)], in_=y, axis=AX))
                    w("scalar", "dv", v_y)
                    a_q0 = do("scalar", "ac", lambda e, y=y32, i=i: e.activation(
                        out=sqscr[:], in_=y[:, 0:512], func=AF.Square,
                        accum_out=sqac[:, 2 * (i % 8):2 * (i % 8) + 1]))
                    w("scalar", "ac", a_q0)
                    a_q1 = do("scalar", "ac", lambda e, y=y32, i=i: e.activation(
                        out=sqscr[:], in_=y[:, 512:1024], func=AF.Square,
                        accum_out=t0_s[:, i:i + 1]))
                    w("scalar", "ac", a_q1)
                    a_q2 = do("scalar", "ac", lambda e, y=y32, i=i: e.activation(
                        out=sqscr[:], in_=y[:, 1024:1536], func=AF.Square,
                        accum_out=t1_s[:, i:i + 1]))
                    w("scalar", "ac", a_q2)
                    a_q3 = do("scalar", "ac", lambda e, y=y32, i=i: e.activation(
                        out=sqscr[:], in_=y[:, 1536:2048], func=AF.Square,
                        accum_out=sqac[:, 2 * (i % 8) + 1:2 * (i % 8) + 2]))
                    w("vector", "dv", v_sum2)
                    v_mu = do("vector", "dv", lambda e, i=i: e.tensor_scalar_mul(
                        out=mu_s[:, 8 + (i % 8):9 + (i % 8)],
                        in0=ssum[:, 8 + (i % 8):9 + (i % 8)], scalar1=1.0 / D))
                    w("vector", "ac", a_q3)
                    v_a = do("vector", "dv", lambda e, i=i: e.tensor_add(
                        out=sqac[:, 2 * (i % 8):2 * (i % 8) + 1],
                        in0=sqac[:, 2 * (i % 8):2 * (i % 8) + 1], in1=t0_s[:, i:i + 1]))
                    w("vector", "dv", v_a)
                    v_b = do("vector", "dv", lambda e, i=i: e.tensor_add(
                        out=t1_s[:, i:i + 1], in0=t1_s[:, i:i + 1],
                        in1=sqac[:, 2 * (i % 8) + 1:2 * (i % 8) + 2]))
                    w("vector", "dv", v_b)
                    v_c = do("vector", "dv", lambda e, i=i: e.tensor_add(
                        out=sqac[:, 2 * (i % 8):2 * (i % 8) + 1],
                        in0=sqac[:, 2 * (i % 8):2 * (i % 8) + 1], in1=t1_s[:, i:i + 1]))
                    w("vector", "dv", max(v_mu, v_c))
                    v_mu2 = do("vector", "dv", lambda e, i=i: e.tensor_mul(
                        out=t0_s[:, i:i + 1], in0=mu_s[:, 8 + (i % 8):9 + (i % 8)],
                        in1=mu_s[:, 8 + (i % 8):9 + (i % 8)]))
                    w("vector", "dv", v_mu2)
                    v_var = do("vector", "dv", lambda e, i=i: e.scalar_tensor_tensor(
                        out=var_s[:, 8 + (i % 8):9 + (i % 8)],
                        in0=sqac[:, 2 * (i % 8):2 * (i % 8) + 1],
                        scalar=1.0 / D, in1=t0_s[:, i:i + 1],
                        op0=ALU.mult, op1=ALU.subtract))
                    w("scalar", "dv", v_var)
                    w("scalar", "dv", eps_dv)
                    a_l2 = do("scalar", "ac", lambda e, i=i: e.activation(
                        out=t1_s[:, i:i + 1], in_=var_s[:, 8 + (i % 8):9 + (i % 8)],
                        func=AF.Ln, bias=eps_c[:, 0:1]))
                    w("scalar", "ac", a_l2)
                    a_rs2 = do("scalar", "ac", lambda e, i=i: e.activation(
                        out=rstd_s[:, 8 + (i % 8):9 + (i % 8)], in_=t1_s[:, i:i + 1],
                        func=AF.Exp, scale=-0.5))
                    w("vector", "ac", a_rs2)
                    v_ln = do("vector", "dv", lambda e, y=y32, i=i, lt=ln_tm: e.tensor_scalar(
                        out=lt[:], in0=y, scalar1=mu_s[:, 8 + (i % 8):9 + (i % 8)],
                        scalar2=rstd_s[:, 8 + (i % 8):9 + (i % 8)],
                        op0=ALU.subtract, op1=ALU.mult))

                w("tensor", "qK", consts_ok)
                w("tensor", "dv", zero_dv)
                w("tensor", "pe", arena_pe)
                for g in range(4):
                    tb = state["tb"]
                    state["tb"] ^= 1
                    tbank = 4 + tb
                    w("tensor", "dv", state.get(("tev", tb), 0))
                    w("tensor", "dv", v_ln)
                    for j in range(4):
                        dc = g * 4 + j
                        pe_v = do("tensor", "pe", lambda e, lt=ln_tm, dc=dc, j=j, tbank=tbank: e.transpose(
                            out=bank_bf(tbank)[:, j * 128:(j + 1) * 128],
                            in_=lt[:, dc * 128:(dc + 1) * 128], identity=ident))
                    w("vector", "pe", pe_v)
                    if affine:
                        for j in range(4):
                            dc = g * 4 + j
                            ev = do("vector", "dv", lambda e, dc=dc, j=j, i=i, tbank=tbank: e.tensor_scalar(
                                out=lnT[:, dc * 1024 + i * 128: dc * 1024 + (i + 1) * 128],
                                in0=bank_bf(tbank)[:, j * 128:(j + 1) * 128],
                                scalar1=lng_c[:, dc:dc + 1], scalar2=lnb_c[:, dc:dc + 1],
                                op0=ALU.mult, op1=ALU.add))
                    else:
                        ev = do("vector", "dv", lambda e, g=g, i=i, tbank=tbank: e.tensor_copy(
                            out=lnT.rearrange("p (c n) -> p c n", n=1024)[
                                :, g * 4:(g + 1) * 4, i * 128:(i + 1) * 128],
                            in_=bank_bf(tbank).rearrange("p (c n) -> p c n", n=128)[:, 0:4, :]))
                    state[("tev", tb)] = ev
            state["scr_free_pe"] = C["pe"]
            return C["dv"], C["ac"]

        def w16_tile(dcc):
            if dcc < 8:
                return w16a[:, dcc * 2048:(dcc + 1) * 2048]
            return w16b[:, (dcc - 8) * 2048:(dcc - 8 + 1) * 2048]

        CAST_SEM = {"gpsimd": "po", "vector": "dv", "scalar": "ac"}

        def cast_to(engx, dst, srcap, waits=()):
            for (s, v) in waits:
                w(engx, s, v)
            sem = CAST_SEM[engx]
            if engx == "scalar":
                v = do(engx, sem, lambda e, dst=dst, srcap=srcap: e.copy(
                    out=dst, in_=srcap))
            else:
                v = do(engx, sem, lambda e, dst=dst, srcap=srcap: e.tensor_copy(
                    out=dst, in_=srcap))
            return (sem, v)

        def stream_w_resident(src, gate_sem=None, gate_val=0, dst_tile=None,
                              cast_engs=("gpsimd",)):
            if dst_tile is None:
                dst_tile = w16_tile
            last = None
            for dcc in range(DC):
                for hhalf in range(2):
                    k = dcc * 2 + hhalf
                    p = k % 2
                    st = wst0 if p == 0 else wst1
                    if gate_sem is not None and dcc == 0:
                        w("sync", gate_sem, gate_val)
                    ssem, sval = state.get(("stage", p), ("po", 0))
                    w("sync", ssem, sval)
                    qsem = "qS%d" % p
                    qv = do("sync", qsem, lambda e, st=st, dcc=dcc, hh=hhalf, src=src: e.dma_start(
                        out=st, in_=src[dcc * 128:(dcc + 1) * 128, hh * 1024:(hh + 1) * 1024]), 16)
                    engx = cast_engs[k % len(cast_engs)]
                    last = cast_to(engx, dst_tile(dcc)[:, hhalf * 1024:(hhalf + 1) * 1024],
                                   st, waits=[(qsem, qv)])
                    state[("stage", p)] = last
            return dict(po=C["po"], dv=C["dv"], ac=C["ac"])

        def wait_ready(owner, r):
            if isinstance(r, dict):
                w(owner, "po", r["po"])
                w(owner, "dv", r["dv"])
                w(owner, "ac", r["ac"])
            else:
                w(owner, "po", r)

        def projection(dst, bias_col, lnT_ready_dv, w_po, extra_pe_wait=0, n_oc=DC,
                       scale_imm=None):
            for oc in range(n_oc):
                for th in range(2):
                    pbk = (2 * oc + th) % 4
                    psem, pval = state.get(("pjev", pbk), ("ac", 0))
                    w("tensor", psem, pval)
                    w("tensor", "dv", lnT_ready_dv)
                    wait_ready("tensor", w_po)
                    w("tensor", "pe", extra_pe_wait)
                    for dcc in range(DC):
                        pe_v = do("tensor", "pe", lambda e, oc=oc, th=th, dcc=dcc, pbk=pbk: e.matmul(
                            out=bank(pbk),
                            lhsT=w16_tile(dcc)[:, oc * 128:(oc + 1) * 128],
                            rhs=lnT[:, dcc * 1024 + th * 512: dcc * 1024 + (th + 1) * 512],
                            start=(dcc == 0), stop=(dcc == DC - 1)))
                    if scale_imm is not None and bias_col is not None:
                        w("vector", "pe", pe_v)
                        evd = do("vector", "dv", lambda e, oc=oc, th=th, pbk=pbk, dst=dst, b=bias_col, s=scale_imm: e.tensor_scalar(
                            out=dst[:, oc * 1024 + th * 512: oc * 1024 + (th + 1) * 512],
                            in0=bank(pbk), scalar1=s, scalar2=b[:, oc:oc + 1],
                            op0=ALU.mult, op1=ALU.add))
                        state[("pjev", pbk)] = ("dv", evd)
                        continue
                    w("scalar", "pe", pe_v)
                    if scale_imm is not None:
                        ev = do("scalar", "ac", lambda e, oc=oc, th=th, pbk=pbk, dst=dst, s=scale_imm: e.activation(
                            out=dst[:, oc * 1024 + th * 512: oc * 1024 + (th + 1) * 512],
                            in_=bank(pbk), func=AF.Copy, scale=s, bias=0.0))
                    elif bias_col is None:
                        ev = do("scalar", "ac", lambda e, oc=oc, th=th, pbk=pbk, dst=dst: e.copy(
                            out=dst[:, oc * 1024 + th * 512: oc * 1024 + (th + 1) * 512],
                            in_=bank(pbk)))
                    else:
                        ev = do("scalar", "ac", lambda e, oc=oc, th=th, pbk=pbk, dst=dst, b=bias_col: e.add(
                            out=dst[:, oc * 1024 + th * 512: oc * 1024 + (th + 1) * 512],
                            in_=bank(pbk),
                            add=b[:, oc:oc + 1]))
                    state[("pjev", pbk)] = ("ac", ev)
            return (C["ac"], C["dv"])

        def attn_unit(h, i, u, q_lhsT, kT, nk, ptm, masks, slot, weave=()):
            CN = 1024
            nch = (nk + CN - 1) // CN
            exp_acs = []
            first_dve = [True]
            wv = list(weave)

            def unit_dve_gate():
                if first_dve[0]:
                    w("vector", "dv", state.get("unit_dv", 0))
                    first_dve[0] = False

            v_m = None
            for c in range(nch):
                k0 = c * CN
                k1 = min(nk, k0 + CN)
                par = c % 2
                sb = 2 * par
                w("tensor", "ac", state.get(("sxp", par), 0))
                n0 = k1 - k0
                pe_v = None
                for j in range(0, n0, 512):
                    nn = min(512, n0 - j)
                    pe_v = do("tensor", "pe", lambda e, sb=sb, j=j, nn=nn, k0=k0, q=q_lhsT, kT=kT: e.matmul(
                        out=bank(sb, 2)[:, j:j + nn], lhsT=q,
                        rhs=kT[:, k0 + j:k0 + j + nn], start=True, stop=True))
                if wv:
                    wv.pop(0)()
                mv = None
                for (bi, m_ap) in masks:
                    b0 = bi * 128
                    if b0 >= k0 and b0 < k1:
                        w("vector", "pe", pe_v)
                        unit_dve_gate()
                        mv = do("vector", "dv", lambda e, sb=sb, b0=b0, k0=k0, m=m_ap: e.tensor_add(
                            out=bank(sb, 2)[:, b0 - k0:b0 - k0 + 128],
                            in0=bank(sb, 2)[:, b0 - k0:b0 - k0 + 128], in1=m))
                w("vector", "pe", pe_v)
                unit_dve_gate()
                if mv is not None:
                    w("vector", "dv", mv)
                if c == 0:
                    v_m = do("vector", "dv", lambda e, sb=sb, n0=n0: e.reduce_max(
                        out=mbuf[:, 0:1], in_=bank(sb, 2)[:, 0:n0], axis=AX, negate=True))
                else:
                    w("vector", "dv", v_m)
                    v_r = do("vector", "dv", lambda e, sb=sb, n0=n0: e.reduce_max(
                        out=rawm[:, 0:1], in_=bank(sb, 2)[:, 0:n0], axis=AX, negate=True))
                    w("vector", "dv", v_r)
                    v_m = do("vector", "dv", lambda e, c=c: e.tensor_scalar(
                        out=mbuf[:, c:c + 1], in0=rawm[:, 0:1],
                        scalar1=mbuf[:, c - 1:c], scalar2=None, op0=ALU.min))
                w("scalar", "dv", v_m)
                if c == 0:
                    w("scalar", "pe", state.get(("ptm_pe", slot), 0))
                a_e = do("scalar", "ac", lambda e, sb=sb, c=c, k0=k0, n0=n0, ptm=ptm: e.activation(
                    out=ptm[:, k0:k0 + n0], in_=bank(sb, 2)[:, 0:n0], func=AF.Exp,
                    bias=mbuf[:, c:c + 1], accum_out=acc_b[:, c:c + 1]))
                state[("sxp", par)] = a_e
                exp_acs.append(a_e)
            for f in wv:
                f()
            last = nch - 1
            w("scalar", "dv", v_m)
            w("scalar", "ac", exp_acs[-1])
            a_f = do("scalar", "ac", lambda e, nch=nch, last=last: e.activation(
                out=fv_b[:, 0:nch], in_=mbuf[:, 0:nch], func=AF.Exp, scale=-1.0,
                bias=mbuf[:, last:last + 1]))
            w("vector", "ac", a_f)
            v_t = do("vector", "dv", lambda e, nch=nch: e.scalar_tensor_tensor(
                out=accf_b[:, 0:nch], in0=acc_b[:, 0:nch], scalar=1.0,
                in1=fv_b[:, 0:nch], op0=ALU.mult, op1=ALU.mult,
                accum_out=tot_b[:, 0:1]))
            w("vector", "dv", v_t)
            v_ri = do("vector", "dv", lambda e: e.reciprocal(
                out=tot_b[:, 1:2], in_=tot_b[:, 0:1]))
            w("vector", "dv", v_ri)
            corr_dv = C["dv"]
            for c in range(nch):
                k0 = c * CN
                k1 = min(nk, k0 + CN)
                corr_dv = do("vector", "dv", lambda e, c=c, k0=k0, k1=k1, ptm=ptm: e.tensor_scalar(
                    out=ptm[:, k0:k1], in0=ptm[:, k0:k1], scalar1=fv_b[:, c:c + 1],
                    scalar2=tot_b[:, 1:2], op0=ALU.mult, op1=ALU.mult))
            state["unit_dv"] = corr_dv
            return dict(exp_ac=exp_acs[-1], corr_dv=corr_dv)

        def make_T_groups(h, i, u, unit, ptm, nk, slot):
            nkc = nk // 128
            ip = i % 2
            ngr = (nkc + 7) // 8
            closures = []

            def grp(g):
                tb = state["tb"]
                state["tb"] ^= 1
                tbank = 4 + tb
                w("tensor", "dv", state.get(("tev", tb), 0))
                w("tensor", "ac", state.get(("tev_ac", tb), 0))
                w("tensor", "ac", unit["exp_ac"])
                w("tensor", "dv", unit["corr_dv"])
                ntr = min(8, nkc - g * 8)
                pe_v = None
                for j in range(ntr):
                    kc = g * 8 + j
                    pe_v = do("tensor", "pe", lambda e, kc=kc, j=j, tbank=tbank, ptm=ptm: e.transpose(
                        out=bank_bf(tbank)[:, j * 128:(j + 1) * 128],
                        in_=ptm[:, kc * 128:(kc + 1) * 128], identity=ident))
                out_ap = pt.rearrange("p (c n) -> p c n", n=256)[
                    :, g * 8:g * 8 + ntr, ip * 128:(ip + 1) * 128]
                in_ap = bank_bf(tbank).rearrange("p (c n) -> p c n", n=128)[:, 0:ntr, :]
                if g % 2 == 0:
                    w("vector", "pe", pe_v)
                    if g == 0:
                        w("vector", "pe", state.get("pt_free_pe", 0))
                    ev = do("vector", "dv", lambda e, o=out_ap, i_=in_ap: e.tensor_copy(
                        out=o, in_=i_))
                    state[("tev", tb)] = ev
                else:
                    w("scalar", "pe", pe_v)
                    if g == 1:
                        w("scalar", "pe", state.get("pt_free_pe", 0))
                    ev_ac = do("scalar", "ac", lambda e, o=out_ap, i_=in_ap: e.copy(
                        out=o, in_=i_))
                    state[("tev_ac", tb)] = ev_ac
                if g == ngr - 1:
                    state[("ptm_pe", slot)] = C["pe"]
                    state[("pt_done_dv", ip)] = C["dv"]
                    state[("pt_done_ac", ip)] = C["ac"]

            for g in range(ngr):
                closures.append(lambda g=g: grp(g))
            return closures

        def emit_av_pair(h, i_odd, vh, nk_odd, nk_even, av_dst_col):
            nkc = nk_odd // 128
            nkc_e = nk_even // 128
            ab = state["avb"]
            state["avb"] ^= 1
            avbank = 6 + ab
            w("tensor", "dv", state[("pt_done_dv", 0)])
            w("tensor", "ac", state[("pt_done_ac", 0)])
            w("tensor", "dv", state[("pt_done_dv", 1)])
            w("tensor", "ac", state[("pt_done_ac", 1)])
            w("tensor", "dv", state.get(("avev", ab), 0))
            pe_av = None
            for kc in range(nkc):
                if kc < nkc_e:
                    o0, nn = 0, 256
                else:
                    o0, nn = 128, 128
                pe_av = do("tensor", "pe", lambda e, kc=kc, avbank=avbank, vh=vh, o0=o0, nn=nn: e.matmul(
                    out=bank(avbank)[:, o0:o0 + nn], lhsT=vh[:, kc * 128:(kc + 1) * 128],
                    rhs=pt[:, kc * 256 + o0: kc * 256 + o0 + nn],
                    start=(kc == 0), stop=(kc == nkc - 1)))
            state["pt_free_pe"] = C["pe"]
            w("vector", "pe", pe_av)
            v_c = do("vector", "dv", lambda e, avbank=avbank, col=av_dst_col: e.tensor_copy(
                out=av[:, col - 128:col + 128], in_=bank(avbank)[:, 0:256]))
            state[("avev", ab)] = v_c
            return v_c

        def ffn_sublayer(x_ready, arena_pe, scratch_po):
            ln_dv, _ = layer_norm(x_ready=x_ready, arena_pe=arena_pe,
                                  scratch_po=scratch_po)
            state[("w16free", 0)] = arena_pe
            state[("w16free", 1)] = arena_pe
            state["ffn_down_pe"] = arena_pe
            state["ffn_dn_ev01"] = x_ready
            state["ffn_tt_dv"] = x_ready
            state[("sgfree", 0)] = x_ready
            state[("sgfree", 1)] = x_ready
            nblk = max(1, n_hc // 8)
            for blk in range(nblk):
                for k in range(8):
                    hc = blk * 8 + k
                    par = hc % 2
                    gu_marks = []
                    for (srcw, stg, stsem, dsts, cengs) in (
                        (dWG, ffw0, "qF0", lambda h=hc: wg16s[h % 2], ("gpsimd", "vector")),
                        (dWU, ffw1, "qF1", lambda h=hc: wu16s[h % 2], ("gpsimd", "scalar")),
                    ):
                        for half in range(2):
                            ssem, sval = state.get(("fst", stsem), ("po", 0))
                            w("sync", ssem, sval)
                            if blk == 0 and k == 0 and half == 0:
                                w("sync", "pe", arena_pe)
                                w("sync", "dv", ln_dv)
                            qv = do("sync", stsem, lambda e, st=stg, srcw=srcw, hc=hc, half=half: e.dma_start(
                                out=st, in_=srcw[hc * 128:(hc + 1) * 128,
                                                 half * 1024:(half + 1) * 1024]), 16)
                            engx = cengs[half]
                            mk = cast_to(engx,
                                         dsts()[:, half * 1024:(half + 1) * 1024], stg,
                                         waits=[(stsem, qv),
                                                ("pe", state.get(("w16free", par), 0))])
                            state[("fst", stsem)] = mk
                            gu_marks.append(mk)
                    wd_marks = []
                    for half in range(2):
                        stsem = "qF2" if half == 0 else "qF3"
                        stg = ffw2 if half == 0 else ffw3
                        ssem, sval = state.get(("fst", stsem), ("po", 0))
                        w("sync", ssem, sval)
                        if blk == 0 and k == 0 and half == 0:
                            w("sync", "pe", arena_pe)
                        qv = do("sync", stsem, lambda e, st=stg, hc=hc, half=half: e.dma_start(
                            out=st, in_=dWD[hc * 128:(hc + 1) * 128,
                                            half * 1024:(half + 1) * 1024]), 16)
                        engx = ("vector", "scalar")[(k + half) % 2]
                        wts = [(stsem, qv), ("pe", state.get("ffn_down_pe", 0))]
                        if blk == 0 and k == 0:
                            wts.append(("dv", x_ready))
                        mk = cast_to(engx,
                                     wd16[:, k * 2048 + half * 1024: k * 2048 + (half + 1) * 1024],
                                     stg, waits=wts)
                        state[("fst", stsem)] = mk
                        wd_marks.append(mk)
                    state[("wd_marks", k)] = list(wd_marks)
                    for (s_, v_) in gu_marks:
                        w("tensor", s_, v_)
                    w("tensor", "dv", ln_dv)
                    w("tensor", "ac", state.get("ffn_silu_ac", 0))
                    w("tensor", "dv", state.get("ffn_dn_ev01", 0))
                    for dcc in range(DC):
                        for th in range(2):
                            pe_g = do("tensor", "pe", lambda e, dcc=dcc, th=th, hc=hc: e.matmul(
                                out=bank(th), lhsT=wg16s[hc % 2][:, dcc * 128:(dcc + 1) * 128],
                                rhs=lnT[:, dcc * 1024 + th * 512: dcc * 1024 + (th + 1) * 512],
                                start=(dcc == 0), stop=(dcc == DC - 1)))
                    w("scalar", "pe", pe_g)
                    w("scalar", "dv", state.get(("sgfree", hc % 2), 0))
                    a_si = do("scalar", "ac", lambda e, hc=hc: e.activation(
                        out=(sg0 if hc % 2 == 0 else sg1)[:], in_=bank(0, 2), func=AF.Silu))
                    state["ffn_silu_ac"] = a_si
                    w("tensor", "dv", state.get("ffn_tt_dv", 0))
                    for dcc in range(DC):
                        for th in range(2):
                            pe_u = do("tensor", "pe", lambda e, dcc=dcc, th=th, hc=hc: e.matmul(
                                out=bank(2 + th), lhsT=wu16s[hc % 2][:, dcc * 128:(dcc + 1) * 128],
                                rhs=lnT[:, dcc * 1024 + th * 512: dcc * 1024 + (th + 1) * 512],
                                start=(dcc == 0), stop=(dcc == DC - 1)))
                    state[("w16free", par)] = C["pe"]
                    w("vector", "pe", pe_u)
                    w("vector", "ac", a_si)
                    w("vector", "pe", state.get("ffn_down_pe", 0))
                    v_tt = do("vector", "dv", lambda e, hc=hc, k=k: e.tensor_mul(
                        out=gated[:, k * 1024:(k + 1) * 1024], in0=bank(2, 2),
                        in1=(sg0 if hc % 2 == 0 else sg1)[:]))
                    state["ffn_tt_dv"] = v_tt
                    state[("sgfree", hc % 2)] = v_tt
                    state[("ffn_gated", k)] = v_tt
                for tt in range(n_tiles):
                    pb = 4 * (tt % 2)
                    w("tensor", "dv", state[("ffn_gated", 7)])
                    if tt == 0:
                        for kk in range(8):
                            for (s_, v_) in state[("wd_marks", kk)]:
                                w("tensor", s_, v_)
                    w("tensor", "ac", state.get("ffn_silu_ac", 0))
                    w("tensor", "dv", state.get(("ffn_dn_ev", tt % 2), 0))
                    for hcc in range(8):
                        for j in range(4):
                            pe_d = do("tensor", "pe", lambda e, hcc=hcc, j=j, tt=tt, pb=pb: e.matmul(
                                out=bank(pb + j),
                                lhsT=gated[:, hcc * 1024 + tt * 128: hcc * 1024 + (tt + 1) * 128],
                                rhs=wd16[:, hcc * 2048 + j * 512: hcc * 2048 + (j + 1) * 512],
                                start=(hcc == 0), stop=(hcc == 7)))
                    w("vector", "pe", pe_d)
                    w("vector", "dv", state.get(("x_guard", tt), 0))
                    ev = do("vector", "dv", lambda e, tt=tt, pb=pb: e.tensor_add(
                        out=x_sb[:, tt * D:(tt + 1) * D],
                        in0=x_sb[:, tt * D:(tt + 1) * D], in1=bank(pb, 4)))
                    state[("ffn_dn_ev", tt % 2)] = ev
                    state[("x_guard", tt)] = ev
                    if tt % 2 == 0:
                        state["ffn_dn_ev01"] = ev
                state["ffn_down_pe"] = C["pe"]
            return C["dv"]
        if 1 in sublayers:
            with scope("s1_ln"):
                ln_dv, ln_ac = layer_norm(double_ln=False, affine=False)

            wq_po = stream_w_resident(
                dWQA, gate_sem=("dv" if cfg.get("zero_arenas") else None),
                gate_val=zero_dv if cfg.get("zero_arenas") else 0,
                cast_engs=("gpsimd",))
            q_ac, _ = projection(qa, None, ln_dv, wq_po, n_oc=n_heads, scale_imm=SCALE)
            state["kv_gate_pe"] = C["pe"]

            kv_po = {}
            def stream_head_kv2(h):
                slot = h % 2
                kh = kh0 if slot == 0 else kh1
                vh = vh0 if slot == 0 else vh1
                for part in range(8):
                    is_v = part >= 4
                    src = dVB if is_v else dKT
                    dst = vh if is_v else kh
                    col0 = (part % 4) * 1024
                    stp = part % 2
                    st = wst0 if stp == 0 else wst1
                    ssem, sval = state.get(("stage", stp), ("po", 0))
                    w("sync", ssem, sval)
                    w("sync", "dv", state.get(("kvdone", slot), 0))
                    qsem = "qS%d" % stp
                    qv = do("sync", qsem, lambda e, st=st, src=src, h=h, col0=col0: e.dma_start(
                        out=st, in_=src[h * 128:(h + 1) * 128, col0:col0 + 1024]), 16)
                    w("gpsimd", qsem, qv)
                    w("gpsimd", "pe", state.get("kv_gate_pe", 0))
                    pv = do("gpsimd", "po", lambda e, st=st, dst=dst, col0=col0: e.tensor_copy(
                        out=dst[:, col0:col0 + 1024], in_=st))
                    state[("stage", stp)] = ("po", pv)
                kv_po[h] = C["po"]

            stream_head_kv2(0)
            if n_heads > 1:
                stream_head_kv2(1)

            ctx_h = scope("s1_heads")
            ctx_h.__enter__()
            pend_weave = []
            pend_av = None

            def mk_av_a(h, i, vh):
                def go(h=h, i=i, vh=vh):
                    res = emit_av_pair(h, i, vh, KVLEN, KVLEN,
                                       av_dst_col=h * 1024 + i * 128)
                    if i == n_tiles - 1:
                        state[("kvdone", h % 2)] = res
                        if h + 2 < n_heads:
                            stream_head_kv2(h + 2)
                return go
            for h in range(n_heads):
                slot = h % 2
                kh = kh0 if slot == 0 else kh1
                vh = vh0 if slot == 0 else vh1
                for i in range(n_tiles):
                    u = h * 8 + i
                    ptm = ptm0 if i % 2 == 0 else ptm1
                    q_lhsT = qa[:, h * 1024 + i * 128: h * 1024 + (i + 1) * 128]
                    w("tensor", "ac", q_ac)
                    w("tensor", "po", kv_po[h])
                    weave = pend_weave
                    if pend_av is not None:
                        weave = weave + [pend_av]
                        pend_av = None
                    unit = attn_unit(h, i, u, q_lhsT, kh, KVLEN, ptm, masks=(),
                                     slot=i % 2, weave=weave)
                    pend_weave = make_T_groups(h, i, u, unit, ptm, KVLEN, slot=i % 2)
                    if i % 2 == 1:
                        pend_av = mk_av_a(h, i, vh)
            for f in pend_weave:
                f()
            if pend_av is not None:
                pend_av()
            pend_weave, pend_av = [], None

            ctx_h.__exit__(None, None, None)
            av_done = C["dv"]
            mv = None
            for q in range(4):
                w("vector", "dv", av_done)
                mv = do("vector", "dv", lambda e, q=q: e.tensor_copy(
                    out=av_m[:, q * 4096:(q + 1) * 4096], in_=av[:, q * 4096:(q + 1) * 4096]))
            av_mv = mv
            w("sync", "dv", av_mv)
            wo_po = stream_w_resident(dWOA)

            for tt in range(n_tiles):
                pb = 4 * (tt % 2)
                w("tensor", "dv", state.get(("opev", tt % 2), 0))
                w("tensor", "dv", av_mv)
                wait_ready("tensor", wo_po)
                for hc in range(n_heads):
                    for j in range(4):
                        pe_v = do("tensor", "pe", lambda e, hc=hc, j=j, tt=tt, pb=pb: e.matmul(
                            out=bank(pb, 4)[:, j * 512:(j + 1) * 512],
                            lhsT=av_m[:, hc * 1024 + tt * 128: hc * 1024 + (tt + 1) * 128],
                            rhs=w16_tile(hc)[:, j * 512:(j + 1) * 512],
                            start=(hc == 0), stop=(hc == n_heads - 1)))
                w("vector", "pe", pe_v)
                ev = do("vector", "dv", lambda e, tt=tt, pb=pb: e.tensor_add(
                    out=x_sb[:, tt * D:(tt + 1) * D],
                    in0=x_sb[:, tt * D:(tt + 1) * D], in1=bank(pb, 4)))
                state[("opev", tt % 2)] = ev
                state[("x_guard", tt)] = ev
            state["sub_done_dv"] = C["dv"]
            state["sub_done_pe"] = C["pe"]
            state["sub_done_po"] = C["po"]

        if 2 in sublayers:
            with scope("ffn1"):
                ffn_sublayer(x_ready=state.get("sub_done_dv", 0),
                             arena_pe=state.get("sub_done_pe", 0),
                             scratch_po=state.get("sub_done_po", 0))
            state["sub_done_dv"] = C["dv"]
            state["sub_done_pe"] = C["pe"]
            state["sub_done_po"] = C["po"]

        if 3 in sublayers:
            sd, sp, spo = (state.get("sub_done_dv", 0), state.get("sub_done_pe", 0),
                           state.get("sub_done_po", 0))
            ln3_dv, _ = layer_norm(double_ln=True, affine=True, x_ready=sd,
                                   arena_pe=sp, scratch_po=spo)
            wk_po = stream_w_resident(dWKB, gate_sem="pe",
                                      gate_val=state["scr_free_pe"])
            k_ac, _ = projection(kb_own, bk_c, ln3_dv, wk_po)
            kproj_pe = C["pe"]
            for oc in range(DC):
                w("gpsimd", "ac", k_ac)
                do("gpsimd", "qC", lambda e, oc=oc: e.dma_start(
                    out=dKSEND[oc * 128:(oc + 1) * 128, :],
                    in_=kb_own[:, oc * 1024:(oc + 1) * 1024]), 16)
            ksend_q = C["qC"]
            w("gpsimd", "qC", ksend_q)
            if cfg.get("self_pair"):
                do("gpsimd", "cc0", lambda e: e.dma_start(
                    out=dKRECV[0:D, :], in_=dKSEND), 16)
                do("gpsimd", "cc0", lambda e: e.dma_start(
                    out=dKRECV[D:2 * D, :], in_=dKSEND), 16)
                cc0_done = 32
            else:
                do("gpsimd", "cc0", lambda e: e.collective_compute(
                    "AllGather", ALU.bypass,
                    replica_groups=[[0, 1], [2, 3], [4, 5], [6, 7]],
                    ins=[dKSEND.opt()], outs=[dKRECV.opt()]), 1)
                cc0_done = 1
            w("sync", "pe", kproj_pe)
            wv_po = stream_w_resident(dWVB)
            w("scalar", "qC", ksend_q)
            v_ac, _ = projection(kb_own, bv_c, ln3_dv, wv_po, extra_pe_wait=kproj_pe)
            vstg = scr[:, 0:4096].bitcast(BF16)
            for i in range(TI):
                for g in range(4):
                    tb = state["tb"]
                    state["tb"] ^= 1
                    tbank = 4 + tb
                    w("tensor", "dv", state.get(("tev", tb), 0))
                    w("tensor", "ac", v_ac)
                    for jj in range(4):
                        oc = g * 4 + jj
                        pe_v = do("tensor", "pe", lambda e, oc=oc, jj=jj, i=i, tbank=tbank: e.transpose(
                            out=bank_bf(tbank)[:, jj * 128:(jj + 1) * 128],
                            in_=kb_own[:, oc * 1024 + i * 128: oc * 1024 + (i + 1) * 128],
                            identity=ident))
                    w("vector", "pe", pe_v)
                    if i == 0 and g == 0:
                        w("vector", "po", C["po"])
                        w("vector", "qC", ksend_q)
                    w("vector", "qC", state.get("vstg_q", 0))
                    ev = do("vector", "dv", lambda e, g=g, tbank=tbank: e.tensor_copy(
                        out=vstg[:, g * 512:(g + 1) * 512],
                        in_=bank_bf(tbank)[:, 0:512]))
                    state[("tev", tb)] = ev
                w("gpsimd", "dv", C["dv"])
                qv = do("gpsimd", "qC", lambda e, i=i: e.dma_start(
                    out=dVSEND[i * 128:(i + 1) * 128, :], in_=vstg[:]), 16)
                state["vstg_q"] = qv
            vsend_q = C["qC"]
            w("gpsimd", "qC", vsend_q)
            if cfg.get("self_pair"):
                do("gpsimd", "cc1", lambda e: e.dma_start(
                    out=dVRECV[0:1024, :], in_=dVSEND), 16)
                do("gpsimd", "cc1", lambda e: e.dma_start(
                    out=dVRECV[1024:2048, :], in_=dVSEND), 16)
                cc1_done = 32
            else:
                do("gpsimd", "cc1", lambda e: e.collective_compute(
                    "AllGather", ALU.bypass,
                    replica_groups=[[0, 1], [2, 3], [4, 5], [6, 7]],
                    ins=[dVSEND.opt()], outs=[dVRECV.opt()]), 1)
                cc1_done = 1
            wq_po2 = stream_w_resident(dWQB, gate_sem="qC", gate_val=vsend_q)
            w("scalar", "qC", vsend_q)
            w("scalar", "dv", bqs_dv)
            q_ac2, q_dv2 = projection(qa, bqs_c, ln3_dv, wq_po2, extra_pe_wait=kproj_pe,
                                      scale_imm=SCALE)
            qproj_pe = C["pe"]
            state[("ptm_pe", 0)] = qproj_pe
            state[("ptm_pe", 1)] = qproj_pe
            state["pt_free_pe"] = qproj_pe
            kb_h = [u8_bf(ar4, 0, 2048), u8_bf(ar4, 4096, 2048)]
            vb_h = [u8_bf(ar4, 8192, 2048), u8_bf(ar4, 16384, 2048)]
            kv_q = {}

            def assemble_head(h):
                slot = h % 2
                qsem = "qA%d" % slot
                w("scalar", "cc0", cc0_done)
                w("scalar", "cc1", cc1_done)
                w("scalar", "pe", qproj_pe)
                w("scalar", "dv", state.get(("kvdone_b", slot), 0))
                for j in range(16):
                    r, it = j % 2, j // 2
                    do("scalar", qsem, lambda e, h=h, j=j, r=r, it=it, slot=slot: e.dma_start(
                        out=kb_h[slot][:, j * 128:(j + 1) * 128],
                        in_=dKRECV[r * D + h * 128: r * D + (h + 1) * 128,
                                   it * 128:(it + 1) * 128]), 16)
                for j in range(16):
                    r, it = j % 2, j // 2
                    do("scalar", qsem, lambda e, h=h, j=j, r=r, it=it, slot=slot: e.dma_start(
                        out=vb_h[slot][:, j * 128:(j + 1) * 128],
                        in_=dVRECV[r * 1024 + it * 128: r * 1024 + (it + 1) * 128,
                                   h * 128:(h + 1) * 128]), 16)
                kv_q[h] = C[qsem]

            assemble_head(0)
            if n_heads > 1:
                assemble_head(1)
            ctx_h3 = scope("s3_heads")
            ctx_h3.__enter__()
            pend_weave = []
            pend_av = None

            def mk_av_b(h, i, slot):
                def go(h=h, i=i, slot=slot):
                    res = emit_av_pair(h, i, vb_h[slot], (2 * i + 2) * 128,
                                       (2 * i) * 128,
                                       av_dst_col=h * 1024 + i * 128)
                    if i == n_tiles - 1:
                        state[("kvdone_b", h % 2)] = res
                        if h + 2 < n_heads:
                            assemble_head(h + 2)
                return go
            for h in range(n_heads):
                slot = h % 2
                for i in range(n_tiles):
                    u = h * 8 + i
                    ptm = ptm0 if i % 2 == 0 else ptm1
                    nk_pad = (2 * i + 2) * 128
                    q_lhsT = qa[:, h * 1024 + i * 128: h * 1024 + (i + 1) * 128]
                    w("tensor", "ac", q_ac2)
                    w("tensor", "dv", q_dv2)
                    w("tensor", "qA%d" % slot, kv_q[h])
                    masks = ((2 * i, maskA), (2 * i + 1, maskB))
                    weave = pend_weave
                    if pend_av is not None:
                        weave = weave + [pend_av]
                        pend_av = None
                    unit = attn_unit(h, i, u, q_lhsT, kb_h[slot], nk_pad, ptm,
                                     masks=masks, slot=i % 2, weave=weave)
                    pend_weave = make_T_groups(h, i, u, unit, ptm, nk_pad,
                                               slot=i % 2)
                    if i % 2 == 1:
                        pend_av = mk_av_b(h, i, slot)
            for f in pend_weave:
                f()
            if pend_av is not None:
                pend_av()
            pend_weave, pend_av = [], None
            ctx_h3.__exit__(None, None, None)
            av_done = C["dv"]
            mv = None
            for qq in range(4):
                w("vector", "dv", av_done)
                mv = do("vector", "dv", lambda e, qq=qq: e.tensor_copy(
                    out=av_m[:, qq * 4096:(qq + 1) * 4096],
                    in_=av[:, qq * 4096:(qq + 1) * 4096]))
            av_mv = mv
            w("sync", "dv", av_mv)
            wo_po2 = stream_w_resident(dWOB)
            bo_row2 = u8_bf(ar2, 0, 2048)
            bo32 = u8_f32(ar2, 8192, 2048)
            headloop_pe = C["pe"]
            w("sync", "pe", headloop_pe)
            bo_q = do("sync", "qX", lambda e: e.dma_start(out=bo32[0:1, :], in_=dBO), 16)
            w("gpsimd", "qX", bo_q)
            w("gpsimd", "pe", headloop_pe)
            bo_po = do("gpsimd", "po", lambda e: e.tensor_copy(
                out=bo_row2[0:1, :], in_=bo32[0:1, :]))
            for tt in range(n_tiles):
                pb = 4 * (tt % 2)
                w("tensor", "dv", state.get(("opev", tt % 2), 0))
                w("tensor", "dv", av_mv)
                wait_ready("tensor", wo_po2)
                for hc in range(n_heads):
                    for j in range(4):
                        pe_v = do("tensor", "pe", lambda e, hc=hc, j=j, tt=tt, pb=pb: e.matmul(
                            out=bank(pb + j),
                            lhsT=av_m[:, hc * 1024 + tt * 128: hc * 1024 + (tt + 1) * 128],
                            rhs=w16_tile(hc)[:, j * 512:(j + 1) * 512],
                            start=(hc == 0), stop=False))
                w("tensor", "po", bo_po)
                w("tensor", "qX", C["qX"])
                for j in range(4):
                    pe_v = do("tensor", "pe", lambda e, j=j, pb=pb: e.matmul(
                        out=bank(pb + j), lhsT=ones1[0:1, :],
                        rhs=bo_row2[0:1, j * 512:(j + 1) * 512],
                        start=False, stop=True))
                w("vector", "pe", pe_v)
                w("vector", "dv", state.get(("x_guard", tt), 0))
                ev = do("vector", "dv", lambda e, tt=tt, pb=pb: e.tensor_add(
                    out=x_sb[:, tt * D:(tt + 1) * D],
                    in0=x_sb[:, tt * D:(tt + 1) * D], in1=bank(pb, 4)))
                state[("opev", tt % 2)] = ev
                state[("x_guard", tt)] = ev
            state["sub_done_dv"] = C["dv"]
            state["sub_done_pe"] = C["pe"]
            state["sub_done_po"] = C["po"]

        if 4 in sublayers:
            with scope("ffn2"):
                ffn_sublayer(x_ready=state.get("sub_done_dv", 0),
                             arena_pe=state.get("sub_done_pe", 0),
                             scratch_po=state.get("sub_done_po", 0))
            state["sub_done_dv"] = C["dv"]
            state["sub_done_pe"] = C["pe"]
            state["sub_done_po"] = C["po"]

        xfin = C["dv"]
        for i in range(TI):
            w("sync", "dv", xfin)
            do("sync", "qY", lambda e, i=i: e.dma_start(
                out=dY[i * 128:(i + 1) * 128, :], in_=x_sb[:, i * D:(i + 1) * D]), 16)
        w("sync", "qY", C["qY"])

    def mk(name):
        def f(eng):
            emit(name, eng)
        return f

    block.sync(mk("sync"))
    block.tensor(mk("tensor"))
    block.vector(mk("vector"))
    block.scalar(mk("scalar"))
    block.gpsimd(mk("gpsimd"))

    stack.close()
    return nc


def prep_inputs(inputs, pc_list=None):
    q = np.asarray(inputs["L0_attn_a_query"], dtype=np.float32)
    kvk = np.asarray(inputs["L0_attn_a_kv_k"], dtype=np.float32).reshape(KVLEN, D)
    kvv = np.asarray(inputs["L0_attn_a_kv_v"], dtype=np.float32).reshape(KVLEN, D)
    kvkT = np.ascontiguousarray(kvk.T)
    kvv_b = np.ascontiguousarray(
        kvv.reshape(32, 128, 16, 128).transpose(2, 1, 0, 3).reshape(NH * 128, KVLEN))
    wqaT = np.ascontiguousarray(np.asarray(inputs["L0_attn_a_Wq"], np.float32).T)
    woaT = np.ascontiguousarray(np.asarray(inputs["L0_attn_a_Wo"], np.float32).T)
    wqb = np.ascontiguousarray(np.asarray(inputs["L0_attn_b_Wq"], np.float32))
    wkb = np.ascontiguousarray(np.asarray(inputs["L0_attn_b_Wk"], np.float32))
    wvb = np.ascontiguousarray(np.asarray(inputs["L0_attn_b_Wv"], np.float32))
    wob = np.ascontiguousarray(np.asarray(inputs["L0_attn_b_Wo"], np.float32))
    bq = np.ascontiguousarray(np.asarray(inputs["L0_attn_b_bq"], np.float32).reshape(16, 128).T)
    bk = np.ascontiguousarray(np.asarray(inputs["L0_attn_b_bk"], np.float32).reshape(16, 128).T)
    bv = np.ascontiguousarray(np.asarray(inputs["L0_attn_b_bv"], np.float32).reshape(16, 128).T)
    bo = np.asarray(inputs["L0_attn_b_bo"], np.float32).reshape(1, D)
    lng = np.ascontiguousarray(np.asarray(inputs["L0_attn_b_ln_g"], np.float32).reshape(16, 128).T)
    lnb = np.ascontiguousarray(np.asarray(inputs["L0_attn_b_ln_b"], np.float32).reshape(16, 128).T)
    wg = np.asarray(inputs["L0_ffn_w_gate"], np.float32)
    wu = np.asarray(inputs["L0_ffn_w_up"], np.float32)
    wd = np.asarray(inputs["L0_ffn_w_down"], np.float32)
    wgT_r = np.ascontiguousarray(
        wg.reshape(64, 128, 16, 128).transpose(0, 3, 2, 1).reshape(FH, D))
    wuT_r = np.ascontiguousarray(
        wu.reshape(64, 128, 16, 128).transpose(0, 3, 2, 1).reshape(FH, D))
    wdT_r = np.ascontiguousarray(wd.T)
    ident = np.eye(128, dtype=BF)
    ones1 = np.ones((1, 128), dtype=BF)
    tril = np.where(np.tril(np.ones((128, 128), bool)), 0.0, NEG).astype(np.float32)
    full = np.full((128, 128), NEG, dtype=np.float32)
    zero = np.zeros((128, 128), dtype=np.float32)

    in_maps = []
    for c in range(8):
        b, pc = c // 2, c % 2
        x0 = np.ascontiguousarray(
            q[b].reshape(16, 128, D)[pc::2].reshape(TI * 128, D))
        maskA = tril if pc == 0 else zero
        maskB = full if pc == 0 else tril
        in_maps.append(dict(
            x0=x0, kvkT=kvkT, kvv_b=kvv_b, wqaT=wqaT, woaT=woaT,
            wqb=wqb, wkb=wkb, wvb=wvb, wob=wob,
            bqb=bq, bkb=bk, bvb=bv, bob=bo, lng=lng, lnb=lnb,
            wgT_r=wgT_r, wuT_r=wuT_r, wdT_r=wdT_r,
            maskA=maskA, maskB=maskB, ident=ident, ones1=ones1,
        ))
    return in_maps


def kernel(**inputs):
    cfg = default_cfg()
    cfg["sublayers"] = (1, 2, 3, 4)
    nc = build_program(cfg)
    in_maps = prep_inputs(inputs)
    res = run_bass_kernel_spmd(nc, in_maps, core_ids=list(range(8)))
    out = np.zeros((4, 2048, D), dtype=np.float32)
    for c in range(8):
        b, pc = c // 2, c % 2
        y = res.results[c]["y"].reshape(TI, 128, D)
        out[b].reshape(16, 128, D)[pc::2] = y
    return out


# version 09
# speedup vs baseline: 1.1578x; 1.0645x over previous
import numpy as np
import ml_dtypes
from contextlib import ExitStack

import concourse.bass as bass
import concourse.mybir as mybir
from concourse.bass_utils import run_bass_kernel_spmd

F32 = mybir.dt.float32
BF16 = mybir.dt.bfloat16
AX = mybir.AxisListType.X
ALU = mybir.AluOpType
AF = mybir.ActivationFunctionType
BF = ml_dtypes.bfloat16

D = 2048
TI = 8
DC = 16
NH = 16
DH = 128
KVLEN = 4096
FH = 8192
NHC = 64
SCALE = 1.0 / np.sqrt(128.0)
EPS_LN = 1e-5
NEG = -1.0e30

ENGS = ("sync", "tensor", "vector", "scalar", "gpsimd")


def default_cfg():
    return dict(
        n_heads=NH,
        n_tiles=TI,
        n_hc=NHC,
        sublayers=(1,),
        zero_arenas=False,
    )


class Em:

    def __init__(self, me, eng, sems):
        self.me = me
        self.eng = eng
        self.S = sems
        self.C = {k: 0 for k in sems}

    def w(self, owner, sem, val):
        if val > 0 and self.me == owner:
            self.eng.wait_ge(self.S[sem], val)

    def do(self, owner, sem, f, n=1):
        if sem is not None:
            self.C[sem] += n
        if self.me == owner:
            inst = f(self.eng)
            if sem is not None:
                inst.then_inc(self.S[sem], n)
        return self.C[sem] if sem is not None else None

    def do0(self, owner, f):
        if self.me == owner:
            f(self.eng)


def build_program(cfg):
    nc = bass.Bass("TRN2", target_bir_lowering=False, debug=False)

    dX = nc.dram_tensor("x0", [TI * 128, D], F32, kind="ExternalInput").ap()
    dKT = nc.dram_tensor("kvkT", [D, KVLEN], F32, kind="ExternalInput").ap()
    dVB = nc.dram_tensor("kvv_b", [NH * 128, KVLEN], F32, kind="ExternalInput").ap()
    dWQA = nc.dram_tensor("wqaT", [D, D], F32, kind="ExternalInput").ap()
    dWOA = nc.dram_tensor("woaT", [D, D], F32, kind="ExternalInput").ap()
    dWQB = nc.dram_tensor("wqb", [D, D], F32, kind="ExternalInput").ap()
    dWKB = nc.dram_tensor("wkb", [D, D], F32, kind="ExternalInput").ap()
    dWVB = nc.dram_tensor("wvb", [D, D], F32, kind="ExternalInput").ap()
    dWOB = nc.dram_tensor("wob", [D, D], F32, kind="ExternalInput").ap()
    dBQ = nc.dram_tensor("bqb", [128, 16], F32, kind="ExternalInput").ap()
    dBK = nc.dram_tensor("bkb", [128, 16], F32, kind="ExternalInput").ap()
    dBV = nc.dram_tensor("bvb", [128, 16], F32, kind="ExternalInput").ap()
    dBO = nc.dram_tensor("bob", [1, D], F32, kind="ExternalInput").ap()
    dLNG = nc.dram_tensor("lng", [128, 16], F32, kind="ExternalInput").ap()
    dLNB = nc.dram_tensor("lnb", [128, 16], F32, kind="ExternalInput").ap()
    dWG = nc.dram_tensor("wgT_r", [FH, D], F32, kind="ExternalInput").ap()
    dWU = nc.dram_tensor("wuT_r", [FH, D], F32, kind="ExternalInput").ap()
    dWD = nc.dram_tensor("wdT_r", [FH, D], F32, kind="ExternalInput").ap()
    dMA = nc.dram_tensor("maskA", [128, 128], F32, kind="ExternalInput").ap()
    dMB = nc.dram_tensor("maskB", [128, 128], F32, kind="ExternalInput").ap()
    dID = nc.dram_tensor("ident", [128, 128], BF16, kind="ExternalInput").ap()
    dONES = nc.dram_tensor("ones1", [1, 128], BF16, kind="ExternalInput").ap()
    dY = nc.dram_tensor("y", [TI * 128, D], F32, kind="ExternalOutput").ap()

    dKSEND = nc.dram_tensor("ksend", [D, 1024], BF16).ap()
    dKRECV = nc.dram_tensor("krecv", [2 * D, 1024], BF16).ap()
    dVSEND = nc.dram_tensor("vsend", [1024, D], BF16).ap()
    dVRECV = nc.dram_tensor("vrecv", [2 * 1024, D], BF16).ap()
    dWG16 = nc.dram_tensor("wg16", [FH, D], BF16).ap()
    dWU16 = nc.dram_tensor("wu16", [FH, D], BF16).ap()
    dWD16 = nc.dram_tensor("wd16", [FH, D], BF16).ap()

    stack = ExitStack()
    e = stack.enter_context

    x_sb = e(nc.sbuf_tensor("x_sb", [128, TI * D], F32))
    ar1 = e(nc.sbuf_tensor("ar1", [128, 32768], mybir.dt.uint8))
    ar2 = e(nc.sbuf_tensor("ar2", [128, 32768], mybir.dt.uint8))
    ar3 = e(nc.sbuf_tensor("ar3", [128, 32768], mybir.dt.uint8))
    ar4 = e(nc.sbuf_tensor("ar4", [128, 32768], mybir.dt.uint8))
    scr = e(nc.sbuf_tensor("scr", [128, 10240], mybir.dt.uint8))
    cst = e(nc.sbuf_tensor("cst", [128, 2048], mybir.dt.uint8))
    sta = e(nc.sbuf_tensor("sta", [128, 2048], mybir.dt.uint8))

    def u8_bf(t, off, n):
        return t[:, off:off + 2 * n].bitcast(BF16)

    def u8_f32(t, off, n):
        return t[:, off:off + 4 * n].bitcast(F32)

    lnT = u8_bf(ar1, 0, 16384)
    pt = u8_bf(ar1, 0, 8192)
    ptm0 = u8_bf(ar1, 16384, 4096)
    ptm1 = u8_bf(ar1, 24576, 4096)
    av_m = u8_bf(ar1, 0, 16384)
    bo_row = u8_bf(ar1, 32768 - 4096, 2048)

    qa = u8_bf(ar2, 0, 16384)
    gated = u8_bf(ar2, 0, 8192)
    sg0 = u8_bf(ar2, 16384, 1024)
    sg1 = u8_bf(ar2, 18432, 1024)
    ffw0 = u8_f32(ar2, 20480, 1024)
    ffw1 = u8_f32(ar2, 24576, 1024)
    kb_own = u8_bf(ar2, 0, 16384)

    av = u8_bf(ar3, 0, 16384)
    wd16 = u8_bf(ar3, 0, 16384)
    w16a = u8_bf(ar3, 0, 16384)

    w16b = u8_bf(ar4, 0, 16384)
    kh0 = u8_bf(ar4, 0, 4096)
    kh1 = u8_bf(ar4, 8192, 4096)
    vh0 = u8_bf(ar4, 16384, 4096)
    vh1 = u8_bf(ar4, 24576, 4096)
    wg16s = [u8_bf(ar4, 0, 2048), u8_bf(ar4, 4096, 2048)]
    wu16s = [u8_bf(ar4, 8192, 2048), u8_bf(ar4, 12288, 2048)]
    ffw2 = u8_f32(ar4, 16384, 1024)
    ffw3 = u8_f32(ar4, 20480, 1024)
    wo16s = [u8_bf(ar4, 0, 2048), u8_bf(ar4, 4096, 2048)]

    wst0 = u8_f32(scr, 0, 1024)
    wst1 = u8_f32(scr, 4096, 1024)
    ln_tm0 = u8_bf(ar2, 0, 2048)
    ln_tm1 = u8_bf(ar2, 4096, 2048)
    sqscr = u8_f32(scr, 8192, 512)

    maskA = u8_f32(cst, 0, 128)
    maskB = u8_f32(cst, 512, 128)
    ident = u8_bf(cst, 1024, 128)
    ones1 = u8_bf(cst, 1280, 128)
    bq_c = u8_f32(cst, 1536, 16)
    bk_c = u8_f32(cst, 1600, 16)
    bv_c = u8_f32(cst, 1664, 16)
    lng_c = u8_f32(cst, 1728, 16)
    lnb_c = u8_f32(cst, 1792, 16)
    eps_c = u8_f32(cst, 1856, 1)
    bqs_c = u8_f32(sta, 1680, 16)

    ssum = u8_f32(sta, 0, 16)
    sqac = u8_f32(sta, 64, 16)
    mu_s = u8_f32(sta, 128, 16)
    var_s = u8_f32(sta, 192, 16)
    rstd_s = u8_f32(sta, 256, 16)
    t0_s = u8_f32(sta, 320, 16)
    t1_s = u8_f32(sta, 384, 16)
    mbuf = u8_f32(sta, 448, 8)
    bias_b = u8_f32(sta, 480, 8)
    acc_b = u8_f32(sta, 512, 8)
    fv_b = u8_f32(sta, 544, 8)
    accf_b = u8_f32(sta, 576, 8)
    tot_b = u8_f32(sta, 608, 2)
    rinv = u8_f32(sta, 640, 128)
    rawm = u8_f32(sta, 1152, 4)
    avst0 = u8_bf(sta, 1168, 128)
    avst1 = u8_bf(sta, 1424, 128)

    ps = e(nc.psum_tensor("ps", [128, 4096], F32))

    def bank(k, n=1):
        return ps[:, 512 * k:512 * (k + n)]

    def bank_bf(k, n=1):
        return ps[:, 512 * k:512 * (k + n)].bitcast(BF16)

    sems = {}
    for name in ("qX", "qS0", "qS1", "qA0", "qA1", "qF0", "qF1", "qF2", "qF3",
                 "qW", "qC", "qK", "qY", "cc0", "cc1", "pe", "dv", "ac", "po"):
        sems[name] = e(nc.semaphore(name))

    block = e(nc.Block())

    n_heads = cfg["n_heads"]
    n_tiles = cfg["n_tiles"]
    n_hc = cfg["n_hc"]
    sublayers = cfg["sublayers"]

    def emit(me, eng):
        E = Em(me, eng, sems)
        w, do, do0, C = E.w, E.do, E.do0, E.C
        def scope(name):
            return nc.named_scope(name)

        xw = {}
        for i in range(2):
            do("sync", "qX", lambda e, i=i: e.dma_start(
                out=x_sb[:, i * D:(i + 1) * D], in_=dX[i * 128:(i + 1) * 128, :]), 16)
        xw[0] = C["qX"]
        for i in range(2, TI):
            do("sync", "qY", lambda e, i=i: e.dma_start(
                out=x_sb[:, i * D:(i + 1) * D], in_=dX[i * 128:(i + 1) * 128, :]), 16)
        xw[1] = C["qY"]
        for (dst, src) in ((maskA, dMA), (maskB, dMB), (ident, dID),
                           (bq_c, dBQ), (bk_c, dBK), (bv_c, dBV),
                           (lng_c, dLNG), (lnb_c, dLNB)):
            do("sync", "qK", lambda e, d=dst, s=src: e.dma_start(out=d, in_=s), 16)
        do("sync", "qK", lambda e: e.dma_start(out=ones1[0:1, :], in_=dONES), 16)
        consts_ok = C["qK"]
        x_loaded = [("qX", xw[0]) if i < 2 else ("qY", xw[1]) for i in range(TI)]
        if cfg.get("zero_arenas"):
            for t in (ar1, ar2, ar3, ar4):
                do("vector", "dv", lambda e, t=t: e.memset(t[:, :], 0))
        zero_dv = C["dv"]
        eps_dv = do("vector", "dv", lambda e: e.memset(eps_c, EPS_LN))
        w("vector", "qK", consts_ok)
        bqs_dv = do("vector", "dv", lambda e: e.tensor_scalar_mul(
            out=bqs_c[:], in0=bq_c[:], scalar1=SCALE))

        state = dict(tb=0, avb=0)

        def layer_norm(double_ln=False, affine=False, x_ready=0, arena_pe=0,
                       scratch_po=0):
            for i in range(n_tiles):
                sl = i % 2
                ln_tm = ln_tm0 if sl == 0 else ln_tm1
                xi = x_sb[:, i * D:(i + 1) * D]
                w("vector", x_loaded[i][0], x_loaded[i][1])
                w("vector", "dv", x_ready)
                v_sum = do("vector", "dv", lambda e, xi=xi, i=i: e.reduce_sum(
                    out=ssum[:, i:i + 1], in_=xi, axis=AX))
                w("scalar", x_loaded[i][0], x_loaded[i][1])
                w("scalar", "dv", x_ready)
                a_sq0 = do("scalar", "ac", lambda e, xi=xi, i=i: e.activation(
                    out=sqscr[:], in_=xi[:, 0:512], func=AF.Square,
                    accum_out=sqac[:, 2 * (i % 8):2 * (i % 8) + 1]))
                w("scalar", "ac", a_sq0)
                a_sq1 = do("scalar", "ac", lambda e, xi=xi, i=i: e.activation(
                    out=sqscr[:], in_=xi[:, 512:1024], func=AF.Square,
                    accum_out=t0_s[:, i:i + 1]))
                w("scalar", "ac", a_sq1)
                a_sq2 = do("scalar", "ac", lambda e, xi=xi, i=i: e.activation(
                    out=sqscr[:], in_=xi[:, 1024:1536], func=AF.Square,
                    accum_out=t1_s[:, i:i + 1]))
                w("scalar", "ac", a_sq2)
                a_sq3 = do("scalar", "ac", lambda e, xi=xi, i=i: e.activation(
                    out=sqscr[:], in_=xi[:, 1536:2048], func=AF.Square,
                    accum_out=sqac[:, 2 * (i % 8) + 1:2 * (i % 8) + 2]))

                def stats_chain(i, v_sum, a_sq3, pfx_mu, pfx_var, pfx_rstd):
                    w("vector", "dv", v_sum)
                    v_mu = do("vector", "dv", lambda e, i=i: e.tensor_scalar_mul(
                        out=mu_s[:, i:i + 1], in0=ssum[:, i:i + 1], scalar1=1.0 / D))
                    w("vector", "ac", a_sq3)
                    v_s2 = do("vector", "dv", lambda e, i=i: e.tensor_add(
                        out=sqac[:, 2 * (i % 8):2 * (i % 8) + 1],
                        in0=sqac[:, 2 * (i % 8):2 * (i % 8) + 1],
                        in1=t0_s[:, i:i + 1]))
                    w("vector", "dv", v_s2)
                    v_s2b = do("vector", "dv", lambda e, i=i: e.tensor_add(
                        out=t1_s[:, i:i + 1], in0=t1_s[:, i:i + 1],
                        in1=sqac[:, 2 * (i % 8) + 1:2 * (i % 8) + 2]))
                    w("vector", "dv", v_s2b)
                    v_s2c = do("vector", "dv", lambda e, i=i: e.tensor_add(
                        out=sqac[:, 2 * (i % 8):2 * (i % 8) + 1],
                        in0=sqac[:, 2 * (i % 8):2 * (i % 8) + 1],
                        in1=t1_s[:, i:i + 1]))
                    w("vector", "dv", max(v_mu, v_s2c))
                    v_mu2 = do("vector", "dv", lambda e, i=i: e.tensor_mul(
                        out=t0_s[:, i:i + 1], in0=mu_s[:, i:i + 1], in1=mu_s[:, i:i + 1]))
                    w("vector", "dv", v_mu2)
                    v_var = do("vector", "dv", lambda e, i=i: e.scalar_tensor_tensor(
                        out=var_s[:, i:i + 1], in0=sqac[:, 2 * (i % 8):2 * (i % 8) + 1],
                        scalar=1.0 / D, in1=t0_s[:, i:i + 1],
                        op0=ALU.mult, op1=ALU.subtract))
                    w("scalar", "dv", v_var)
                    w("scalar", "dv", eps_dv)
                    a_ln = do("scalar", "ac", lambda e, i=i: e.activation(
                        out=t1_s[:, i:i + 1], in_=var_s[:, i:i + 1], func=AF.Ln,
                        bias=eps_c[:, 0:1]))
                    w("scalar", "ac", a_ln)
                    a_rs = do("scalar", "ac", lambda e, i=i: e.activation(
                        out=rstd_s[:, i:i + 1], in_=t1_s[:, i:i + 1], func=AF.Exp,
                        scale=-0.5))
                    return a_rs

                a_rs = stats_chain(i, v_sum, a_sq3, mu_s, var_s, rstd_s)

                if not double_ln:
                    w("vector", "ac", a_rs)
                    if i < 2:
                        w("vector", "pe", arena_pe)
                    v_ln = do("vector", "dv", lambda e, xi=xi, i=i, lt=ln_tm: e.tensor_scalar(
                        out=lt[:], in0=xi, scalar1=mu_s[:, i:i + 1],
                        scalar2=rstd_s[:, i:i + 1], op0=ALU.subtract, op1=ALU.mult))
                else:
                    y32 = ar2[:, 8192:16384].bitcast(F32)
                    w("vector", "ac", a_rs)
                    if i < 2:
                        w("vector", "pe", arena_pe)
                    v_y = do("vector", "dv", lambda e, xi=xi, i=i, y=y32: e.tensor_scalar(
                        out=y[:], in0=xi, scalar1=mu_s[:, i:i + 1],
                        scalar2=rstd_s[:, i:i + 1], op0=ALU.subtract, op1=ALU.mult))
                    w("vector", "dv", v_y)
                    v_sum2 = do("vector", "dv", lambda e, y=y32, i=i: e.reduce_sum(
                        out=ssum[:, 8 + (i % 8):9 + (i % 8)], in_=y, axis=AX))
                    w("scalar", "dv", v_y)
                    a_q0 = do("scalar", "ac", lambda e, y=y32, i=i: e.activation(
                        out=sqscr[:], in_=y[:, 0:512], func=AF.Square,
                        accum_out=sqac[:, 2 * (i % 8):2 * (i % 8) + 1]))
                    w("scalar", "ac", a_q0)
                    a_q1 = do("scalar", "ac", lambda e, y=y32, i=i: e.activation(
                        out=sqscr[:], in_=y[:, 512:1024], func=AF.Square,
                        accum_out=t0_s[:, i:i + 1]))
                    w("scalar", "ac", a_q1)
                    a_q2 = do("scalar", "ac", lambda e, y=y32, i=i: e.activation(
                        out=sqscr[:], in_=y[:, 1024:1536], func=AF.Square,
                        accum_out=t1_s[:, i:i + 1]))
                    w("scalar", "ac", a_q2)
                    a_q3 = do("scalar", "ac", lambda e, y=y32, i=i: e.activation(
                        out=sqscr[:], in_=y[:, 1536:2048], func=AF.Square,
                        accum_out=sqac[:, 2 * (i % 8) + 1:2 * (i % 8) + 2]))
                    w("vector", "dv", v_sum2)
                    v_mu = do("vector", "dv", lambda e, i=i: e.tensor_scalar_mul(
                        out=mu_s[:, 8 + (i % 8):9 + (i % 8)],
                        in0=ssum[:, 8 + (i % 8):9 + (i % 8)], scalar1=1.0 / D))
                    w("vector", "ac", a_q3)
                    v_a = do("vector", "dv", lambda e, i=i: e.tensor_add(
                        out=sqac[:, 2 * (i % 8):2 * (i % 8) + 1],
                        in0=sqac[:, 2 * (i % 8):2 * (i % 8) + 1], in1=t0_s[:, i:i + 1]))
                    w("vector", "dv", v_a)
                    v_b = do("vector", "dv", lambda e, i=i: e.tensor_add(
                        out=t1_s[:, i:i + 1], in0=t1_s[:, i:i + 1],
                        in1=sqac[:, 2 * (i % 8) + 1:2 * (i % 8) + 2]))
                    w("vector", "dv", v_b)
                    v_c = do("vector", "dv", lambda e, i=i: e.tensor_add(
                        out=sqac[:, 2 * (i % 8):2 * (i % 8) + 1],
                        in0=sqac[:, 2 * (i % 8):2 * (i % 8) + 1], in1=t1_s[:, i:i + 1]))
                    w("vector", "dv", max(v_mu, v_c))
                    v_mu2 = do("vector", "dv", lambda e, i=i: e.tensor_mul(
                        out=t0_s[:, i:i + 1], in0=mu_s[:, 8 + (i % 8):9 + (i % 8)],
                        in1=mu_s[:, 8 + (i % 8):9 + (i % 8)]))
                    w("vector", "dv", v_mu2)
                    v_var = do("vector", "dv", lambda e, i=i: e.scalar_tensor_tensor(
                        out=var_s[:, 8 + (i % 8):9 + (i % 8)],
                        in0=sqac[:, 2 * (i % 8):2 * (i % 8) + 1],
                        scalar=1.0 / D, in1=t0_s[:, i:i + 1],
                        op0=ALU.mult, op1=ALU.subtract))
                    w("scalar", "dv", v_var)
                    w("scalar", "dv", eps_dv)
                    a_l2 = do("scalar", "ac", lambda e, i=i: e.activation(
                        out=t1_s[:, i:i + 1], in_=var_s[:, 8 + (i % 8):9 + (i % 8)],
                        func=AF.Ln, bias=eps_c[:, 0:1]))
                    w("scalar", "ac", a_l2)
                    a_rs2 = do("scalar", "ac", lambda e, i=i: e.activation(
                        out=rstd_s[:, 8 + (i % 8):9 + (i % 8)], in_=t1_s[:, i:i + 1],
                        func=AF.Exp, scale=-0.5))
                    w("vector", "ac", a_rs2)
                    v_ln = do("vector", "dv", lambda e, y=y32, i=i, lt=ln_tm: e.tensor_scalar(
                        out=lt[:], in0=y, scalar1=mu_s[:, 8 + (i % 8):9 + (i % 8)],
                        scalar2=rstd_s[:, 8 + (i % 8):9 + (i % 8)],
                        op0=ALU.subtract, op1=ALU.mult))

                w("tensor", "qK", consts_ok)
                w("tensor", "dv", zero_dv)
                w("tensor", "pe", arena_pe)
                for g in range(4):
                    tb = state["tb"]
                    state["tb"] ^= 1
                    tbank = 4 + tb
                    w("tensor", "dv", state.get(("tev", tb), 0))
                    w("tensor", "dv", v_ln)
                    for j in range(4):
                        dc = g * 4 + j
                        pe_v = do("tensor", "pe", lambda e, lt=ln_tm, dc=dc, j=j, tbank=tbank: e.transpose(
                            out=bank_bf(tbank)[:, j * 128:(j + 1) * 128],
                            in_=lt[:, dc * 128:(dc + 1) * 128], identity=ident))
                    w("vector", "pe", pe_v)
                    if affine:
                        for j in range(4):
                            dc = g * 4 + j
                            ev = do("vector", "dv", lambda e, dc=dc, j=j, i=i, tbank=tbank: e.tensor_scalar(
                                out=lnT[:, dc * 1024 + i * 128: dc * 1024 + (i + 1) * 128],
                                in0=bank_bf(tbank)[:, j * 128:(j + 1) * 128],
                                scalar1=lng_c[:, dc:dc + 1], scalar2=lnb_c[:, dc:dc + 1],
                                op0=ALU.mult, op1=ALU.add))
                    else:
                        ev = do("vector", "dv", lambda e, g=g, i=i, tbank=tbank: e.tensor_copy(
                            out=lnT.rearrange("p (c n) -> p c n", n=1024)[
                                :, g * 4:(g + 1) * 4, i * 128:(i + 1) * 128],
                            in_=bank_bf(tbank).rearrange("p (c n) -> p c n", n=128)[:, 0:4, :]))
                    state[("tev", tb)] = ev
            state["scr_free_pe"] = C["pe"]
            return C["dv"], C["ac"]

        def w16_tile(dcc):
            if dcc < 8:
                return w16a[:, dcc * 2048:(dcc + 1) * 2048]
            return w16b[:, (dcc - 8) * 2048:(dcc - 8 + 1) * 2048]

        CAST_SEM = {"gpsimd": "po", "vector": "dv", "scalar": "ac"}

        def cast_to(engx, dst, srcap, waits=()):
            for (s, v) in waits:
                w(engx, s, v)
            sem = CAST_SEM[engx]
            if engx == "scalar":
                v = do(engx, sem, lambda e, dst=dst, srcap=srcap: e.copy(
                    out=dst, in_=srcap))
            else:
                v = do(engx, sem, lambda e, dst=dst, srcap=srcap: e.tensor_copy(
                    out=dst, in_=srcap))
            return (sem, v)

        def stream_w_resident(src, gate_sem=None, gate_val=0, dst_tile=None,
                              cast_engs=("gpsimd", "scalar", "vector")):
            if dst_tile is None:
                dst_tile = w16_tile
            last = None
            for dcc in range(DC):
                for hhalf in range(2):
                    k = dcc * 2 + hhalf
                    p = k % 2
                    st = wst0 if p == 0 else wst1
                    if gate_sem is not None and dcc == 0:
                        w("sync", gate_sem, gate_val)
                    ssem, sval = state.get(("stage", p), ("po", 0))
                    w("sync", ssem, sval)
                    qsem = "qS%d" % p
                    qv = do("sync", qsem, lambda e, st=st, dcc=dcc, hh=hhalf, src=src: e.dma_start(
                        out=st, in_=src[dcc * 128:(dcc + 1) * 128, hh * 1024:(hh + 1) * 1024]), 16)
                    engx = cast_engs[k % len(cast_engs)]
                    last = cast_to(engx, dst_tile(dcc)[:, hhalf * 1024:(hhalf + 1) * 1024],
                                   st, waits=[(qsem, qv)])
                    state[("stage", p)] = last
            return dict(po=C["po"], dv=C["dv"], ac=C["ac"])

        def wait_ready(owner, r):
            if isinstance(r, dict):
                w(owner, "po", r["po"])
                w(owner, "dv", r["dv"])
                w(owner, "ac", r["ac"])
            else:
                w(owner, "po", r)

        def projection(dst, bias_col, lnT_ready_dv, w_po, extra_pe_wait=0, n_oc=DC,
                       scale_imm=None):
            for oc in range(n_oc):
                for th in range(2):
                    pbk = (2 * oc + th) % 4
                    psem, pval = state.get(("pjev", pbk), ("ac", 0))
                    w("tensor", psem, pval)
                    w("tensor", "dv", lnT_ready_dv)
                    wait_ready("tensor", w_po)
                    w("tensor", "pe", extra_pe_wait)
                    for dcc in range(DC):
                        pe_v = do("tensor", "pe", lambda e, oc=oc, th=th, dcc=dcc, pbk=pbk: e.matmul(
                            out=bank(pbk),
                            lhsT=w16_tile(dcc)[:, oc * 128:(oc + 1) * 128],
                            rhs=lnT[:, dcc * 1024 + th * 512: dcc * 1024 + (th + 1) * 512],
                            start=(dcc == 0), stop=(dcc == DC - 1)))
                    if scale_imm is not None and bias_col is not None:
                        w("vector", "pe", pe_v)
                        evd = do("vector", "dv", lambda e, oc=oc, th=th, pbk=pbk, dst=dst, b=bias_col, s=scale_imm: e.tensor_scalar(
                            out=dst[:, oc * 1024 + th * 512: oc * 1024 + (th + 1) * 512],
                            in0=bank(pbk), scalar1=s, scalar2=b[:, oc:oc + 1],
                            op0=ALU.mult, op1=ALU.add))
                        state[("pjev", pbk)] = ("dv", evd)
                        continue
                    w("scalar", "pe", pe_v)
                    if scale_imm is not None:
                        ev = do("scalar", "ac", lambda e, oc=oc, th=th, pbk=pbk, dst=dst, s=scale_imm: e.activation(
                            out=dst[:, oc * 1024 + th * 512: oc * 1024 + (th + 1) * 512],
                            in_=bank(pbk), func=AF.Copy, scale=s, bias=0.0))
                    elif bias_col is None:
                        ev = do("scalar", "ac", lambda e, oc=oc, th=th, pbk=pbk, dst=dst: e.copy(
                            out=dst[:, oc * 1024 + th * 512: oc * 1024 + (th + 1) * 512],
                            in_=bank(pbk)))
                    else:
                        ev = do("scalar", "ac", lambda e, oc=oc, th=th, pbk=pbk, dst=dst, b=bias_col: e.add(
                            out=dst[:, oc * 1024 + th * 512: oc * 1024 + (th + 1) * 512],
                            in_=bank(pbk),
                            add=b[:, oc:oc + 1]))
                    state[("pjev", pbk)] = ("ac", ev)
            return (C["ac"], C["dv"])

        def attn_unit(h, i, u, q_lhsT, kT, nk, ptm, masks, slot, weave=()):
            CN = 1024
            nch = (nk + CN - 1) // CN
            exp_acs = []
            first_dve = [True]
            wv = list(weave)

            def unit_dve_gate():
                if first_dve[0]:
                    w("vector", "dv", state.get("unit_dv", 0))
                    first_dve[0] = False

            v_m = None
            for c in range(nch):
                k0 = c * CN
                k1 = min(nk, k0 + CN)
                par = c % 2
                sb = 2 * par
                w("tensor", "ac", state.get(("sxp", par), 0))
                n0 = k1 - k0
                pe_v = None
                for j in range(0, n0, 512):
                    nn = min(512, n0 - j)
                    pe_v = do("tensor", "pe", lambda e, sb=sb, j=j, nn=nn, k0=k0, q=q_lhsT, kT=kT: e.matmul(
                        out=bank(sb, 2)[:, j:j + nn], lhsT=q,
                        rhs=kT[:, k0 + j:k0 + j + nn], start=True, stop=True))
                if wv:
                    wv.pop(0)()
                mv = None
                for (bi, m_ap) in masks:
                    b0 = bi * 128
                    if b0 >= k0 and b0 < k1:
                        w("vector", "pe", pe_v)
                        unit_dve_gate()
                        mv = do("vector", "dv", lambda e, sb=sb, b0=b0, k0=k0, m=m_ap: e.tensor_add(
                            out=bank(sb, 2)[:, b0 - k0:b0 - k0 + 128],
                            in0=bank(sb, 2)[:, b0 - k0:b0 - k0 + 128], in1=m))
                w("vector", "pe", pe_v)
                unit_dve_gate()
                if mv is not None:
                    w("vector", "dv", mv)
                if c == 0:
                    v_m = do("vector", "dv", lambda e, sb=sb, n0=n0: e.reduce_max(
                        out=mbuf[:, 0:1], in_=bank(sb, 2)[:, 0:n0], axis=AX, negate=True))
                else:
                    w("vector", "dv", v_m)
                    v_r = do("vector", "dv", lambda e, sb=sb, n0=n0: e.reduce_max(
                        out=rawm[:, 0:1], in_=bank(sb, 2)[:, 0:n0], axis=AX, negate=True))
                    w("vector", "dv", v_r)
                    v_m = do("vector", "dv", lambda e, c=c: e.tensor_scalar(
                        out=mbuf[:, c:c + 1], in0=rawm[:, 0:1],
                        scalar1=mbuf[:, c - 1:c], scalar2=None, op0=ALU.min))
                w("scalar", "dv", v_m)
                if c == 0:
                    w("scalar", "pe", state.get(("ptm_pe", slot), 0))
                a_e = do("scalar", "ac", lambda e, sb=sb, c=c, k0=k0, n0=n0, ptm=ptm: e.activation(
                    out=ptm[:, k0:k0 + n0], in_=bank(sb, 2)[:, 0:n0], func=AF.Exp,
                    bias=mbuf[:, c:c + 1], accum_out=acc_b[:, c:c + 1]))
                state[("sxp", par)] = a_e
                exp_acs.append(a_e)
            for f in wv:
                f()
            last = nch - 1
            w("scalar", "dv", v_m)
            w("scalar", "ac", exp_acs[-1])
            a_f = do("scalar", "ac", lambda e, nch=nch, last=last: e.activation(
                out=fv_b[:, 0:nch], in_=mbuf[:, 0:nch], func=AF.Exp, scale=-1.0,
                bias=mbuf[:, last:last + 1]))
            w("vector", "ac", a_f)
            v_t = do("vector", "dv", lambda e, nch=nch: e.scalar_tensor_tensor(
                out=accf_b[:, 0:nch], in0=acc_b[:, 0:nch], scalar=1.0,
                in1=fv_b[:, 0:nch], op0=ALU.mult, op1=ALU.mult,
                accum_out=tot_b[:, 0:1]))
            w("vector", "dv", v_t)
            v_ri = do("vector", "dv", lambda e: e.reciprocal(
                out=tot_b[:, 1:2], in_=tot_b[:, 0:1]))
            w("vector", "dv", v_ri)
            corr_dv = C["dv"]
            for c in range(nch):
                k0 = c * CN
                k1 = min(nk, k0 + CN)
                corr_dv = do("vector", "dv", lambda e, c=c, k0=k0, k1=k1, ptm=ptm: e.tensor_scalar(
                    out=ptm[:, k0:k1], in0=ptm[:, k0:k1], scalar1=fv_b[:, c:c + 1],
                    scalar2=tot_b[:, 1:2], op0=ALU.mult, op1=ALU.mult))
            state["unit_dv"] = corr_dv
            return dict(exp_ac=exp_acs[-1], corr_dv=corr_dv)

        def make_T_groups(h, i, u, unit, ptm, nk, slot):
            nkc = nk // 128
            ip = i % 2
            ngr = (nkc + 7) // 8
            closures = []

            def grp(g):
                tb = state["tb"]
                state["tb"] ^= 1
                tbank = 4 + tb
                w("tensor", "dv", state.get(("tev", tb), 0))
                w("tensor", "ac", state.get(("tev_ac", tb), 0))
                w("tensor", "ac", unit["exp_ac"])
                w("tensor", "dv", unit["corr_dv"])
                ntr = min(8, nkc - g * 8)
                pe_v = None
                for j in range(ntr):
                    kc = g * 8 + j
                    pe_v = do("tensor", "pe", lambda e, kc=kc, j=j, tbank=tbank, ptm=ptm: e.transpose(
                        out=bank_bf(tbank)[:, j * 128:(j + 1) * 128],
                        in_=ptm[:, kc * 128:(kc + 1) * 128], identity=ident))
                out_ap = pt.rearrange("p (c n) -> p c n", n=256)[
                    :, g * 8:g * 8 + ntr, ip * 128:(ip + 1) * 128]
                in_ap = bank_bf(tbank).rearrange("p (c n) -> p c n", n=128)[:, 0:ntr, :]
                if g % 2 == 0:
                    w("vector", "pe", pe_v)
                    if g == 0:
                        w("vector", "pe", state.get("pt_free_pe", 0))
                    ev = do("vector", "dv", lambda e, o=out_ap, i_=in_ap: e.tensor_copy(
                        out=o, in_=i_))
                    state[("tev", tb)] = ev
                else:
                    w("scalar", "pe", pe_v)
                    if g == 1:
                        w("scalar", "pe", state.get("pt_free_pe", 0))
                    ev_ac = do("scalar", "ac", lambda e, o=out_ap, i_=in_ap: e.copy(
                        out=o, in_=i_))
                    state[("tev_ac", tb)] = ev_ac
                if g == ngr - 1:
                    state[("ptm_pe", slot)] = C["pe"]
                    state[("pt_done_dv", ip)] = C["dv"]
                    state[("pt_done_ac", ip)] = C["ac"]

            for g in range(ngr):
                closures.append(lambda g=g: grp(g))
            return closures

        def emit_av_pair(h, i_odd, vh, nk_odd, nk_even, av_dst_col):
            nkc = nk_odd // 128
            nkc_e = nk_even // 128
            ab = state["avb"]
            state["avb"] ^= 1
            avbank = 6 + ab
            w("tensor", "dv", state[("pt_done_dv", 0)])
            w("tensor", "ac", state[("pt_done_ac", 0)])
            w("tensor", "dv", state[("pt_done_dv", 1)])
            w("tensor", "ac", state[("pt_done_ac", 1)])
            w("tensor", "dv", state.get(("avev", ab), 0))
            pe_av = None
            for kc in range(nkc):
                if kc < nkc_e:
                    o0, nn = 0, 256
                else:
                    o0, nn = 128, 128
                pe_av = do("tensor", "pe", lambda e, kc=kc, avbank=avbank, vh=vh, o0=o0, nn=nn: e.matmul(
                    out=bank(avbank)[:, o0:o0 + nn], lhsT=vh[:, kc * 128:(kc + 1) * 128],
                    rhs=pt[:, kc * 256 + o0: kc * 256 + o0 + nn],
                    start=(kc == 0), stop=(kc == nkc - 1)))
            state["pt_free_pe"] = C["pe"]
            w("vector", "pe", pe_av)
            v_c = do("vector", "dv", lambda e, avbank=avbank, col=av_dst_col: e.tensor_copy(
                out=av[:, col - 128:col + 128], in_=bank(avbank)[:, 0:256]))
            state[("avev", ab)] = v_c
            return v_c

        def ffn_sublayer(x_ready, arena_pe, scratch_po):
            ln_dv, _ = layer_norm(x_ready=x_ready, arena_pe=arena_pe,
                                  scratch_po=scratch_po)
            state[("w16free", 0)] = arena_pe
            state[("w16free", 1)] = arena_pe
            state["ffn_down_pe"] = arena_pe
            state["ffn_dn_ev01"] = x_ready
            state["ffn_tt_dv"] = x_ready
            state[("sgfree", 0)] = x_ready
            state[("sgfree", 1)] = x_ready
            nblk = max(1, n_hc // 8)
            for blk in range(nblk):
                for k in range(8):
                    hc = blk * 8 + k
                    par = hc % 2
                    gu_marks = []
                    for (srcw, stg, stsem, dsts, cengs) in (
                        (dWG, ffw0, "qF0", lambda h=hc: wg16s[h % 2], ("gpsimd", "vector")),
                        (dWU, ffw1, "qF1", lambda h=hc: wu16s[h % 2], ("gpsimd", "scalar")),
                    ):
                        for half in range(2):
                            ssem, sval = state.get(("fst", stsem), ("po", 0))
                            w("sync", ssem, sval)
                            if blk == 0 and k == 0 and half == 0:
                                w("sync", "pe", arena_pe)
                                w("sync", "dv", ln_dv)
                            qv = do("sync", stsem, lambda e, st=stg, srcw=srcw, hc=hc, half=half: e.dma_start(
                                out=st, in_=srcw[hc * 128:(hc + 1) * 128,
                                                 half * 1024:(half + 1) * 1024]), 16)
                            engx = cengs[half]
                            mk = cast_to(engx,
                                         dsts()[:, half * 1024:(half + 1) * 1024], stg,
                                         waits=[(stsem, qv),
                                                ("pe", state.get(("w16free", par), 0))])
                            state[("fst", stsem)] = mk
                            gu_marks.append(mk)
                    wd_marks = []
                    for half in range(2):
                        stsem = "qF2" if half == 0 else "qF3"
                        stg = ffw2 if half == 0 else ffw3
                        ssem, sval = state.get(("fst", stsem), ("po", 0))
                        w("sync", ssem, sval)
                        if blk == 0 and k == 0 and half == 0:
                            w("sync", "pe", arena_pe)
                        qv = do("sync", stsem, lambda e, st=stg, hc=hc, half=half: e.dma_start(
                            out=st, in_=dWD[hc * 128:(hc + 1) * 128,
                                            half * 1024:(half + 1) * 1024]), 16)
                        engx = ("vector", "scalar")[(k + half) % 2]
                        wts = [(stsem, qv), ("pe", state.get("ffn_down_pe", 0))]
                        if blk == 0 and k == 0:
                            wts.append(("dv", x_ready))
                        mk = cast_to(engx,
                                     wd16[:, k * 2048 + half * 1024: k * 2048 + (half + 1) * 1024],
                                     stg, waits=wts)
                        state[("fst", stsem)] = mk
                        wd_marks.append(mk)
                    state[("wd_marks", k)] = list(wd_marks)
                    for (s_, v_) in gu_marks:
                        w("tensor", s_, v_)
                    w("tensor", "dv", ln_dv)
                    w("tensor", "ac", state.get("ffn_silu_ac", 0))
                    w("tensor", "dv", state.get("ffn_dn_ev01", 0))
                    for dcc in range(DC):
                        for th in range(2):
                            pe_g = do("tensor", "pe", lambda e, dcc=dcc, th=th, hc=hc: e.matmul(
                                out=bank(th), lhsT=wg16s[hc % 2][:, dcc * 128:(dcc + 1) * 128],
                                rhs=lnT[:, dcc * 1024 + th * 512: dcc * 1024 + (th + 1) * 512],
                                start=(dcc == 0), stop=(dcc == DC - 1)))
                    w("scalar", "pe", pe_g)
                    w("scalar", "dv", state.get(("sgfree", hc % 2), 0))
                    a_si = do("scalar", "ac", lambda e, hc=hc: e.activation(
                        out=(sg0 if hc % 2 == 0 else sg1)[:], in_=bank(0, 2), func=AF.Silu))
                    state["ffn_silu_ac"] = a_si
                    w("tensor", "dv", state.get("ffn_tt_dv", 0))
                    for dcc in range(DC):
                        for th in range(2):
                            pe_u = do("tensor", "pe", lambda e, dcc=dcc, th=th, hc=hc: e.matmul(
                                out=bank(2 + th), lhsT=wu16s[hc % 2][:, dcc * 128:(dcc + 1) * 128],
                                rhs=lnT[:, dcc * 1024 + th * 512: dcc * 1024 + (th + 1) * 512],
                                start=(dcc == 0), stop=(dcc == DC - 1)))
                    state[("w16free", par)] = C["pe"]
                    w("vector", "pe", pe_u)
                    w("vector", "ac", a_si)
                    w("vector", "pe", state.get("ffn_down_pe", 0))
                    v_tt = do("vector", "dv", lambda e, hc=hc, k=k: e.tensor_mul(
                        out=gated[:, k * 1024:(k + 1) * 1024], in0=bank(2, 2),
                        in1=(sg0 if hc % 2 == 0 else sg1)[:]))
                    state["ffn_tt_dv"] = v_tt
                    state[("sgfree", hc % 2)] = v_tt
                    state[("ffn_gated", k)] = v_tt
                for tt in range(n_tiles):
                    pb = 4 * (tt % 2)
                    w("tensor", "dv", state[("ffn_gated", 7)])
                    if tt == 0:
                        for kk in range(8):
                            for (s_, v_) in state[("wd_marks", kk)]:
                                w("tensor", s_, v_)
                    w("tensor", "ac", state.get("ffn_silu_ac", 0))
                    w("tensor", "dv", state.get(("ffn_dn_ev", tt % 2), 0))
                    for hcc in range(8):
                        for j in range(4):
                            pe_d = do("tensor", "pe", lambda e, hcc=hcc, j=j, tt=tt, pb=pb: e.matmul(
                                out=bank(pb + j),
                                lhsT=gated[:, hcc * 1024 + tt * 128: hcc * 1024 + (tt + 1) * 128],
                                rhs=wd16[:, hcc * 2048 + j * 512: hcc * 2048 + (j + 1) * 512],
                                start=(hcc == 0), stop=(hcc == 7)))
                    w("vector", "pe", pe_d)
                    w("vector", "dv", state.get(("x_guard", tt), 0))
                    ev = do("vector", "dv", lambda e, tt=tt, pb=pb: e.tensor_add(
                        out=x_sb[:, tt * D:(tt + 1) * D],
                        in0=x_sb[:, tt * D:(tt + 1) * D], in1=bank(pb, 4)))
                    state[("ffn_dn_ev", tt % 2)] = ev
                    state[("x_guard", tt)] = ev
                    if tt % 2 == 0:
                        state["ffn_dn_ev01"] = ev
                state["ffn_down_pe"] = C["pe"]
            return C["dv"]
        if 1 in sublayers:
            with scope("s1_ln"):
                ln_dv, ln_ac = layer_norm(double_ln=False, affine=False)

            wq_po = stream_w_resident(
                dWQA, gate_sem=("dv" if cfg.get("zero_arenas") else None),
                gate_val=zero_dv if cfg.get("zero_arenas") else 0,
                cast_engs=("gpsimd",))
            q_ac, _ = projection(qa, None, ln_dv, wq_po, n_oc=n_heads, scale_imm=SCALE)
            state["kv_gate_pe"] = C["pe"]

            kv_po = {}
            def stream_head_kv2(h):
                slot = h % 2
                kh = kh0 if slot == 0 else kh1
                vh = vh0 if slot == 0 else vh1
                for part in range(8):
                    is_v = part >= 4
                    src = dVB if is_v else dKT
                    dst = vh if is_v else kh
                    col0 = (part % 4) * 1024
                    stp = part % 2
                    st = wst0 if stp == 0 else wst1
                    ssem, sval = state.get(("stage", stp), ("po", 0))
                    w("sync", ssem, sval)
                    w("sync", "dv", state.get(("kvdone", slot), 0))
                    qsem = "qS%d" % stp
                    qv = do("sync", qsem, lambda e, st=st, src=src, h=h, col0=col0: e.dma_start(
                        out=st, in_=src[h * 128:(h + 1) * 128, col0:col0 + 1024]), 16)
                    w("gpsimd", qsem, qv)
                    w("gpsimd", "pe", state.get("kv_gate_pe", 0))
                    pv = do("gpsimd", "po", lambda e, st=st, dst=dst, col0=col0: e.tensor_copy(
                        out=dst[:, col0:col0 + 1024], in_=st))
                    state[("stage", stp)] = ("po", pv)
                kv_po[h] = C["po"]

            stream_head_kv2(0)
            if n_heads > 1:
                stream_head_kv2(1)

            ctx_h = scope("s1_heads")
            ctx_h.__enter__()
            pend_weave = []
            pend_av = None

            def mk_av_a(h, i, vh):
                def go(h=h, i=i, vh=vh):
                    res = emit_av_pair(h, i, vh, KVLEN, KVLEN,
                                       av_dst_col=h * 1024 + i * 128)
                    if i == n_tiles - 1:
                        state[("kvdone", h % 2)] = res
                        if h + 2 < n_heads:
                            stream_head_kv2(h + 2)
                return go
            for h in range(n_heads):
                slot = h % 2
                kh = kh0 if slot == 0 else kh1
                vh = vh0 if slot == 0 else vh1
                for i in range(n_tiles):
                    u = h * 8 + i
                    ptm = ptm0 if i % 2 == 0 else ptm1
                    q_lhsT = qa[:, h * 1024 + i * 128: h * 1024 + (i + 1) * 128]
                    w("tensor", "ac", q_ac)
                    w("tensor", "po", kv_po[h])
                    weave = pend_weave
                    if pend_av is not None:
                        weave = weave + [pend_av]
                        pend_av = None
                    unit = attn_unit(h, i, u, q_lhsT, kh, KVLEN, ptm, masks=(),
                                     slot=i % 2, weave=weave)
                    pend_weave = make_T_groups(h, i, u, unit, ptm, KVLEN, slot=i % 2)
                    if i % 2 == 1:
                        pend_av = mk_av_a(h, i, vh)
            for f in pend_weave:
                f()
            if pend_av is not None:
                pend_av()
            pend_weave, pend_av = [], None

            ctx_h.__exit__(None, None, None)
            av_done = C["dv"]
            mv = None
            for q in range(4):
                w("vector", "dv", av_done)
                mv = do("vector", "dv", lambda e, q=q: e.tensor_copy(
                    out=av_m[:, q * 4096:(q + 1) * 4096], in_=av[:, q * 4096:(q + 1) * 4096]))
            av_mv = mv
            w("sync", "dv", av_mv)
            wo_po = stream_w_resident(dWOA, cast_engs=("gpsimd", "vector", "scalar"))

            for tt in range(n_tiles):
                pb = 4 * (tt % 2)
                w("tensor", "dv", state.get(("opev", tt % 2), 0))
                w("tensor", "dv", av_mv)
                wait_ready("tensor", wo_po)
                for hc in range(n_heads):
                    for j in range(4):
                        pe_v = do("tensor", "pe", lambda e, hc=hc, j=j, tt=tt, pb=pb: e.matmul(
                            out=bank(pb, 4)[:, j * 512:(j + 1) * 512],
                            lhsT=av_m[:, hc * 1024 + tt * 128: hc * 1024 + (tt + 1) * 128],
                            rhs=w16_tile(hc)[:, j * 512:(j + 1) * 512],
                            start=(hc == 0), stop=(hc == n_heads - 1)))
                w("vector", "pe", pe_v)
                ev = do("vector", "dv", lambda e, tt=tt, pb=pb: e.tensor_add(
                    out=x_sb[:, tt * D:(tt + 1) * D],
                    in0=x_sb[:, tt * D:(tt + 1) * D], in1=bank(pb, 4)))
                state[("opev", tt % 2)] = ev
                state[("x_guard", tt)] = ev
            state["sub_done_dv"] = C["dv"]
            state["sub_done_pe"] = C["pe"]
            state["sub_done_po"] = C["po"]

        if 2 in sublayers:
            with scope("ffn1"):
                ffn_sublayer(x_ready=state.get("sub_done_dv", 0),
                             arena_pe=state.get("sub_done_pe", 0),
                             scratch_po=state.get("sub_done_po", 0))
            state["sub_done_dv"] = C["dv"]
            state["sub_done_pe"] = C["pe"]
            state["sub_done_po"] = C["po"]

        if 3 in sublayers:
            sd, sp, spo = (state.get("sub_done_dv", 0), state.get("sub_done_pe", 0),
                           state.get("sub_done_po", 0))
            ln3_dv, _ = layer_norm(double_ln=True, affine=True, x_ready=sd,
                                   arena_pe=sp, scratch_po=spo)
            wk_po = stream_w_resident(dWKB, gate_sem="pe",
                                      gate_val=state["scr_free_pe"])
            k_ac, _ = projection(kb_own, bk_c, ln3_dv, wk_po)
            kproj_pe = C["pe"]
            for oc in range(DC):
                w("gpsimd", "ac", k_ac)
                do("gpsimd", "qC", lambda e, oc=oc: e.dma_start(
                    out=dKSEND[oc * 128:(oc + 1) * 128, :],
                    in_=kb_own[:, oc * 1024:(oc + 1) * 1024]), 16)
            ksend_q = C["qC"]
            w("gpsimd", "qC", ksend_q)
            if cfg.get("self_pair"):
                do("gpsimd", "cc0", lambda e: e.dma_start(
                    out=dKRECV[0:D, :], in_=dKSEND), 16)
                do("gpsimd", "cc0", lambda e: e.dma_start(
                    out=dKRECV[D:2 * D, :], in_=dKSEND), 16)
                cc0_done = 32
            else:
                do("gpsimd", "cc0", lambda e: e.collective_compute(
                    "AllGather", ALU.bypass,
                    replica_groups=[[0, 1], [2, 3], [4, 5], [6, 7]],
                    ins=[dKSEND.opt()], outs=[dKRECV.opt()]), 1)
                cc0_done = 1
            w("sync", "pe", kproj_pe)
            wv_po = stream_w_resident(dWVB, cast_engs=("gpsimd", "vector", "scalar"))
            w("scalar", "qC", ksend_q)
            v_ac, _ = projection(kb_own, bv_c, ln3_dv, wv_po, extra_pe_wait=kproj_pe)
            vstg = scr[:, 0:4096].bitcast(BF16)
            for i in range(TI):
                for g in range(4):
                    tb = state["tb"]
                    state["tb"] ^= 1
                    tbank = 4 + tb
                    w("tensor", "dv", state.get(("tev", tb), 0))
                    w("tensor", "ac", v_ac)
                    for jj in range(4):
                        oc = g * 4 + jj
                        pe_v = do("tensor", "pe", lambda e, oc=oc, jj=jj, i=i, tbank=tbank: e.transpose(
                            out=bank_bf(tbank)[:, jj * 128:(jj + 1) * 128],
                            in_=kb_own[:, oc * 1024 + i * 128: oc * 1024 + (i + 1) * 128],
                            identity=ident))
                    w("vector", "pe", pe_v)
                    if i == 0 and g == 0:
                        w("vector", "po", C["po"])
                        w("vector", "qC", ksend_q)
                    w("vector", "qC", state.get("vstg_q", 0))
                    ev = do("vector", "dv", lambda e, g=g, tbank=tbank: e.tensor_copy(
                        out=vstg[:, g * 512:(g + 1) * 512],
                        in_=bank_bf(tbank)[:, 0:512]))
                    state[("tev", tb)] = ev
                w("gpsimd", "dv", C["dv"])
                qv = do("gpsimd", "qC", lambda e, i=i: e.dma_start(
                    out=dVSEND[i * 128:(i + 1) * 128, :], in_=vstg[:]), 16)
                state["vstg_q"] = qv
            vsend_q = C["qC"]
            w("gpsimd", "qC", vsend_q)
            if cfg.get("self_pair"):
                do("gpsimd", "cc1", lambda e: e.dma_start(
                    out=dVRECV[0:1024, :], in_=dVSEND), 16)
                do("gpsimd", "cc1", lambda e: e.dma_start(
                    out=dVRECV[1024:2048, :], in_=dVSEND), 16)
                cc1_done = 32
            else:
                do("gpsimd", "cc1", lambda e: e.collective_compute(
                    "AllGather", ALU.bypass,
                    replica_groups=[[0, 1], [2, 3], [4, 5], [6, 7]],
                    ins=[dVSEND.opt()], outs=[dVRECV.opt()]), 1)
                cc1_done = 1
            wq_po2 = stream_w_resident(dWQB, gate_sem="qC", gate_val=vsend_q,
                                       cast_engs=("gpsimd", "vector", "scalar"))
            w("scalar", "qC", vsend_q)
            w("scalar", "dv", bqs_dv)
            q_ac2, q_dv2 = projection(qa, bqs_c, ln3_dv, wq_po2, extra_pe_wait=kproj_pe,
                                      scale_imm=SCALE)
            qproj_pe = C["pe"]
            state[("ptm_pe", 0)] = qproj_pe
            state[("ptm_pe", 1)] = qproj_pe
            state["pt_free_pe"] = qproj_pe
            kb_h = [u8_bf(ar4, 0, 2048), u8_bf(ar4, 4096, 2048)]
            vb_h = [u8_bf(ar4, 8192, 2048), u8_bf(ar4, 16384, 2048)]
            kv_q = {}

            def assemble_head(h):
                slot = h % 2
                qsem = "qA%d" % slot
                w("sync", "cc0", cc0_done)
                w("sync", "cc1", cc1_done)
                w("sync", "pe", qproj_pe)
                w("sync", "dv", state.get(("kvdone_b", slot), 0))
                for j in range(16):
                    r, it = j % 2, j // 2
                    do("sync", qsem, lambda e, h=h, j=j, r=r, it=it, slot=slot: e.dma_start(
                        out=kb_h[slot][:, j * 128:(j + 1) * 128],
                        in_=dKRECV[r * D + h * 128: r * D + (h + 1) * 128,
                                   it * 128:(it + 1) * 128]), 16)
                for j in range(16):
                    r, it = j % 2, j // 2
                    do("sync", qsem, lambda e, h=h, j=j, r=r, it=it, slot=slot: e.dma_start(
                        out=vb_h[slot][:, j * 128:(j + 1) * 128],
                        in_=dVRECV[r * 1024 + it * 128: r * 1024 + (it + 1) * 128,
                                   h * 128:(h + 1) * 128]), 16)
                kv_q[h] = C[qsem]

            assemble_head(0)
            if n_heads > 1:
                assemble_head(1)
            ctx_h3 = scope("s3_heads")
            ctx_h3.__enter__()
            pend_weave = []
            pend_av = None

            def mk_av_b(h, i, slot):
                def go(h=h, i=i, slot=slot):
                    res = emit_av_pair(h, i, vb_h[slot], (2 * i + 2) * 128,
                                       (2 * i) * 128,
                                       av_dst_col=h * 1024 + i * 128)
                    if i == n_tiles - 1:
                        state[("kvdone_b", h % 2)] = res
                        if h + 2 < n_heads:
                            assemble_head(h + 2)
                return go
            for h in range(n_heads):
                slot = h % 2
                for i in range(n_tiles):
                    u = h * 8 + i
                    ptm = ptm0 if i % 2 == 0 else ptm1
                    nk_pad = (2 * i + 2) * 128
                    q_lhsT = qa[:, h * 1024 + i * 128: h * 1024 + (i + 1) * 128]
                    w("tensor", "ac", q_ac2)
                    w("tensor", "dv", q_dv2)
                    w("tensor", "qA%d" % slot, kv_q[h])
                    masks = ((2 * i, maskA), (2 * i + 1, maskB))
                    weave = pend_weave
                    if pend_av is not None:
                        weave = weave + [pend_av]
                        pend_av = None
                    unit = attn_unit(h, i, u, q_lhsT, kb_h[slot], nk_pad, ptm,
                                     masks=masks, slot=i % 2, weave=weave)
                    pend_weave = make_T_groups(h, i, u, unit, ptm, nk_pad,
                                               slot=i % 2)
                    if i % 2 == 1:
                        pend_av = mk_av_b(h, i, slot)
            for f in pend_weave:
                f()
            if pend_av is not None:
                pend_av()
            pend_weave, pend_av = [], None
            ctx_h3.__exit__(None, None, None)
            av_done = C["dv"]
            mv = None
            for qq in range(4):
                w("vector", "dv", av_done)
                mv = do("vector", "dv", lambda e, qq=qq: e.tensor_copy(
                    out=av_m[:, qq * 4096:(qq + 1) * 4096],
                    in_=av[:, qq * 4096:(qq + 1) * 4096]))
            av_mv = mv
            w("sync", "dv", av_mv)
            wo_po2 = stream_w_resident(dWOB, cast_engs=("gpsimd", "vector", "scalar"))
            bo_row2 = u8_bf(ar2, 0, 2048)
            bo32 = u8_f32(ar2, 8192, 2048)
            headloop_pe = C["pe"]
            w("sync", "pe", headloop_pe)
            bo_q = do("sync", "qX", lambda e: e.dma_start(out=bo32[0:1, :], in_=dBO), 16)
            w("gpsimd", "qX", bo_q)
            w("gpsimd", "pe", headloop_pe)
            bo_po = do("gpsimd", "po", lambda e: e.tensor_copy(
                out=bo_row2[0:1, :], in_=bo32[0:1, :]))
            for tt in range(n_tiles):
                pb = 4 * (tt % 2)
                w("tensor", "dv", state.get(("opev", tt % 2), 0))
                w("tensor", "dv", av_mv)
                wait_ready("tensor", wo_po2)
                for hc in range(n_heads):
                    for j in range(4):
                        pe_v = do("tensor", "pe", lambda e, hc=hc, j=j, tt=tt, pb=pb: e.matmul(
                            out=bank(pb + j),
                            lhsT=av_m[:, hc * 1024 + tt * 128: hc * 1024 + (tt + 1) * 128],
                            rhs=w16_tile(hc)[:, j * 512:(j + 1) * 512],
                            start=(hc == 0), stop=False))
                w("tensor", "po", bo_po)
                w("tensor", "qX", C["qX"])
                for j in range(4):
                    pe_v = do("tensor", "pe", lambda e, j=j, pb=pb: e.matmul(
                        out=bank(pb + j), lhsT=ones1[0:1, :],
                        rhs=bo_row2[0:1, j * 512:(j + 1) * 512],
                        start=False, stop=True))
                w("vector", "pe", pe_v)
                w("vector", "dv", state.get(("x_guard", tt), 0))
                ev = do("vector", "dv", lambda e, tt=tt, pb=pb: e.tensor_add(
                    out=x_sb[:, tt * D:(tt + 1) * D],
                    in0=x_sb[:, tt * D:(tt + 1) * D], in1=bank(pb, 4)))
                state[("opev", tt % 2)] = ev
                state[("x_guard", tt)] = ev
            state["sub_done_dv"] = C["dv"]
            state["sub_done_pe"] = C["pe"]
            state["sub_done_po"] = C["po"]

        if 4 in sublayers:
            with scope("ffn2"):
                ffn_sublayer(x_ready=state.get("sub_done_dv", 0),
                             arena_pe=state.get("sub_done_pe", 0),
                             scratch_po=state.get("sub_done_po", 0))
            state["sub_done_dv"] = C["dv"]
            state["sub_done_pe"] = C["pe"]
            state["sub_done_po"] = C["po"]

        xfin = C["dv"]
        for i in range(TI):
            w("sync", "dv", xfin)
            do("sync", "qY", lambda e, i=i: e.dma_start(
                out=dY[i * 128:(i + 1) * 128, :], in_=x_sb[:, i * D:(i + 1) * D]), 16)
        w("sync", "qY", C["qY"])

    def mk(name):
        def f(eng):
            emit(name, eng)
        return f

    block.sync(mk("sync"))
    block.tensor(mk("tensor"))
    block.vector(mk("vector"))
    block.scalar(mk("scalar"))
    block.gpsimd(mk("gpsimd"))

    stack.close()
    return nc


def prep_inputs(inputs, pc_list=None):
    q = np.asarray(inputs["L0_attn_a_query"], dtype=np.float32)
    kvk = np.asarray(inputs["L0_attn_a_kv_k"], dtype=np.float32).reshape(KVLEN, D)
    kvv = np.asarray(inputs["L0_attn_a_kv_v"], dtype=np.float32).reshape(KVLEN, D)
    kvkT = np.ascontiguousarray(kvk.T)
    kvv_b = np.ascontiguousarray(
        kvv.reshape(32, 128, 16, 128).transpose(2, 1, 0, 3).reshape(NH * 128, KVLEN))
    wqaT = np.ascontiguousarray(np.asarray(inputs["L0_attn_a_Wq"], np.float32).T)
    woaT = np.ascontiguousarray(np.asarray(inputs["L0_attn_a_Wo"], np.float32).T)
    wqb = np.ascontiguousarray(np.asarray(inputs["L0_attn_b_Wq"], np.float32))
    wkb = np.ascontiguousarray(np.asarray(inputs["L0_attn_b_Wk"], np.float32))
    wvb = np.ascontiguousarray(np.asarray(inputs["L0_attn_b_Wv"], np.float32))
    wob = np.ascontiguousarray(np.asarray(inputs["L0_attn_b_Wo"], np.float32))
    bq = np.ascontiguousarray(np.asarray(inputs["L0_attn_b_bq"], np.float32).reshape(16, 128).T)
    bk = np.ascontiguousarray(np.asarray(inputs["L0_attn_b_bk"], np.float32).reshape(16, 128).T)
    bv = np.ascontiguousarray(np.asarray(inputs["L0_attn_b_bv"], np.float32).reshape(16, 128).T)
    bo = np.asarray(inputs["L0_attn_b_bo"], np.float32).reshape(1, D)
    lng = np.ascontiguousarray(np.asarray(inputs["L0_attn_b_ln_g"], np.float32).reshape(16, 128).T)
    lnb = np.ascontiguousarray(np.asarray(inputs["L0_attn_b_ln_b"], np.float32).reshape(16, 128).T)
    wg = np.asarray(inputs["L0_ffn_w_gate"], np.float32)
    wu = np.asarray(inputs["L0_ffn_w_up"], np.float32)
    wd = np.asarray(inputs["L0_ffn_w_down"], np.float32)
    wgT_r = np.ascontiguousarray(
        wg.reshape(64, 128, 16, 128).transpose(0, 3, 2, 1).reshape(FH, D))
    wuT_r = np.ascontiguousarray(
        wu.reshape(64, 128, 16, 128).transpose(0, 3, 2, 1).reshape(FH, D))
    wdT_r = np.ascontiguousarray(wd.T)
    ident = np.eye(128, dtype=BF)
    ones1 = np.ones((1, 128), dtype=BF)
    tril = np.where(np.tril(np.ones((128, 128), bool)), 0.0, NEG).astype(np.float32)
    full = np.full((128, 128), NEG, dtype=np.float32)
    zero = np.zeros((128, 128), dtype=np.float32)

    in_maps = []
    for c in range(8):
        b, pc = c // 2, c % 2
        x0 = np.ascontiguousarray(
            q[b].reshape(16, 128, D)[pc::2].reshape(TI * 128, D))
        maskA = tril if pc == 0 else zero
        maskB = full if pc == 0 else tril
        in_maps.append(dict(
            x0=x0, kvkT=kvkT, kvv_b=kvv_b, wqaT=wqaT, woaT=woaT,
            wqb=wqb, wkb=wkb, wvb=wvb, wob=wob,
            bqb=bq, bkb=bk, bvb=bv, bob=bo, lng=lng, lnb=lnb,
            wgT_r=wgT_r, wuT_r=wuT_r, wdT_r=wdT_r,
            maskA=maskA, maskB=maskB, ident=ident, ones1=ones1,
        ))
    return in_maps


def kernel(**inputs):
    cfg = default_cfg()
    cfg["sublayers"] = (1, 2, 3, 4)
    nc = build_program(cfg)
    in_maps = prep_inputs(inputs)
    res = run_bass_kernel_spmd(nc, in_maps, core_ids=list(range(8)))
    out = np.zeros((4, 2048, D), dtype=np.float32)
    for c in range(8):
        b, pc = c // 2, c % 2
        y = res.results[c]["y"].reshape(TI, 128, D)
        out[b].reshape(16, 128, D)[pc::2] = y
    return out


# version 12
# speedup vs baseline: 1.1809x; 1.0199x over previous
import numpy as np
import ml_dtypes
from contextlib import ExitStack

import concourse.bass as bass
import concourse.mybir as mybir
from concourse.bass_utils import run_bass_kernel_spmd

F32 = mybir.dt.float32
BF16 = mybir.dt.bfloat16
AX = mybir.AxisListType.X
ALU = mybir.AluOpType
AF = mybir.ActivationFunctionType
BF = ml_dtypes.bfloat16

D = 2048
TI = 8
DC = 16
NH = 16
DH = 128
KVLEN = 4096
FH = 8192
NHC = 64
SCALE = 1.0 / np.sqrt(128.0)
EPS_LN = 1e-5
NEG = -1.0e30

ENGS = ("sync", "tensor", "vector", "scalar", "gpsimd")


def default_cfg():
    return dict(
        n_heads=NH,
        n_tiles=TI,
        n_hc=NHC,
        sublayers=(1,),
        zero_arenas=False,
    )


class Em:

    def __init__(self, me, eng, sems):
        self.me = me
        self.eng = eng
        self.S = sems
        self.C = {k: 0 for k in sems}

    def w(self, owner, sem, val):
        if val > 0 and self.me == owner:
            self.eng.wait_ge(self.S[sem], val)

    def do(self, owner, sem, f, n=1):
        if sem is not None:
            self.C[sem] += n
        if self.me == owner:
            inst = f(self.eng)
            if sem is not None:
                inst.then_inc(self.S[sem], n)
        return self.C[sem] if sem is not None else None

    def do0(self, owner, f):
        if self.me == owner:
            f(self.eng)


def build_program(cfg):
    nc = bass.Bass("TRN2", target_bir_lowering=False, debug=False)

    dX = nc.dram_tensor("x0", [TI * 128, D], F32, kind="ExternalInput").ap()
    dKT = nc.dram_tensor("kvkT", [D, KVLEN], F32, kind="ExternalInput").ap()
    dVB = nc.dram_tensor("kvv_b", [NH * 128, KVLEN], F32, kind="ExternalInput").ap()
    dWQA = nc.dram_tensor("wqaT", [D, D], F32, kind="ExternalInput").ap()
    dWOA = nc.dram_tensor("woaT", [D, D], F32, kind="ExternalInput").ap()
    dWQB = nc.dram_tensor("wqb", [D, D], F32, kind="ExternalInput").ap()
    dWKB = nc.dram_tensor("wkb", [D, D], F32, kind="ExternalInput").ap()
    dWVB = nc.dram_tensor("wvb", [D, D], F32, kind="ExternalInput").ap()
    dWOB = nc.dram_tensor("wob", [D, D], F32, kind="ExternalInput").ap()
    dBQ = nc.dram_tensor("bqb", [128, 16], F32, kind="ExternalInput").ap()
    dBK = nc.dram_tensor("bkb", [128, 16], F32, kind="ExternalInput").ap()
    dBV = nc.dram_tensor("bvb", [128, 16], F32, kind="ExternalInput").ap()
    dBO = nc.dram_tensor("bob", [1, D], F32, kind="ExternalInput").ap()
    dLNG = nc.dram_tensor("lng", [128, 16], F32, kind="ExternalInput").ap()
    dLNB = nc.dram_tensor("lnb", [128, 16], F32, kind="ExternalInput").ap()
    dWG = nc.dram_tensor("wgT_r", [FH, D], F32, kind="ExternalInput").ap()
    dWU = nc.dram_tensor("wuT_r", [FH, D], F32, kind="ExternalInput").ap()
    dWD = nc.dram_tensor("wdT_r", [FH, D], F32, kind="ExternalInput").ap()
    dMA = nc.dram_tensor("maskA", [128, 128], F32, kind="ExternalInput").ap()
    dMB = nc.dram_tensor("maskB", [128, 128], F32, kind="ExternalInput").ap()
    dID = nc.dram_tensor("ident", [128, 128], BF16, kind="ExternalInput").ap()
    dONES = nc.dram_tensor("ones1", [1, 128], BF16, kind="ExternalInput").ap()
    dY = nc.dram_tensor("y", [TI * 128, D], F32, kind="ExternalOutput").ap()

    dKSEND = nc.dram_tensor("ksend", [D, 1024], BF16).ap()
    dKRECV = nc.dram_tensor("krecv", [2 * D, 1024], BF16).ap()
    dVSEND = nc.dram_tensor("vsend", [1024, D], BF16).ap()
    dVRECV = nc.dram_tensor("vrecv", [2 * 1024, D], BF16).ap()
    dWG16 = nc.dram_tensor("wg16", [FH, D], BF16).ap()
    dWU16 = nc.dram_tensor("wu16", [FH, D], BF16).ap()
    dWD16 = nc.dram_tensor("wd16", [FH, D], BF16).ap()

    stack = ExitStack()
    e = stack.enter_context

    x_sb = e(nc.sbuf_tensor("x_sb", [128, TI * D], F32))
    ar1 = e(nc.sbuf_tensor("ar1", [128, 32768], mybir.dt.uint8))
    ar2 = e(nc.sbuf_tensor("ar2", [128, 32768], mybir.dt.uint8))
    ar3 = e(nc.sbuf_tensor("ar3", [128, 32768], mybir.dt.uint8))
    ar4 = e(nc.sbuf_tensor("ar4", [128, 32768], mybir.dt.uint8))
    scr = e(nc.sbuf_tensor("scr", [128, 10240], mybir.dt.uint8))
    cst = e(nc.sbuf_tensor("cst", [128, 2048], mybir.dt.uint8))
    sta = e(nc.sbuf_tensor("sta", [128, 2048], mybir.dt.uint8))

    def u8_bf(t, off, n):
        return t[:, off:off + 2 * n].bitcast(BF16)

    def u8_f32(t, off, n):
        return t[:, off:off + 4 * n].bitcast(F32)

    lnT = u8_bf(ar1, 0, 16384)
    pt = u8_bf(ar1, 0, 8192)
    ptm0 = u8_bf(ar1, 16384, 4096)
    ptm1 = u8_bf(ar1, 24576, 4096)
    av_m = u8_bf(ar1, 0, 16384)
    bo_row = u8_bf(ar1, 32768 - 4096, 2048)

    qa = u8_bf(ar2, 0, 16384)
    gated = u8_bf(ar2, 0, 8192)
    sg0 = u8_bf(ar2, 16384, 1024)
    sg1 = u8_bf(ar2, 18432, 1024)
    ffw0 = u8_f32(ar2, 20480, 1024)
    ffw1 = u8_f32(ar2, 24576, 1024)
    kb_own = u8_bf(ar2, 0, 16384)

    av = u8_bf(ar3, 0, 16384)
    wd16 = u8_bf(ar3, 0, 16384)
    w16a = u8_bf(ar3, 0, 16384)

    w16b = u8_bf(ar4, 0, 16384)
    kh0 = u8_bf(ar4, 0, 4096)
    kh1 = u8_bf(ar4, 8192, 4096)
    vh0 = u8_bf(ar4, 16384, 4096)
    vh1 = u8_bf(ar4, 24576, 4096)
    wg16s = [u8_bf(ar4, 0, 2048), u8_bf(ar4, 4096, 2048)]
    wu16s = [u8_bf(ar4, 8192, 2048), u8_bf(ar4, 12288, 2048)]
    ffw2 = u8_f32(ar4, 16384, 1024)
    ffw3 = u8_f32(ar4, 20480, 1024)
    wo16s = [u8_bf(ar4, 0, 2048), u8_bf(ar4, 4096, 2048)]

    wst0 = u8_f32(scr, 0, 1024)
    wst1 = u8_f32(scr, 4096, 1024)
    ln_tm0 = u8_bf(ar2, 0, 2048)
    ln_tm1 = u8_bf(ar2, 4096, 2048)
    sqscr = u8_f32(scr, 8192, 512)

    maskA = u8_f32(cst, 0, 128)
    maskB = u8_f32(cst, 512, 128)
    ident = u8_bf(cst, 1024, 128)
    ones1 = u8_bf(cst, 1280, 128)
    bq_c = u8_f32(cst, 1536, 16)
    bk_c = u8_f32(cst, 1600, 16)
    bv_c = u8_f32(cst, 1664, 16)
    lng_c = u8_f32(cst, 1728, 16)
    lnb_c = u8_f32(cst, 1792, 16)
    eps_c = u8_f32(cst, 1856, 1)
    bqs_c = u8_f32(sta, 1680, 16)

    ssum = u8_f32(sta, 0, 16)
    sqac = u8_f32(sta, 64, 16)
    mu_s = u8_f32(sta, 128, 16)
    var_s = u8_f32(sta, 192, 16)
    rstd_s = u8_f32(sta, 256, 16)
    t0_s = u8_f32(sta, 320, 16)
    t1_s = u8_f32(sta, 384, 16)
    mbuf = u8_f32(sta, 448, 8)
    bias_b = u8_f32(sta, 480, 8)
    acc_b = u8_f32(sta, 512, 8)
    fv_b = u8_f32(sta, 544, 8)
    accf_b = u8_f32(sta, 576, 8)
    tot_b = u8_f32(sta, 608, 2)
    rinv = u8_f32(sta, 640, 128)
    rawm = u8_f32(sta, 1152, 4)
    avst0 = u8_bf(sta, 1168, 128)
    avst1 = u8_bf(sta, 1424, 128)

    ps = e(nc.psum_tensor("ps", [128, 4096], F32))

    def bank(k, n=1):
        return ps[:, 512 * k:512 * (k + n)]

    def bank_bf(k, n=1):
        return ps[:, 512 * k:512 * (k + n)].bitcast(BF16)

    sems = {}
    for name in ("qX", "qS0", "qS1", "qA0", "qA1", "qF0", "qF1", "qF2", "qF3",
                 "qW", "qC", "qK", "qY", "cc0", "cc1", "pe", "dv", "ac", "po"):
        sems[name] = e(nc.semaphore(name))

    block = e(nc.Block())

    n_heads = cfg["n_heads"]
    n_tiles = cfg["n_tiles"]
    n_hc = cfg["n_hc"]
    sublayers = cfg["sublayers"]

    def emit(me, eng):
        E = Em(me, eng, sems)
        w, do, do0, C = E.w, E.do, E.do0, E.C
        def scope(name):
            return nc.named_scope(name)

        x_loaded = []
        do("sync", "qX", lambda e: e.dma_start(
            out=x_sb[:, 0:D], in_=dX[0:128, :]), 16)
        x_loaded.append(("qX", C["qX"]))
        for i in range(1, TI):
            do("sync", "qY", lambda e, i=i: e.dma_start(
                out=x_sb[:, i * D:(i + 1) * D], in_=dX[i * 128:(i + 1) * 128, :]), 16)
            x_loaded.append(("qY", C["qY"]))
        for (dst, src) in ((maskA, dMA), (maskB, dMB), (ident, dID),
                           (bq_c, dBQ), (bk_c, dBK), (bv_c, dBV),
                           (lng_c, dLNG), (lnb_c, dLNB)):
            do("sync", "qK", lambda e, d=dst, s=src: e.dma_start(out=d, in_=s), 16)
        do("sync", "qK", lambda e: e.dma_start(out=ones1[0:1, :], in_=dONES), 16)
        consts_ok = C["qK"]
        if cfg.get("zero_arenas"):
            for t in (ar1, ar2, ar3, ar4):
                do("vector", "dv", lambda e, t=t: e.memset(t[:, :], 0))
        zero_dv = C["dv"]
        eps_dv = do("vector", "dv", lambda e: e.memset(eps_c, EPS_LN))
        w("vector", "qK", consts_ok)
        bqs_dv = do("vector", "dv", lambda e: e.tensor_scalar_mul(
            out=bqs_c[:], in0=bq_c[:], scalar1=SCALE))

        state = dict(tb=0, avb=0)

        def layer_norm(double_ln=False, affine=False, x_ready=0, arena_pe=0,
                       scratch_po=0):
            for i in range(n_tiles):
                sl = i % 2
                ln_tm = ln_tm0 if sl == 0 else ln_tm1
                xi = x_sb[:, i * D:(i + 1) * D]
                w("vector", x_loaded[i][0], x_loaded[i][1])
                w("vector", "dv", x_ready)
                v_sum = do("vector", "dv", lambda e, xi=xi, i=i: e.reduce_sum(
                    out=ssum[:, i:i + 1], in_=xi, axis=AX))
                w("scalar", x_loaded[i][0], x_loaded[i][1])
                w("scalar", "dv", x_ready)
                a_sq0 = do("scalar", "ac", lambda e, xi=xi, i=i: e.activation(
                    out=sqscr[:], in_=xi[:, 0:512], func=AF.Square,
                    accum_out=sqac[:, 2 * (i % 8):2 * (i % 8) + 1]))
                w("scalar", "ac", a_sq0)
                a_sq1 = do("scalar", "ac", lambda e, xi=xi, i=i: e.activation(
                    out=sqscr[:], in_=xi[:, 512:1024], func=AF.Square,
                    accum_out=t0_s[:, i:i + 1]))
                w("scalar", "ac", a_sq1)
                a_sq2 = do("scalar", "ac", lambda e, xi=xi, i=i: e.activation(
                    out=sqscr[:], in_=xi[:, 1024:1536], func=AF.Square,
                    accum_out=t1_s[:, i:i + 1]))
                w("scalar", "ac", a_sq2)
                a_sq3 = do("scalar", "ac", lambda e, xi=xi, i=i: e.activation(
                    out=sqscr[:], in_=xi[:, 1536:2048], func=AF.Square,
                    accum_out=sqac[:, 2 * (i % 8) + 1:2 * (i % 8) + 2]))

                def stats_chain(i, v_sum, a_sq3, pfx_mu, pfx_var, pfx_rstd):
                    w("vector", "dv", v_sum)
                    v_mu = do("vector", "dv", lambda e, i=i: e.tensor_scalar_mul(
                        out=mu_s[:, i:i + 1], in0=ssum[:, i:i + 1], scalar1=1.0 / D))
                    w("vector", "ac", a_sq3)
                    v_s2 = do("vector", "dv", lambda e, i=i: e.tensor_add(
                        out=sqac[:, 2 * (i % 8):2 * (i % 8) + 1],
                        in0=sqac[:, 2 * (i % 8):2 * (i % 8) + 1],
                        in1=t0_s[:, i:i + 1]))
                    w("vector", "dv", v_s2)
                    v_s2b = do("vector", "dv", lambda e, i=i: e.tensor_add(
                        out=t1_s[:, i:i + 1], in0=t1_s[:, i:i + 1],
                        in1=sqac[:, 2 * (i % 8) + 1:2 * (i % 8) + 2]))
                    w("vector", "dv", v_s2b)
                    v_s2c = do("vector", "dv", lambda e, i=i: e.tensor_add(
                        out=sqac[:, 2 * (i % 8):2 * (i % 8) + 1],
                        in0=sqac[:, 2 * (i % 8):2 * (i % 8) + 1],
                        in1=t1_s[:, i:i + 1]))
                    w("vector", "dv", max(v_mu, v_s2c))
                    v_mu2 = do("vector", "dv", lambda e, i=i: e.tensor_mul(
                        out=t0_s[:, i:i + 1], in0=mu_s[:, i:i + 1], in1=mu_s[:, i:i + 1]))
                    w("vector", "dv", v_mu2)
                    v_var = do("vector", "dv", lambda e, i=i: e.scalar_tensor_tensor(
                        out=var_s[:, i:i + 1], in0=sqac[:, 2 * (i % 8):2 * (i % 8) + 1],
                        scalar=1.0 / D, in1=t0_s[:, i:i + 1],
                        op0=ALU.mult, op1=ALU.subtract))
                    w("scalar", "dv", v_var)
                    w("scalar", "dv", eps_dv)
                    a_ln = do("scalar", "ac", lambda e, i=i: e.activation(
                        out=t1_s[:, i:i + 1], in_=var_s[:, i:i + 1], func=AF.Ln,
                        bias=eps_c[:, 0:1]))
                    w("scalar", "ac", a_ln)
                    a_rs = do("scalar", "ac", lambda e, i=i: e.activation(
                        out=rstd_s[:, i:i + 1], in_=t1_s[:, i:i + 1], func=AF.Exp,
                        scale=-0.5))
                    return a_rs

                a_rs = stats_chain(i, v_sum, a_sq3, mu_s, var_s, rstd_s)

                if not double_ln:
                    w("vector", "ac", a_rs)
                    if i < 2:
                        w("vector", "pe", arena_pe)
                    v_ln = do("vector", "dv", lambda e, xi=xi, i=i, lt=ln_tm: e.tensor_scalar(
                        out=lt[:], in0=xi, scalar1=mu_s[:, i:i + 1],
                        scalar2=rstd_s[:, i:i + 1], op0=ALU.subtract, op1=ALU.mult))
                else:
                    y32 = ar2[:, 8192:16384].bitcast(F32)
                    w("vector", "ac", a_rs)
                    if i < 2:
                        w("vector", "pe", arena_pe)
                    v_y = do("vector", "dv", lambda e, xi=xi, i=i, y=y32: e.tensor_scalar(
                        out=y[:], in0=xi, scalar1=mu_s[:, i:i + 1],
                        scalar2=rstd_s[:, i:i + 1], op0=ALU.subtract, op1=ALU.mult))
                    w("vector", "dv", v_y)
                    v_sum2 = do("vector", "dv", lambda e, y=y32, i=i: e.reduce_sum(
                        out=ssum[:, 8 + (i % 8):9 + (i % 8)], in_=y, axis=AX))
                    w("scalar", "dv", v_y)
                    a_q0 = do("scalar", "ac", lambda e, y=y32, i=i: e.activation(
                        out=sqscr[:], in_=y[:, 0:512], func=AF.Square,
                        accum_out=sqac[:, 2 * (i % 8):2 * (i % 8) + 1]))
                    w("scalar", "ac", a_q0)
                    a_q1 = do("scalar", "ac", lambda e, y=y32, i=i: e.activation(
                        out=sqscr[:], in_=y[:, 512:1024], func=AF.Square,
                        accum_out=t0_s[:, i:i + 1]))
                    w("scalar", "ac", a_q1)
                    a_q2 = do("scalar", "ac", lambda e, y=y32, i=i: e.activation(
                        out=sqscr[:], in_=y[:, 1024:1536], func=AF.Square,
                        accum_out=t1_s[:, i:i + 1]))
                    w("scalar", "ac", a_q2)
                    a_q3 = do("scalar", "ac", lambda e, y=y32, i=i: e.activation(
                        out=sqscr[:], in_=y[:, 1536:2048], func=AF.Square,
                        accum_out=sqac[:, 2 * (i % 8) + 1:2 * (i % 8) + 2]))
                    w("vector", "dv", v_sum2)
                    v_mu = do("vector", "dv", lambda e, i=i: e.tensor_scalar_mul(
                        out=mu_s[:, 8 + (i % 8):9 + (i % 8)],
                        in0=ssum[:, 8 + (i % 8):9 + (i % 8)], scalar1=1.0 / D))
                    w("vector", "ac", a_q3)
                    v_a = do("vector", "dv", lambda e, i=i: e.tensor_add(
                        out=sqac[:, 2 * (i % 8):2 * (i % 8) + 1],
                        in0=sqac[:, 2 * (i % 8):2 * (i % 8) + 1], in1=t0_s[:, i:i + 1]))
                    w("vector", "dv", v_a)
                    v_b = do("vector", "dv", lambda e, i=i: e.tensor_add(
                        out=t1_s[:, i:i + 1], in0=t1_s[:, i:i + 1],
                        in1=sqac[:, 2 * (i % 8) + 1:2 * (i % 8) + 2]))
                    w("vector", "dv", v_b)
                    v_c = do("vector", "dv", lambda e, i=i: e.tensor_add(
                        out=sqac[:, 2 * (i % 8):2 * (i % 8) + 1],
                        in0=sqac[:, 2 * (i % 8):2 * (i % 8) + 1], in1=t1_s[:, i:i + 1]))
                    w("vector", "dv", max(v_mu, v_c))
                    v_mu2 = do("vector", "dv", lambda e, i=i: e.tensor_mul(
                        out=t0_s[:, i:i + 1], in0=mu_s[:, 8 + (i % 8):9 + (i % 8)],
                        in1=mu_s[:, 8 + (i % 8):9 + (i % 8)]))
                    w("vector", "dv", v_mu2)
                    v_var = do("vector", "dv", lambda e, i=i: e.scalar_tensor_tensor(
                        out=var_s[:, 8 + (i % 8):9 + (i % 8)],
                        in0=sqac[:, 2 * (i % 8):2 * (i % 8) + 1],
                        scalar=1.0 / D, in1=t0_s[:, i:i + 1],
                        op0=ALU.mult, op1=ALU.subtract))
                    w("scalar", "dv", v_var)
                    w("scalar", "dv", eps_dv)
                    a_l2 = do("scalar", "ac", lambda e, i=i: e.activation(
                        out=t1_s[:, i:i + 1], in_=var_s[:, 8 + (i % 8):9 + (i % 8)],
                        func=AF.Ln, bias=eps_c[:, 0:1]))
                    w("scalar", "ac", a_l2)
                    a_rs2 = do("scalar", "ac", lambda e, i=i: e.activation(
                        out=rstd_s[:, 8 + (i % 8):9 + (i % 8)], in_=t1_s[:, i:i + 1],
                        func=AF.Exp, scale=-0.5))
                    w("vector", "ac", a_rs2)
                    v_ln = do("vector", "dv", lambda e, y=y32, i=i, lt=ln_tm: e.tensor_scalar(
                        out=lt[:], in0=y, scalar1=mu_s[:, 8 + (i % 8):9 + (i % 8)],
                        scalar2=rstd_s[:, 8 + (i % 8):9 + (i % 8)],
                        op0=ALU.subtract, op1=ALU.mult))

                w("tensor", "qK", consts_ok)
                w("tensor", "dv", zero_dv)
                w("tensor", "pe", arena_pe)
                for g in range(4):
                    tb = state["tb"]
                    state["tb"] ^= 1
                    tbank = 4 + tb
                    w("tensor", "dv", state.get(("tev", tb), 0))
                    w("tensor", "dv", v_ln)
                    for j in range(4):
                        dc = g * 4 + j
                        pe_v = do("tensor", "pe", lambda e, lt=ln_tm, dc=dc, j=j, tbank=tbank: e.transpose(
                            out=bank_bf(tbank)[:, j * 128:(j + 1) * 128],
                            in_=lt[:, dc * 128:(dc + 1) * 128], identity=ident))
                    w("vector", "pe", pe_v)
                    if affine:
                        for j in range(4):
                            dc = g * 4 + j
                            ev = do("vector", "dv", lambda e, dc=dc, j=j, i=i, tbank=tbank: e.tensor_scalar(
                                out=lnT[:, dc * 1024 + i * 128: dc * 1024 + (i + 1) * 128],
                                in0=bank_bf(tbank)[:, j * 128:(j + 1) * 128],
                                scalar1=lng_c[:, dc:dc + 1], scalar2=lnb_c[:, dc:dc + 1],
                                op0=ALU.mult, op1=ALU.add))
                    else:
                        ev = do("vector", "dv", lambda e, g=g, i=i, tbank=tbank: e.tensor_copy(
                            out=lnT.rearrange("p (c n) -> p c n", n=1024)[
                                :, g * 4:(g + 1) * 4, i * 128:(i + 1) * 128],
                            in_=bank_bf(tbank).rearrange("p (c n) -> p c n", n=128)[:, 0:4, :]))
                    state[("tev", tb)] = ev
            state["scr_free_pe"] = C["pe"]
            return C["dv"], C["ac"]

        def w16_tile(dcc):
            if dcc < 8:
                return w16a[:, dcc * 2048:(dcc + 1) * 2048]
            return w16b[:, (dcc - 8) * 2048:(dcc - 8 + 1) * 2048]

        CAST_SEM = {"gpsimd": "po", "vector": "dv", "scalar": "ac"}

        def cast_to(engx, dst, srcap, waits=()):
            for (s, v) in waits:
                w(engx, s, v)
            sem = CAST_SEM[engx]
            if engx == "scalar":
                v = do(engx, sem, lambda e, dst=dst, srcap=srcap: e.copy(
                    out=dst, in_=srcap))
            else:
                v = do(engx, sem, lambda e, dst=dst, srcap=srcap: e.tensor_copy(
                    out=dst, in_=srcap))
            return (sem, v)

        STG = [(wst0, "qS0"), (wst1, "qS1")]

        def stream_w_resident(src, gate_sem=None, gate_val=0, dst_tile=None,
                              cast_engs=("vector", "scalar"), cast_wait=None,
                              ring=None):
            if dst_tile is None:
                dst_tile = w16_tile
            if ring is None:
                ring = STG
            for dcc in range(DC):
                for hhalf in range(2):
                    k = dcc * 2 + hhalf
                    p = k % len(ring)
                    st, qsem = ring[p]
                    if gate_sem is not None and k == 0:
                        w("sync", gate_sem, gate_val)
                    ssem, sval = state.get(("fst", qsem), ("po", 0))
                    w("sync", ssem, sval)
                    qv = do("sync", qsem, lambda e, st=st, dcc=dcc, hh=hhalf, src=src: e.dma_start(
                        out=st, in_=src[dcc * 128:(dcc + 1) * 128, hh * 1024:(hh + 1) * 1024]), 16)
                    engx = cast_engs[k % len(cast_engs)]
                    wl = [(qsem, qv)]
                    if cast_wait is not None:
                        wl.append(cast_wait)
                    last = cast_to(engx, dst_tile(dcc)[:, hhalf * 1024:(hhalf + 1) * 1024],
                                   st, waits=wl)
                    state[("fst", qsem)] = last
            return dict(po=C["po"], dv=C["dv"], ac=C["ac"])

        def wait_ready(owner, r):
            if isinstance(r, dict):
                w(owner, "po", r["po"])
                w(owner, "dv", r["dv"])
                w(owner, "ac", r["ac"])
            else:
                w(owner, "po", r)

        def projection(dst, bias_col, lnT_ready_dv, w_po, extra_pe_wait=0, n_oc=DC,
                       scale_imm=None):
            for oc in range(n_oc):
                for th in range(2):
                    pbk = (2 * oc + th) % 4
                    psem, pval = state.get(("pjev", pbk), ("ac", 0))
                    w("tensor", psem, pval)
                    w("tensor", "dv", lnT_ready_dv)
                    wait_ready("tensor", w_po)
                    w("tensor", "pe", extra_pe_wait)
                    for dcc in range(DC):
                        pe_v = do("tensor", "pe", lambda e, oc=oc, th=th, dcc=dcc, pbk=pbk: e.matmul(
                            out=bank(pbk),
                            lhsT=w16_tile(dcc)[:, oc * 128:(oc + 1) * 128],
                            rhs=lnT[:, dcc * 1024 + th * 512: dcc * 1024 + (th + 1) * 512],
                            start=(dcc == 0), stop=(dcc == DC - 1)))
                    if scale_imm is not None and bias_col is not None:
                        w("vector", "pe", pe_v)
                        evd = do("vector", "dv", lambda e, oc=oc, th=th, pbk=pbk, dst=dst, b=bias_col, s=scale_imm: e.tensor_scalar(
                            out=dst[:, oc * 1024 + th * 512: oc * 1024 + (th + 1) * 512],
                            in0=bank(pbk), scalar1=s, scalar2=b[:, oc:oc + 1],
                            op0=ALU.mult, op1=ALU.add))
                        state[("pjev", pbk)] = ("dv", evd)
                        continue
                    w("scalar", "pe", pe_v)
                    if scale_imm is not None:
                        ev = do("scalar", "ac", lambda e, oc=oc, th=th, pbk=pbk, dst=dst, s=scale_imm: e.activation(
                            out=dst[:, oc * 1024 + th * 512: oc * 1024 + (th + 1) * 512],
                            in_=bank(pbk), func=AF.Copy, scale=s, bias=0.0))
                    elif bias_col is None:
                        ev = do("scalar", "ac", lambda e, oc=oc, th=th, pbk=pbk, dst=dst: e.copy(
                            out=dst[:, oc * 1024 + th * 512: oc * 1024 + (th + 1) * 512],
                            in_=bank(pbk)))
                    else:
                        ev = do("scalar", "ac", lambda e, oc=oc, th=th, pbk=pbk, dst=dst, b=bias_col: e.add(
                            out=dst[:, oc * 1024 + th * 512: oc * 1024 + (th + 1) * 512],
                            in_=bank(pbk),
                            add=b[:, oc:oc + 1]))
                    state[("pjev", pbk)] = ("ac", ev)
            return (C["ac"], C["dv"])

        def attn_unit(h, i, u, q_lhsT, kT, nk, ptm, masks, slot, weave=()):
            CN = 1024
            nch = (nk + CN - 1) // CN
            exp_acs = []
            first_dve = [True]
            wv = list(weave)

            def unit_dve_gate():
                if first_dve[0]:
                    w("vector", "dv", state.get("unit_dv", 0))
                    first_dve[0] = False

            v_m = None
            for c in range(nch):
                k0 = c * CN
                k1 = min(nk, k0 + CN)
                par = c % 2
                sb = 2 * par
                w("tensor", "ac", state.get(("sxp", par), 0))
                n0 = k1 - k0
                pe_v = None
                for j in range(0, n0, 512):
                    nn = min(512, n0 - j)
                    pe_v = do("tensor", "pe", lambda e, sb=sb, j=j, nn=nn, k0=k0, q=q_lhsT, kT=kT: e.matmul(
                        out=bank(sb, 2)[:, j:j + nn], lhsT=q,
                        rhs=kT[:, k0 + j:k0 + j + nn], start=True, stop=True))
                if wv:
                    wv.pop(0)()
                mv = None
                for (bi, m_ap) in masks:
                    b0 = bi * 128
                    if b0 >= k0 and b0 < k1:
                        w("vector", "pe", pe_v)
                        unit_dve_gate()
                        mv = do("vector", "dv", lambda e, sb=sb, b0=b0, k0=k0, m=m_ap: e.tensor_add(
                            out=bank(sb, 2)[:, b0 - k0:b0 - k0 + 128],
                            in0=bank(sb, 2)[:, b0 - k0:b0 - k0 + 128], in1=m))
                w("vector", "pe", pe_v)
                unit_dve_gate()
                if mv is not None:
                    w("vector", "dv", mv)
                if c == 0:
                    v_m = do("vector", "dv", lambda e, sb=sb, n0=n0: e.reduce_max(
                        out=mbuf[:, 0:1], in_=bank(sb, 2)[:, 0:n0], axis=AX, negate=True))
                else:
                    w("vector", "dv", v_m)
                    v_r = do("vector", "dv", lambda e, sb=sb, n0=n0: e.reduce_max(
                        out=rawm[:, 0:1], in_=bank(sb, 2)[:, 0:n0], axis=AX, negate=True))
                    w("vector", "dv", v_r)
                    v_m = do("vector", "dv", lambda e, c=c: e.tensor_scalar(
                        out=mbuf[:, c:c + 1], in0=rawm[:, 0:1],
                        scalar1=mbuf[:, c - 1:c], scalar2=None, op0=ALU.min))
                w("scalar", "dv", v_m)
                if c == 0:
                    w("scalar", "pe", state.get(("ptm_pe", slot), 0))
                a_e = do("scalar", "ac", lambda e, sb=sb, c=c, k0=k0, n0=n0, ptm=ptm: e.activation(
                    out=ptm[:, k0:k0 + n0], in_=bank(sb, 2)[:, 0:n0], func=AF.Exp,
                    bias=mbuf[:, c:c + 1], accum_out=acc_b[:, c:c + 1]))
                state[("sxp", par)] = a_e
                exp_acs.append(a_e)
            for f in wv:
                f()
            last = nch - 1
            w("scalar", "dv", v_m)
            w("scalar", "ac", exp_acs[-1])
            a_f = do("scalar", "ac", lambda e, nch=nch, last=last: e.activation(
                out=fv_b[:, 0:nch], in_=mbuf[:, 0:nch], func=AF.Exp, scale=-1.0,
                bias=mbuf[:, last:last + 1]))
            w("vector", "ac", a_f)
            v_t = do("vector", "dv", lambda e, nch=nch: e.scalar_tensor_tensor(
                out=accf_b[:, 0:nch], in0=acc_b[:, 0:nch], scalar=1.0,
                in1=fv_b[:, 0:nch], op0=ALU.mult, op1=ALU.mult,
                accum_out=tot_b[:, 0:1]))
            w("vector", "dv", v_t)
            v_ri = do("vector", "dv", lambda e: e.reciprocal(
                out=tot_b[:, 1:2], in_=tot_b[:, 0:1]))
            w("vector", "dv", v_ri)
            corr_dv = C["dv"]
            for c in range(nch):
                k0 = c * CN
                k1 = min(nk, k0 + CN)
                corr_dv = do("vector", "dv", lambda e, c=c, k0=k0, k1=k1, ptm=ptm: e.tensor_scalar(
                    out=ptm[:, k0:k1], in0=ptm[:, k0:k1], scalar1=fv_b[:, c:c + 1],
                    scalar2=tot_b[:, 1:2], op0=ALU.mult, op1=ALU.mult))
            state["unit_dv"] = corr_dv
            return dict(exp_ac=exp_acs[-1], corr_dv=corr_dv)

        def make_T_groups(h, i, u, unit, ptm, nk, slot):
            nkc = nk // 128
            ip = i % 2
            ngr = (nkc + 7) // 8
            closures = []

            def grp(g):
                tb = state["tb"]
                state["tb"] ^= 1
                tbank = 4 + tb
                w("tensor", "dv", state.get(("tev", tb), 0))
                w("tensor", "ac", state.get(("tev_ac", tb), 0))
                w("tensor", "ac", unit["exp_ac"])
                w("tensor", "dv", unit["corr_dv"])
                ntr = min(8, nkc - g * 8)
                pe_v = None
                for j in range(ntr):
                    kc = g * 8 + j
                    pe_v = do("tensor", "pe", lambda e, kc=kc, j=j, tbank=tbank, ptm=ptm: e.transpose(
                        out=bank_bf(tbank)[:, j * 128:(j + 1) * 128],
                        in_=ptm[:, kc * 128:(kc + 1) * 128], identity=ident))
                out_ap = pt.rearrange("p (c n) -> p c n", n=256)[
                    :, g * 8:g * 8 + ntr, ip * 128:(ip + 1) * 128]
                in_ap = bank_bf(tbank).rearrange("p (c n) -> p c n", n=128)[:, 0:ntr, :]
                if g % 2 == 0:
                    w("vector", "pe", pe_v)
                    if g == 0:
                        w("vector", "pe", state.get("pt_free_pe", 0))
                    ev = do("vector", "dv", lambda e, o=out_ap, i_=in_ap: e.tensor_copy(
                        out=o, in_=i_))
                    state[("tev", tb)] = ev
                else:
                    w("scalar", "pe", pe_v)
                    if g == 1:
                        w("scalar", "pe", state.get("pt_free_pe", 0))
                    ev_ac = do("scalar", "ac", lambda e, o=out_ap, i_=in_ap: e.copy(
                        out=o, in_=i_))
                    state[("tev_ac", tb)] = ev_ac
                if g == ngr - 1:
                    state[("ptm_pe", slot)] = C["pe"]
                    state[("pt_done_dv", ip)] = C["dv"]
                    state[("pt_done_ac", ip)] = C["ac"]

            for g in range(ngr):
                closures.append(lambda g=g: grp(g))
            return closures

        def emit_av_pair(h, i_odd, vh, nk_odd, nk_even, av_dst_col):
            nkc = nk_odd // 128
            nkc_e = nk_even // 128
            ab = state["avb"]
            state["avb"] ^= 1
            avbank = 6 + ab
            w("tensor", "dv", state[("pt_done_dv", 0)])
            w("tensor", "ac", state[("pt_done_ac", 0)])
            w("tensor", "dv", state[("pt_done_dv", 1)])
            w("tensor", "ac", state[("pt_done_ac", 1)])
            w("tensor", "dv", state.get(("avev", ab), 0))
            pe_av = None
            for kc in range(nkc):
                if kc < nkc_e:
                    o0, nn = 0, 256
                else:
                    o0, nn = 128, 128
                pe_av = do("tensor", "pe", lambda e, kc=kc, avbank=avbank, vh=vh, o0=o0, nn=nn: e.matmul(
                    out=bank(avbank)[:, o0:o0 + nn], lhsT=vh[:, kc * 128:(kc + 1) * 128],
                    rhs=pt[:, kc * 256 + o0: kc * 256 + o0 + nn],
                    start=(kc == 0), stop=(kc == nkc - 1)))
            state["pt_free_pe"] = C["pe"]
            w("vector", "pe", pe_av)
            v_c = do("vector", "dv", lambda e, avbank=avbank, col=av_dst_col: e.tensor_copy(
                out=av[:, col - 128:col + 128], in_=bank(avbank)[:, 0:256]))
            state[("avev", ab)] = v_c
            return v_c

        def ffn_sublayer(x_ready, arena_pe, scratch_po):
            ln_dv, _ = layer_norm(x_ready=x_ready, arena_pe=arena_pe,
                                  scratch_po=scratch_po)
            state[("w16free", 0)] = arena_pe
            state[("w16free", 1)] = arena_pe
            state["ffn_down_pe"] = arena_pe
            state["ffn_dn_ev01"] = x_ready
            state["ffn_tt_dv"] = x_ready
            state[("sgfree", 0)] = x_ready
            state[("sgfree", 1)] = x_ready
            nblk = max(1, n_hc // 8)
            for blk in range(nblk):
                for k in range(8):
                    hc = blk * 8 + k
                    par = hc % 2
                    gu_marks = []
                    for (srcw, stg, stsem, dsts, cengs) in (
                        (dWG, ffw0, "qF0", lambda h=hc: wg16s[h % 2], ("gpsimd", "vector")),
                        (dWU, ffw1, "qF1", lambda h=hc: wu16s[h % 2], ("gpsimd", "scalar")),
                    ):
                        for half in range(2):
                            ssem, sval = state.get(("fst", stsem), ("po", 0))
                            w("sync", ssem, sval)
                            if blk == 0 and k == 0 and half == 0:
                                w("sync", "pe", arena_pe)
                                w("sync", "dv", ln_dv)
                            qv = do("sync", stsem, lambda e, st=stg, srcw=srcw, hc=hc, half=half: e.dma_start(
                                out=st, in_=srcw[hc * 128:(hc + 1) * 128,
                                                 half * 1024:(half + 1) * 1024]), 16)
                            engx = cengs[half]
                            mk = cast_to(engx,
                                         dsts()[:, half * 1024:(half + 1) * 1024], stg,
                                         waits=[(stsem, qv),
                                                ("pe", state.get(("w16free", par), 0))])
                            state[("fst", stsem)] = mk
                            gu_marks.append(mk)
                    wd_marks = []
                    for half in range(2):
                        stsem = "qF2" if half == 0 else "qF3"
                        stg = ffw2 if half == 0 else ffw3
                        ssem, sval = state.get(("fst", stsem), ("po", 0))
                        w("sync", ssem, sval)
                        if blk == 0 and k == 0 and half == 0:
                            w("sync", "pe", arena_pe)
                        qv = do("sync", stsem, lambda e, st=stg, hc=hc, half=half: e.dma_start(
                            out=st, in_=dWD[hc * 128:(hc + 1) * 128,
                                            half * 1024:(half + 1) * 1024]), 16)
                        engx = ("vector", "scalar")[(k + half) % 2]
                        wts = [(stsem, qv), ("pe", state.get("ffn_down_pe", 0))]
                        if blk == 0 and k == 0:
                            wts.append(("dv", x_ready))
                        mk = cast_to(engx,
                                     wd16[:, k * 2048 + half * 1024: k * 2048 + (half + 1) * 1024],
                                     stg, waits=wts)
                        state[("fst", stsem)] = mk
                        wd_marks.append(mk)
                    state[("wd_marks", k)] = list(wd_marks)
                    for (s_, v_) in gu_marks:
                        w("tensor", s_, v_)
                    w("tensor", "dv", ln_dv)
                    w("tensor", "ac", state.get("ffn_silu_ac", 0))
                    w("tensor", "dv", state.get("ffn_dn_ev01", 0))
                    for dcc in range(DC):
                        for th in range(2):
                            pe_g = do("tensor", "pe", lambda e, dcc=dcc, th=th, hc=hc: e.matmul(
                                out=bank(th), lhsT=wg16s[hc % 2][:, dcc * 128:(dcc + 1) * 128],
                                rhs=lnT[:, dcc * 1024 + th * 512: dcc * 1024 + (th + 1) * 512],
                                start=(dcc == 0), stop=(dcc == DC - 1)))
                    w("scalar", "pe", pe_g)
                    w("scalar", "dv", state.get(("sgfree", hc % 2), 0))
                    a_si = do("scalar", "ac", lambda e, hc=hc: e.activation(
                        out=(sg0 if hc % 2 == 0 else sg1)[:], in_=bank(0, 2), func=AF.Silu))
                    state["ffn_silu_ac"] = a_si
                    w("tensor", "dv", state.get("ffn_tt_dv", 0))
                    for dcc in range(DC):
                        for th in range(2):
                            pe_u = do("tensor", "pe", lambda e, dcc=dcc, th=th, hc=hc: e.matmul(
                                out=bank(2 + th), lhsT=wu16s[hc % 2][:, dcc * 128:(dcc + 1) * 128],
                                rhs=lnT[:, dcc * 1024 + th * 512: dcc * 1024 + (th + 1) * 512],
                                start=(dcc == 0), stop=(dcc == DC - 1)))
                    state[("w16free", par)] = C["pe"]
                    w("vector", "pe", pe_u)
                    w("vector", "ac", a_si)
                    w("vector", "pe", state.get("ffn_down_pe", 0))
                    v_tt = do("vector", "dv", lambda e, hc=hc, k=k: e.tensor_mul(
                        out=gated[:, k * 1024:(k + 1) * 1024], in0=bank(2, 2),
                        in1=(sg0 if hc % 2 == 0 else sg1)[:]))
                    state["ffn_tt_dv"] = v_tt
                    state[("sgfree", hc % 2)] = v_tt
                    state[("ffn_gated", k)] = v_tt
                for tt in range(n_tiles):
                    pb = 4 * (tt % 2)
                    w("tensor", "dv", state[("ffn_gated", 7)])
                    if tt == 0:
                        for kk in range(8):
                            for (s_, v_) in state[("wd_marks", kk)]:
                                w("tensor", s_, v_)
                    w("tensor", "ac", state.get("ffn_silu_ac", 0))
                    w("tensor", "dv", state.get(("ffn_dn_ev", tt % 2), 0))
                    for hcc in range(8):
                        for j in range(4):
                            pe_d = do("tensor", "pe", lambda e, hcc=hcc, j=j, tt=tt, pb=pb: e.matmul(
                                out=bank(pb + j),
                                lhsT=gated[:, hcc * 1024 + tt * 128: hcc * 1024 + (tt + 1) * 128],
                                rhs=wd16[:, hcc * 2048 + j * 512: hcc * 2048 + (j + 1) * 512],
                                start=(hcc == 0), stop=(hcc == 7)))
                    w("vector", "pe", pe_d)
                    w("vector", "dv", state.get(("x_guard", tt), 0))
                    ev = do("vector", "dv", lambda e, tt=tt, pb=pb: e.tensor_add(
                        out=x_sb[:, tt * D:(tt + 1) * D],
                        in0=x_sb[:, tt * D:(tt + 1) * D], in1=bank(pb, 4)))
                    state[("ffn_dn_ev", tt % 2)] = ev
                    state[("x_guard", tt)] = ev
                    if tt % 2 == 0:
                        state["ffn_dn_ev01"] = ev
                state["ffn_down_pe"] = C["pe"]
            return C["dv"]
        if 1 in sublayers:
            with scope("s1_ln"):
                ln_dv, ln_ac = layer_norm(double_ln=False, affine=False)

            wq_po = stream_w_resident(
                dWQA, gate_sem=("dv" if cfg.get("zero_arenas") else None),
                gate_val=zero_dv if cfg.get("zero_arenas") else 0,
                cast_engs=("gpsimd",))
            q_ac, _ = projection(qa, None, ln_dv, wq_po, n_oc=n_heads, scale_imm=SCALE)
            state["kv_gate_pe"] = C["pe"]

            kv_po = {}
            def stream_head_kv2(h):
                slot = h % 2
                kh = kh0 if slot == 0 else kh1
                vh = vh0 if slot == 0 else vh1
                for part in range(8):
                    is_v = part >= 4
                    src = dVB if is_v else dKT
                    dst = vh if is_v else kh
                    col0 = (part % 4) * 1024
                    st, qsem = STG[part % len(STG)]
                    ssem, sval = state.get(("fst", qsem), ("po", 0))
                    w("sync", ssem, sval)
                    w("sync", "dv", state.get(("kvdone", slot), 0))
                    qv = do("sync", qsem, lambda e, st=st, src=src, h=h, col0=col0: e.dma_start(
                        out=st, in_=src[h * 128:(h + 1) * 128, col0:col0 + 1024]), 16)
                    w("gpsimd", qsem, qv)
                    w("gpsimd", "pe", state.get("kv_gate_pe", 0))
                    pv = do("gpsimd", "po", lambda e, st=st, dst=dst, col0=col0: e.tensor_copy(
                        out=dst[:, col0:col0 + 1024], in_=st))
                    state[("fst", qsem)] = ("po", pv)
                kv_po[h] = C["po"]

            stream_head_kv2(0)
            if n_heads > 1:
                stream_head_kv2(1)

            ctx_h = scope("s1_heads")
            ctx_h.__enter__()
            pend_weave = []
            pend_av = None

            def mk_av_a(h, i, vh):
                def go(h=h, i=i, vh=vh):
                    res = emit_av_pair(h, i, vh, KVLEN, KVLEN,
                                       av_dst_col=h * 1024 + i * 128)
                    if i == n_tiles - 1:
                        state[("kvdone", h % 2)] = res
                        if h + 2 < n_heads:
                            stream_head_kv2(h + 2)
                return go
            for h in range(n_heads):
                slot = h % 2
                kh = kh0 if slot == 0 else kh1
                vh = vh0 if slot == 0 else vh1
                for i in range(n_tiles):
                    u = h * 8 + i
                    ptm = ptm0 if i % 2 == 0 else ptm1
                    q_lhsT = qa[:, h * 1024 + i * 128: h * 1024 + (i + 1) * 128]
                    w("tensor", "ac", q_ac)
                    w("tensor", "po", kv_po[h])
                    weave = pend_weave
                    if pend_av is not None:
                        weave = weave + [pend_av]
                        pend_av = None
                    unit = attn_unit(h, i, u, q_lhsT, kh, KVLEN, ptm, masks=(),
                                     slot=i % 2, weave=weave)
                    pend_weave = make_T_groups(h, i, u, unit, ptm, KVLEN, slot=i % 2)
                    if i % 2 == 1:
                        pend_av = mk_av_a(h, i, vh)
            for f in pend_weave:
                f()
            if pend_av is not None:
                pend_av()
            pend_weave, pend_av = [], None

            ctx_h.__exit__(None, None, None)
            av_done = C["dv"]
            mv = None
            for q in range(4):
                w("vector", "dv", av_done)
                mv = do("vector", "dv", lambda e, q=q: e.tensor_copy(
                    out=av_m[:, q * 4096:(q + 1) * 4096], in_=av[:, q * 4096:(q + 1) * 4096]))
            av_mv = mv
            w("sync", "dv", av_mv)
            wo_po = stream_w_resident(dWOA)

            for tt in range(n_tiles):
                pb = 4 * (tt % 2)
                w("tensor", "dv", state.get(("opev", tt % 2), 0))
                w("tensor", "dv", av_mv)
                wait_ready("tensor", wo_po)
                for hc in range(n_heads):
                    for j in range(4):
                        pe_v = do("tensor", "pe", lambda e, hc=hc, j=j, tt=tt, pb=pb: e.matmul(
                            out=bank(pb, 4)[:, j * 512:(j + 1) * 512],
                            lhsT=av_m[:, hc * 1024 + tt * 128: hc * 1024 + (tt + 1) * 128],
                            rhs=w16_tile(hc)[:, j * 512:(j + 1) * 512],
                            start=(hc == 0), stop=(hc == n_heads - 1)))
                w("vector", "pe", pe_v)
                ev = do("vector", "dv", lambda e, tt=tt, pb=pb: e.tensor_add(
                    out=x_sb[:, tt * D:(tt + 1) * D],
                    in0=x_sb[:, tt * D:(tt + 1) * D], in1=bank(pb, 4)))
                state[("opev", tt % 2)] = ev
                state[("x_guard", tt)] = ev
            state["sub_done_dv"] = C["dv"]
            state["sub_done_pe"] = C["pe"]
            state["sub_done_po"] = C["po"]

        if 2 in sublayers:
            with scope("ffn1"):
                ffn_sublayer(x_ready=state.get("sub_done_dv", 0),
                             arena_pe=state.get("sub_done_pe", 0),
                             scratch_po=state.get("sub_done_po", 0))
            state["sub_done_dv"] = C["dv"]
            state["sub_done_pe"] = C["pe"]
            state["sub_done_po"] = C["po"]

        if 3 in sublayers:
            sd, sp, spo = (state.get("sub_done_dv", 0), state.get("sub_done_pe", 0),
                           state.get("sub_done_po", 0))
            ln3_dv, _ = layer_norm(double_ln=True, affine=True, x_ready=sd,
                                   arena_pe=sp, scratch_po=spo)
            wk_po = stream_w_resident(dWKB, gate_sem="pe", gate_val=sp,
                                      cast_engs=("gpsimd", "gpsimd",
                                                 "vector", "scalar"))
            k_ac, _ = projection(kb_own, bk_c, ln3_dv, wk_po)
            kproj_pe = C["pe"]
            for oc in range(DC):
                w("gpsimd", "ac", k_ac)
                do("gpsimd", "qC", lambda e, oc=oc: e.dma_start(
                    out=dKSEND[oc * 128:(oc + 1) * 128, :],
                    in_=kb_own[:, oc * 1024:(oc + 1) * 1024]), 16)
            ksend_q = C["qC"]
            w("gpsimd", "qC", ksend_q)
            if cfg.get("self_pair"):
                do("gpsimd", "cc0", lambda e: e.dma_start(
                    out=dKRECV[0:D, :], in_=dKSEND), 16)
                do("gpsimd", "cc0", lambda e: e.dma_start(
                    out=dKRECV[D:2 * D, :], in_=dKSEND), 16)
                cc0_done = 32
            else:
                do("gpsimd", "cc0", lambda e: e.collective_compute(
                    "AllGather", ALU.bypass,
                    replica_groups=[[0, 1], [2, 3], [4, 5], [6, 7]],
                    ins=[dKSEND.opt()], outs=[dKRECV.opt()]), 1)
                cc0_done = 1
            wv_po = stream_w_resident(dWVB, cast_wait=("pe", kproj_pe))
            w("scalar", "qC", ksend_q)
            v_ac, _ = projection(kb_own, bv_c, ln3_dv, wv_po, extra_pe_wait=kproj_pe)
            vproj_pe = C["pe"]
            vstg = scr[:, 0:4096].bitcast(BF16)
            for i in range(TI):
                for g in range(4):
                    tb = state["tb"]
                    state["tb"] ^= 1
                    tbank = 4 + tb
                    w("tensor", "dv", state.get(("tev", tb), 0))
                    w("tensor", "ac", v_ac)
                    for jj in range(4):
                        oc = g * 4 + jj
                        pe_v = do("tensor", "pe", lambda e, oc=oc, jj=jj, i=i, tbank=tbank: e.transpose(
                            out=bank_bf(tbank)[:, jj * 128:(jj + 1) * 128],
                            in_=kb_own[:, oc * 1024 + i * 128: oc * 1024 + (i + 1) * 128],
                            identity=ident))
                    w("vector", "pe", pe_v)
                    if i == 0 and g == 0:
                        w("vector", "po", C["po"])
                        w("vector", "qC", ksend_q)
                    w("vector", "qC", state.get("vstg_q", 0))
                    ev = do("vector", "dv", lambda e, g=g, tbank=tbank: e.tensor_copy(
                        out=vstg[:, g * 512:(g + 1) * 512],
                        in_=bank_bf(tbank)[:, 0:512]))
                    state[("tev", tb)] = ev
                w("gpsimd", "dv", C["dv"])
                qv = do("gpsimd", "qC", lambda e, i=i: e.dma_start(
                    out=dVSEND[i * 128:(i + 1) * 128, :], in_=vstg[:]), 16)
                state["vstg_q"] = qv
            vsend_q = C["qC"]
            w("gpsimd", "qC", vsend_q)
            if cfg.get("self_pair"):
                do("gpsimd", "cc1", lambda e: e.dma_start(
                    out=dVRECV[0:1024, :], in_=dVSEND), 16)
                do("gpsimd", "cc1", lambda e: e.dma_start(
                    out=dVRECV[1024:2048, :], in_=dVSEND), 16)
                cc1_done = 32
            else:
                do("gpsimd", "cc1", lambda e: e.collective_compute(
                    "AllGather", ALU.bypass,
                    replica_groups=[[0, 1], [2, 3], [4, 5], [6, 7]],
                    ins=[dVSEND.opt()], outs=[dVRECV.opt()]), 1)
                cc1_done = 1
            wq_po2 = stream_w_resident(dWQB, gate_sem="qC", gate_val=vsend_q,
                                       cast_wait=("pe", vproj_pe))
            w("scalar", "qC", vsend_q)
            w("scalar", "dv", bqs_dv)
            q_ac2, q_dv2 = projection(qa, bqs_c, ln3_dv, wq_po2, extra_pe_wait=kproj_pe,
                                      scale_imm=SCALE)
            qproj_pe = C["pe"]
            state[("ptm_pe", 0)] = qproj_pe
            state[("ptm_pe", 1)] = qproj_pe
            state["pt_free_pe"] = qproj_pe
            kb_h = [u8_bf(ar4, 0, 2048), u8_bf(ar4, 4096, 2048)]
            vb_h = [u8_bf(ar4, 8192, 2048), u8_bf(ar4, 16384, 2048)]
            kv_q = {}

            def assemble_head(h):
                slot = h % 2
                qsem = "qA%d" % slot
                w("sync", "cc0", cc0_done)
                w("sync", "cc1", cc1_done)
                w("sync", "pe", qproj_pe)
                w("sync", "dv", state.get(("kvdone_b", slot), 0))
                for j in range(16):
                    r, it = j % 2, j // 2
                    do("sync", qsem, lambda e, h=h, j=j, r=r, it=it, slot=slot: e.dma_start(
                        out=kb_h[slot][:, j * 128:(j + 1) * 128],
                        in_=dKRECV[r * D + h * 128: r * D + (h + 1) * 128,
                                   it * 128:(it + 1) * 128]), 16)
                for j in range(16):
                    r, it = j % 2, j // 2
                    do("sync", qsem, lambda e, h=h, j=j, r=r, it=it, slot=slot: e.dma_start(
                        out=vb_h[slot][:, j * 128:(j + 1) * 128],
                        in_=dVRECV[r * 1024 + it * 128: r * 1024 + (it + 1) * 128,
                                   h * 128:(h + 1) * 128]), 16)
                kv_q[h] = C[qsem]

            assemble_head(0)
            if n_heads > 1:
                assemble_head(1)
            ctx_h3 = scope("s3_heads")
            ctx_h3.__enter__()
            pend_weave = []
            pend_av = None

            def mk_av_b(h, i, slot):
                def go(h=h, i=i, slot=slot):
                    res = emit_av_pair(h, i, vb_h[slot], (2 * i + 2) * 128,
                                       (2 * i) * 128,
                                       av_dst_col=h * 1024 + i * 128)
                    if i == n_tiles - 1:
                        state[("kvdone_b", h % 2)] = res
                        if h + 2 < n_heads:
                            assemble_head(h + 2)
                return go
            for h in range(n_heads):
                slot = h % 2
                for i in range(n_tiles):
                    u = h * 8 + i
                    ptm = ptm0 if i % 2 == 0 else ptm1
                    nk_pad = (2 * i + 2) * 128
                    q_lhsT = qa[:, h * 1024 + i * 128: h * 1024 + (i + 1) * 128]
                    w("tensor", "ac", q_ac2)
                    w("tensor", "dv", q_dv2)
                    w("tensor", "qA%d" % slot, kv_q[h])
                    masks = ((2 * i, maskA), (2 * i + 1, maskB))
                    weave = pend_weave
                    if pend_av is not None:
                        weave = weave + [pend_av]
                        pend_av = None
                    unit = attn_unit(h, i, u, q_lhsT, kb_h[slot], nk_pad, ptm,
                                     masks=masks, slot=i % 2, weave=weave)
                    pend_weave = make_T_groups(h, i, u, unit, ptm, nk_pad,
                                               slot=i % 2)
                    if i % 2 == 1:
                        pend_av = mk_av_b(h, i, slot)
            for f in pend_weave:
                f()
            if pend_av is not None:
                pend_av()
            pend_weave, pend_av = [], None
            ctx_h3.__exit__(None, None, None)
            av_done = C["dv"]
            mv = None
            for qq in range(4):
                w("vector", "dv", av_done)
                mv = do("vector", "dv", lambda e, qq=qq: e.tensor_copy(
                    out=av_m[:, qq * 4096:(qq + 1) * 4096],
                    in_=av[:, qq * 4096:(qq + 1) * 4096]))
            av_mv = mv
            w("sync", "dv", av_mv)
            wo_po2 = stream_w_resident(dWOB)
            bo_row2 = u8_bf(ar2, 0, 2048)
            bo32 = u8_f32(ar2, 8192, 2048)
            headloop_pe = C["pe"]
            w("sync", "pe", headloop_pe)
            bo_q = do("sync", "qX", lambda e: e.dma_start(out=bo32[0:1, :], in_=dBO), 16)
            w("gpsimd", "qX", bo_q)
            w("gpsimd", "pe", headloop_pe)
            bo_po = do("gpsimd", "po", lambda e: e.tensor_copy(
                out=bo_row2[0:1, :], in_=bo32[0:1, :]))
            for tt in range(n_tiles):
                pb = 4 * (tt % 2)
                w("tensor", "dv", state.get(("opev", tt % 2), 0))
                w("tensor", "dv", av_mv)
                wait_ready("tensor", wo_po2)
                for hc in range(n_heads):
                    for j in range(4):
                        pe_v = do("tensor", "pe", lambda e, hc=hc, j=j, tt=tt, pb=pb: e.matmul(
                            out=bank(pb + j),
                            lhsT=av_m[:, hc * 1024 + tt * 128: hc * 1024 + (tt + 1) * 128],
                            rhs=w16_tile(hc)[:, j * 512:(j + 1) * 512],
                            start=(hc == 0), stop=False))
                w("tensor", "po", bo_po)
                w("tensor", "qX", C["qX"])
                for j in range(4):
                    pe_v = do("tensor", "pe", lambda e, j=j, pb=pb: e.matmul(
                        out=bank(pb + j), lhsT=ones1[0:1, :],
                        rhs=bo_row2[0:1, j * 512:(j + 1) * 512],
                        start=False, stop=True))
                w("vector", "pe", pe_v)
                w("vector", "dv", state.get(("x_guard", tt), 0))
                ev = do("vector", "dv", lambda e, tt=tt, pb=pb: e.tensor_add(
                    out=x_sb[:, tt * D:(tt + 1) * D],
                    in0=x_sb[:, tt * D:(tt + 1) * D], in1=bank(pb, 4)))
                state[("opev", tt % 2)] = ev
                state[("x_guard", tt)] = ev
            state["sub_done_dv"] = C["dv"]
            state["sub_done_pe"] = C["pe"]
            state["sub_done_po"] = C["po"]

        if 4 in sublayers:
            with scope("ffn2"):
                ffn_sublayer(x_ready=state.get("sub_done_dv", 0),
                             arena_pe=state.get("sub_done_pe", 0),
                             scratch_po=state.get("sub_done_po", 0))
            state["sub_done_dv"] = C["dv"]
            state["sub_done_pe"] = C["pe"]
            state["sub_done_po"] = C["po"]

        xfin = C["dv"]
        for i in range(TI):
            w("sync", "dv", xfin)
            do("sync", "qY", lambda e, i=i: e.dma_start(
                out=dY[i * 128:(i + 1) * 128, :], in_=x_sb[:, i * D:(i + 1) * D]), 16)
        w("sync", "qY", C["qY"])

    def mk(name):
        def f(eng):
            emit(name, eng)
        return f

    block.sync(mk("sync"))
    block.tensor(mk("tensor"))
    block.vector(mk("vector"))
    block.scalar(mk("scalar"))
    block.gpsimd(mk("gpsimd"))

    stack.close()
    return nc


def prep_inputs(inputs, pc_list=None):
    q = np.asarray(inputs["L0_attn_a_query"], dtype=np.float32)
    kvk = np.asarray(inputs["L0_attn_a_kv_k"], dtype=np.float32).reshape(KVLEN, D)
    kvv = np.asarray(inputs["L0_attn_a_kv_v"], dtype=np.float32).reshape(KVLEN, D)
    kvkT = np.ascontiguousarray(kvk.T)
    kvv_b = np.ascontiguousarray(
        kvv.reshape(32, 128, 16, 128).transpose(2, 1, 0, 3).reshape(NH * 128, KVLEN))
    wqaT = np.ascontiguousarray(np.asarray(inputs["L0_attn_a_Wq"], np.float32).T)
    woaT = np.ascontiguousarray(np.asarray(inputs["L0_attn_a_Wo"], np.float32).T)
    wqb = np.ascontiguousarray(np.asarray(inputs["L0_attn_b_Wq"], np.float32))
    wkb = np.ascontiguousarray(np.asarray(inputs["L0_attn_b_Wk"], np.float32))
    wvb = np.ascontiguousarray(np.asarray(inputs["L0_attn_b_Wv"], np.float32))
    wob = np.ascontiguousarray(np.asarray(inputs["L0_attn_b_Wo"], np.float32))
    bq = np.ascontiguousarray(np.asarray(inputs["L0_attn_b_bq"], np.float32).reshape(16, 128).T)
    bk = np.ascontiguousarray(np.asarray(inputs["L0_attn_b_bk"], np.float32).reshape(16, 128).T)
    bv = np.ascontiguousarray(np.asarray(inputs["L0_attn_b_bv"], np.float32).reshape(16, 128).T)
    bo = np.asarray(inputs["L0_attn_b_bo"], np.float32).reshape(1, D)
    lng = np.ascontiguousarray(np.asarray(inputs["L0_attn_b_ln_g"], np.float32).reshape(16, 128).T)
    lnb = np.ascontiguousarray(np.asarray(inputs["L0_attn_b_ln_b"], np.float32).reshape(16, 128).T)
    wg = np.asarray(inputs["L0_ffn_w_gate"], np.float32)
    wu = np.asarray(inputs["L0_ffn_w_up"], np.float32)
    wd = np.asarray(inputs["L0_ffn_w_down"], np.float32)
    wgT_r = np.ascontiguousarray(
        wg.reshape(64, 128, 16, 128).transpose(0, 3, 2, 1).reshape(FH, D))
    wuT_r = np.ascontiguousarray(
        wu.reshape(64, 128, 16, 128).transpose(0, 3, 2, 1).reshape(FH, D))
    wdT_r = np.ascontiguousarray(wd.T)
    ident = np.eye(128, dtype=BF)
    ones1 = np.ones((1, 128), dtype=BF)
    tril = np.where(np.tril(np.ones((128, 128), bool)), 0.0, NEG).astype(np.float32)
    full = np.full((128, 128), NEG, dtype=np.float32)
    zero = np.zeros((128, 128), dtype=np.float32)

    in_maps = []
    for c in range(8):
        b, pc = c // 2, c % 2
        x0 = np.ascontiguousarray(
            q[b].reshape(16, 128, D)[pc::2].reshape(TI * 128, D))
        maskA = tril if pc == 0 else zero
        maskB = full if pc == 0 else tril
        in_maps.append(dict(
            x0=x0, kvkT=kvkT, kvv_b=kvv_b, wqaT=wqaT, woaT=woaT,
            wqb=wqb, wkb=wkb, wvb=wvb, wob=wob,
            bqb=bq, bkb=bk, bvb=bv, bob=bo, lng=lng, lnb=lnb,
            wgT_r=wgT_r, wuT_r=wuT_r, wdT_r=wdT_r,
            maskA=maskA, maskB=maskB, ident=ident, ones1=ones1,
        ))
    return in_maps


def kernel(**inputs):
    cfg = default_cfg()
    cfg["sublayers"] = (1, 2, 3, 4)
    nc = build_program(cfg)
    in_maps = prep_inputs(inputs)
    res = run_bass_kernel_spmd(nc, in_maps, core_ids=list(range(8)))
    out = np.zeros((4, 2048, D), dtype=np.float32)
    for c in range(8):
        b, pc = c // 2, c % 2
        y = res.results[c]["y"].reshape(TI, 128, D)
        out[b].reshape(16, 128, D)[pc::2] = y
    return out


# version 13
# speedup vs baseline: 1.4216x; 1.2038x over previous
import numpy as np
import ml_dtypes
from contextlib import ExitStack

import concourse.bass as bass
import concourse.mybir as mybir
from concourse.bass_utils import run_bass_kernel_spmd

F32 = mybir.dt.float32
BF16 = mybir.dt.bfloat16
AX = mybir.AxisListType.X
ALU = mybir.AluOpType
AF = mybir.ActivationFunctionType
BF = ml_dtypes.bfloat16

D = 2048
TI = 8
DC = 16
NH = 16
DH = 128
KVLEN = 4096
FH = 8192
NHC = 64
SCALE = 1.0 / np.sqrt(128.0)
EPS_LN = 1e-5
NEG = -1.0e30

ENGS = ("sync", "tensor", "vector", "scalar", "gpsimd")


def default_cfg():
    return dict(
        n_heads=NH,
        n_tiles=TI,
        n_hc=NHC,
        sublayers=(1,),
        zero_arenas=False,
    )


class Em:

    def __init__(self, me, eng, sems):
        self.me = me
        self.eng = eng
        self.S = sems
        self.C = {k: 0 for k in sems}

    def w(self, owner, sem, val):
        if val > 0 and self.me == owner:
            self.eng.wait_ge(self.S[sem], val)

    def do(self, owner, sem, f, n=1):
        if sem is not None:
            self.C[sem] += n
        if self.me == owner:
            inst = f(self.eng)
            if sem is not None:
                inst.then_inc(self.S[sem], n)
        return self.C[sem] if sem is not None else None

    def do0(self, owner, f):
        if self.me == owner:
            f(self.eng)


def build_program(cfg):
    nc = bass.Bass("TRN2", target_bir_lowering=False, debug=False)

    dX = nc.dram_tensor("x0", [TI * 128, D], F32, kind="ExternalInput").ap()
    dKT = nc.dram_tensor("kvkT", [D, KVLEN], F32, kind="ExternalInput").ap()
    dVB = nc.dram_tensor("kvv_b", [NH * 128, KVLEN], F32, kind="ExternalInput").ap()
    dWQA = nc.dram_tensor("wqaT", [D, D], F32, kind="ExternalInput").ap()
    dWOA = nc.dram_tensor("woaT", [D, D], F32, kind="ExternalInput").ap()
    dWQB = nc.dram_tensor("wqb", [D, D], F32, kind="ExternalInput").ap()
    dWKB = nc.dram_tensor("wkb", [D, D], F32, kind="ExternalInput").ap()
    dWVB = nc.dram_tensor("wvb", [D, D], F32, kind="ExternalInput").ap()
    dWOB = nc.dram_tensor("wob", [D, D], F32, kind="ExternalInput").ap()
    dBQ = nc.dram_tensor("bqb", [128, 16], F32, kind="ExternalInput").ap()
    dBK = nc.dram_tensor("bkb", [128, 16], F32, kind="ExternalInput").ap()
    dBV = nc.dram_tensor("bvb", [128, 16], F32, kind="ExternalInput").ap()
    dBO = nc.dram_tensor("bob", [1, D], F32, kind="ExternalInput").ap()
    dLNG = nc.dram_tensor("lng", [128, 16], F32, kind="ExternalInput").ap()
    dLNB = nc.dram_tensor("lnb", [128, 16], F32, kind="ExternalInput").ap()
    dWG = nc.dram_tensor("wgT_r", [FH, D], F32, kind="ExternalInput").ap()
    dWU = nc.dram_tensor("wuT_r", [FH, D], F32, kind="ExternalInput").ap()
    dWD = nc.dram_tensor("wdT_r", [FH, D], F32, kind="ExternalInput").ap()
    dM = [nc.dram_tensor("mulm%d" % t, [128, 256], BF16, kind="ExternalInput").ap()
          for t in range(4)]
    dONESM = nc.dram_tensor("onesM", [128, 128], BF16, kind="ExternalInput").ap()
    dID = nc.dram_tensor("ident", [128, 128], BF16, kind="ExternalInput").ap()
    dONES = nc.dram_tensor("ones1", [1, 128], BF16, kind="ExternalInput").ap()
    dY = nc.dram_tensor("y", [TI * 128, D], F32, kind="ExternalOutput").ap()

    dKSEND = nc.dram_tensor("ksend", [D, 1024], BF16).ap()
    dKRECV = nc.dram_tensor("krecv", [2 * D, 1024], BF16).ap()
    dVSEND = nc.dram_tensor("vsend", [1024, D], BF16).ap()
    dVRECV = nc.dram_tensor("vrecv", [2 * 1024, D], BF16).ap()
    dWG16 = nc.dram_tensor("wg16", [FH, D], BF16).ap()
    dWU16 = nc.dram_tensor("wu16", [FH, D], BF16).ap()
    dWD16 = nc.dram_tensor("wd16", [FH, D], BF16).ap()

    stack = ExitStack()
    e = stack.enter_context

    x_sb = e(nc.sbuf_tensor("x_sb", [128, TI * D], F32))
    ar1 = e(nc.sbuf_tensor("ar1", [128, 32768], mybir.dt.uint8))
    ar2 = e(nc.sbuf_tensor("ar2", [128, 32768], mybir.dt.uint8))
    ar3 = e(nc.sbuf_tensor("ar3", [128, 32768], mybir.dt.uint8))
    ar4 = e(nc.sbuf_tensor("ar4", [128, 32768], mybir.dt.uint8))
    scr = e(nc.sbuf_tensor("scr", [128, 10240], mybir.dt.uint8))
    cst = e(nc.sbuf_tensor("cst", [128, 2048], mybir.dt.uint8))
    sta = e(nc.sbuf_tensor("sta", [128, 2048], mybir.dt.uint8))

    def u8_bf(t, off, n):
        return t[:, off:off + 2 * n].bitcast(BF16)

    def u8_f32(t, off, n):
        return t[:, off:off + 4 * n].bitcast(F32)

    lnT = u8_bf(ar1, 0, 16384)
    pt = u8_bf(ar1, 0, 8192)
    ptb0 = u8_bf(ar1, 0, 8192)
    ptb1 = u8_bf(ar1, 16384, 8192)
    av_m = u8_bf(ar1, 0, 16384)
    bo_row = u8_bf(ar1, 32768 - 4096, 2048)

    qa = u8_bf(ar2, 0, 16384)
    gated = u8_bf(ar2, 0, 8192)
    sg0 = u8_bf(ar2, 16384, 1024)
    sg1 = u8_bf(ar2, 18432, 1024)
    ffw0 = u8_f32(ar2, 20480, 1024)
    ffw1 = u8_f32(ar2, 24576, 1024)
    kb_own = u8_bf(ar2, 0, 16384)

    av = u8_bf(ar3, 0, 16384)
    wd16 = u8_bf(ar3, 0, 16384)
    w16a = u8_bf(ar3, 0, 16384)

    w16b = u8_bf(ar4, 0, 16384)
    kh0 = u8_bf(ar4, 0, 4096)
    kh1 = u8_bf(ar4, 8192, 4096)
    vh0 = u8_bf(ar4, 16384, 4096)
    vh1 = u8_bf(ar4, 24576, 4096)
    wg16s = [u8_bf(ar4, 0, 2048), u8_bf(ar4, 4096, 2048)]
    wu16s = [u8_bf(ar4, 8192, 2048), u8_bf(ar4, 12288, 2048)]
    ffw2 = u8_f32(ar4, 16384, 1024)
    ffw3 = u8_f32(ar4, 20480, 1024)
    wo16s = [u8_bf(ar4, 0, 2048), u8_bf(ar4, 4096, 2048)]

    wst0 = u8_f32(scr, 0, 1024)
    wst1 = u8_f32(scr, 4096, 1024)
    ln_tm0 = u8_bf(ar2, 0, 2048)
    ln_tm1 = u8_bf(ar2, 4096, 2048)
    sqscr = u8_f32(scr, 8192, 512)

    m_c0 = u8_bf(cst, 0, 256)
    m_c1 = u8_bf(cst, 512, 256)
    ident = u8_bf(cst, 1024, 128)
    ones1 = u8_bf(cst, 1280, 128)
    bq_c = u8_f32(cst, 1536, 16)
    bk_c = u8_f32(cst, 1600, 16)
    bv_c = u8_f32(cst, 1664, 16)
    lng_c = u8_f32(cst, 1728, 16)
    lnb_c = u8_f32(cst, 1792, 16)
    eps_c = u8_f32(cst, 1856, 1)
    bqs_c = u8_f32(sta, 1680, 16)

    ssum = u8_f32(sta, 0, 16)
    sqac = u8_f32(sta, 64, 16)
    mu_s = u8_f32(sta, 128, 16)
    var_s = u8_f32(sta, 192, 16)
    rstd_s = u8_f32(sta, 256, 16)
    t0_s = u8_f32(sta, 320, 16)
    t1_s = u8_f32(sta, 384, 16)
    m_c2 = u8_bf(sta, 448, 256)
    m_c3 = u8_bf(sta, 960, 256)
    onesM = u8_bf(sta, 1744, 128)
    rrep = [u8_f32(scr, 8192, 256),
            u8_f32(scr, 9216, 256)]

    ps = e(nc.psum_tensor("ps", [128, 4096], F32))

    def bank(k, n=1):
        return ps[:, 512 * k:512 * (k + n)]

    def bank_bf(k, n=1):
        return ps[:, 512 * k:512 * (k + n)].bitcast(BF16)

    sems = {}
    for name in ("qX", "qS0", "qS1", "qA0", "qA1", "qF0", "qF1", "qF2", "qF3",
                 "qW", "qC", "qK", "qY", "cc0", "cc1", "pe", "dv", "ac", "po"):
        sems[name] = e(nc.semaphore(name))

    block = e(nc.Block())

    n_heads = cfg["n_heads"]
    n_tiles = cfg["n_tiles"]
    n_hc = cfg["n_hc"]
    sublayers = cfg["sublayers"]

    def emit(me, eng):
        E = Em(me, eng, sems)
        w, do, do0, C = E.w, E.do, E.do0, E.C
        def scope(name):
            return nc.named_scope(name)

        x_loaded = []
        do("sync", "qX", lambda e: e.dma_start(
            out=x_sb[:, 0:D], in_=dX[0:128, :]), 16)
        x_loaded.append(("qX", C["qX"]))
        for i in range(1, TI):
            do("sync", "qY", lambda e, i=i: e.dma_start(
                out=x_sb[:, i * D:(i + 1) * D], in_=dX[i * 128:(i + 1) * 128, :]), 16)
            x_loaded.append(("qY", C["qY"]))
        for (dst, src) in ((m_c0, dM[0]), (m_c1, dM[1]), (m_c2, dM[2]),
                           (m_c3, dM[3]), (onesM, dONESM), (ident, dID),
                           (bq_c, dBQ), (bk_c, dBK), (bv_c, dBV),
                           (lng_c, dLNG), (lnb_c, dLNB)):
            do("sync", "qK", lambda e, d=dst, s=src: e.dma_start(out=d, in_=s), 16)
        do("sync", "qK", lambda e: e.dma_start(out=ones1[0:1, :], in_=dONES), 16)
        consts_ok = C["qK"]
        if cfg.get("zero_arenas"):
            for t in (ar1, ar2, ar3, ar4):
                do("vector", "dv", lambda e, t=t: e.memset(t[:, :], 0))
        zero_dv = C["dv"]
        eps_dv = do("vector", "dv", lambda e: e.memset(eps_c, EPS_LN))
        w("vector", "qK", consts_ok)
        bqs_dv = do("vector", "dv", lambda e: e.tensor_scalar_mul(
            out=bqs_c[:], in0=bq_c[:], scalar1=SCALE))

        state = dict(tb=0, avb=0)

        def layer_norm(double_ln=False, affine=False, x_ready=0, arena_pe=0,
                       scratch_po=0):
            for i in range(n_tiles):
                sl = i % 2
                ln_tm = ln_tm0 if sl == 0 else ln_tm1
                xi = x_sb[:, i * D:(i + 1) * D]
                w("vector", x_loaded[i][0], x_loaded[i][1])
                w("vector", "dv", x_ready)
                v_sum = do("vector", "dv", lambda e, xi=xi, i=i: e.reduce_sum(
                    out=ssum[:, i:i + 1], in_=xi, axis=AX))
                w("scalar", x_loaded[i][0], x_loaded[i][1])
                w("scalar", "dv", x_ready)
                a_sq0 = do("scalar", "ac", lambda e, xi=xi, i=i: e.activation(
                    out=sqscr[:], in_=xi[:, 0:512], func=AF.Square,
                    accum_out=sqac[:, 2 * (i % 8):2 * (i % 8) + 1]))
                w("scalar", "ac", a_sq0)
                a_sq1 = do("scalar", "ac", lambda e, xi=xi, i=i: e.activation(
                    out=sqscr[:], in_=xi[:, 512:1024], func=AF.Square,
                    accum_out=t0_s[:, i:i + 1]))
                w("scalar", "ac", a_sq1)
                a_sq2 = do("scalar", "ac", lambda e, xi=xi, i=i: e.activation(
                    out=sqscr[:], in_=xi[:, 1024:1536], func=AF.Square,
                    accum_out=t1_s[:, i:i + 1]))
                w("scalar", "ac", a_sq2)
                a_sq3 = do("scalar", "ac", lambda e, xi=xi, i=i: e.activation(
                    out=sqscr[:], in_=xi[:, 1536:2048], func=AF.Square,
                    accum_out=sqac[:, 2 * (i % 8) + 1:2 * (i % 8) + 2]))

                def stats_chain(i, v_sum, a_sq3, pfx_mu, pfx_var, pfx_rstd):
                    w("vector", "dv", v_sum)
                    v_mu = do("vector", "dv", lambda e, i=i: e.tensor_scalar_mul(
                        out=mu_s[:, i:i + 1], in0=ssum[:, i:i + 1], scalar1=1.0 / D))
                    w("vector", "ac", a_sq3)
                    v_s2 = do("vector", "dv", lambda e, i=i: e.tensor_add(
                        out=sqac[:, 2 * (i % 8):2 * (i % 8) + 1],
                        in0=sqac[:, 2 * (i % 8):2 * (i % 8) + 1],
                        in1=t0_s[:, i:i + 1]))
                    w("vector", "dv", v_s2)
                    v_s2b = do("vector", "dv", lambda e, i=i: e.tensor_add(
                        out=t1_s[:, i:i + 1], in0=t1_s[:, i:i + 1],
                        in1=sqac[:, 2 * (i % 8) + 1:2 * (i % 8) + 2]))
                    w("vector", "dv", v_s2b)
                    v_s2c = do("vector", "dv", lambda e, i=i: e.tensor_add(
                        out=sqac[:, 2 * (i % 8):2 * (i % 8) + 1],
                        in0=sqac[:, 2 * (i % 8):2 * (i % 8) + 1],
                        in1=t1_s[:, i:i + 1]))
                    w("vector", "dv", max(v_mu, v_s2c))
                    v_mu2 = do("vector", "dv", lambda e, i=i: e.tensor_mul(
                        out=t0_s[:, i:i + 1], in0=mu_s[:, i:i + 1], in1=mu_s[:, i:i + 1]))
                    w("vector", "dv", v_mu2)
                    v_var = do("vector", "dv", lambda e, i=i: e.scalar_tensor_tensor(
                        out=var_s[:, i:i + 1], in0=sqac[:, 2 * (i % 8):2 * (i % 8) + 1],
                        scalar=1.0 / D, in1=t0_s[:, i:i + 1],
                        op0=ALU.mult, op1=ALU.subtract))
                    w("scalar", "dv", v_var)
                    w("scalar", "dv", eps_dv)
                    a_ln = do("scalar", "ac", lambda e, i=i: e.activation(
                        out=t1_s[:, i:i + 1], in_=var_s[:, i:i + 1], func=AF.Ln,
                        bias=eps_c[:, 0:1]))
                    w("scalar", "ac", a_ln)
                    a_rs = do("scalar", "ac", lambda e, i=i: e.activation(
                        out=rstd_s[:, i:i + 1], in_=t1_s[:, i:i + 1], func=AF.Exp,
                        scale=-0.5))
                    return a_rs

                a_rs = stats_chain(i, v_sum, a_sq3, mu_s, var_s, rstd_s)

                if not double_ln:
                    w("vector", "ac", a_rs)
                    if i < 2:
                        w("vector", "pe", arena_pe)
                    v_ln = do("vector", "dv", lambda e, xi=xi, i=i, lt=ln_tm: e.tensor_scalar(
                        out=lt[:], in0=xi, scalar1=mu_s[:, i:i + 1],
                        scalar2=rstd_s[:, i:i + 1], op0=ALU.subtract, op1=ALU.mult))
                else:
                    y32 = ar2[:, 8192:16384].bitcast(F32)
                    w("vector", "ac", a_rs)
                    if i < 2:
                        w("vector", "pe", arena_pe)
                    v_y = do("vector", "dv", lambda e, xi=xi, i=i, y=y32: e.tensor_scalar(
                        out=y[:], in0=xi, scalar1=mu_s[:, i:i + 1],
                        scalar2=rstd_s[:, i:i + 1], op0=ALU.subtract, op1=ALU.mult))
                    w("vector", "dv", v_y)
                    v_sum2 = do("vector", "dv", lambda e, y=y32, i=i: e.reduce_sum(
                        out=ssum[:, 8 + (i % 8):9 + (i % 8)], in_=y, axis=AX))
                    w("scalar", "dv", v_y)
                    a_q0 = do("scalar", "ac", lambda e, y=y32, i=i: e.activation(
                        out=sqscr[:], in_=y[:, 0:512], func=AF.Square,
                        accum_out=sqac[:, 2 * (i % 8):2 * (i % 8) + 1]))
                    w("scalar", "ac", a_q0)
                    a_q1 = do("scalar", "ac", lambda e, y=y32, i=i: e.activation(
                        out=sqscr[:], in_=y[:, 512:1024], func=AF.Square,
                        accum_out=t0_s[:, i:i + 1]))
                    w("scalar", "ac", a_q1)
                    a_q2 = do("scalar", "ac", lambda e, y=y32, i=i: e.activation(
                        out=sqscr[:], in_=y[:, 1024:1536], func=AF.Square,
                        accum_out=t1_s[:, i:i + 1]))
                    w("scalar", "ac", a_q2)
                    a_q3 = do("scalar", "ac", lambda e, y=y32, i=i: e.activation(
                        out=sqscr[:], in_=y[:, 1536:2048], func=AF.Square,
                        accum_out=sqac[:, 2 * (i % 8) + 1:2 * (i % 8) + 2]))
                    w("vector", "dv", v_sum2)
                    v_mu = do("vector", "dv", lambda e, i=i: e.tensor_scalar_mul(
                        out=mu_s[:, 8 + (i % 8):9 + (i % 8)],
                        in0=ssum[:, 8 + (i % 8):9 + (i % 8)], scalar1=1.0 / D))
                    w("vector", "ac", a_q3)
                    v_a = do("vector", "dv", lambda e, i=i: e.tensor_add(
                        out=sqac[:, 2 * (i % 8):2 * (i % 8) + 1],
                        in0=sqac[:, 2 * (i % 8):2 * (i % 8) + 1], in1=t0_s[:, i:i + 1]))
                    w("vector", "dv", v_a)
                    v_b = do("vector", "dv", lambda e, i=i: e.tensor_add(
                        out=t1_s[:, i:i + 1], in0=t1_s[:, i:i + 1],
                        in1=sqac[:, 2 * (i % 8) + 1:2 * (i % 8) + 2]))
                    w("vector", "dv", v_b)
                    v_c = do("vector", "dv", lambda e, i=i: e.tensor_add(
                        out=sqac[:, 2 * (i % 8):2 * (i % 8) + 1],
                        in0=sqac[:, 2 * (i % 8):2 * (i % 8) + 1], in1=t1_s[:, i:i + 1]))
                    w("vector", "dv", max(v_mu, v_c))
                    v_mu2 = do("vector", "dv", lambda e, i=i: e.tensor_mul(
                        out=t0_s[:, i:i + 1], in0=mu_s[:, 8 + (i % 8):9 + (i % 8)],
                        in1=mu_s[:, 8 + (i % 8):9 + (i % 8)]))
                    w("vector", "dv", v_mu2)
                    v_var = do("vector", "dv", lambda e, i=i: e.scalar_tensor_tensor(
                        out=var_s[:, 8 + (i % 8):9 + (i % 8)],
                        in0=sqac[:, 2 * (i % 8):2 * (i % 8) + 1],
                        scalar=1.0 / D, in1=t0_s[:, i:i + 1],
                        op0=ALU.mult, op1=ALU.subtract))
                    w("scalar", "dv", v_var)
                    w("scalar", "dv", eps_dv)
                    a_l2 = do("scalar", "ac", lambda e, i=i: e.activation(
                        out=t1_s[:, i:i + 1], in_=var_s[:, 8 + (i % 8):9 + (i % 8)],
                        func=AF.Ln, bias=eps_c[:, 0:1]))
                    w("scalar", "ac", a_l2)
                    a_rs2 = do("scalar", "ac", lambda e, i=i: e.activation(
                        out=rstd_s[:, 8 + (i % 8):9 + (i % 8)], in_=t1_s[:, i:i + 1],
                        func=AF.Exp, scale=-0.5))
                    w("vector", "ac", a_rs2)
                    v_ln = do("vector", "dv", lambda e, y=y32, i=i, lt=ln_tm: e.tensor_scalar(
                        out=lt[:], in0=y, scalar1=mu_s[:, 8 + (i % 8):9 + (i % 8)],
                        scalar2=rstd_s[:, 8 + (i % 8):9 + (i % 8)],
                        op0=ALU.subtract, op1=ALU.mult))

                w("tensor", "qK", consts_ok)
                w("tensor", "dv", zero_dv)
                w("tensor", "pe", arena_pe)
                for g in range(4):
                    tb = state["tb"]
                    state["tb"] ^= 1
                    tbank = 4 + tb
                    w("tensor", "dv", state.get(("tev", tb), 0))
                    w("tensor", "dv", v_ln)
                    for j in range(4):
                        dc = g * 4 + j
                        pe_v = do("tensor", "pe", lambda e, lt=ln_tm, dc=dc, j=j, tbank=tbank: e.transpose(
                            out=bank_bf(tbank)[:, j * 128:(j + 1) * 128],
                            in_=lt[:, dc * 128:(dc + 1) * 128], identity=ident))
                    w("vector", "pe", pe_v)
                    if affine:
                        for j in range(4):
                            dc = g * 4 + j
                            ev = do("vector", "dv", lambda e, dc=dc, j=j, i=i, tbank=tbank: e.tensor_scalar(
                                out=lnT[:, dc * 1024 + i * 128: dc * 1024 + (i + 1) * 128],
                                in0=bank_bf(tbank)[:, j * 128:(j + 1) * 128],
                                scalar1=lng_c[:, dc:dc + 1], scalar2=lnb_c[:, dc:dc + 1],
                                op0=ALU.mult, op1=ALU.add))
                    else:
                        ev = do("vector", "dv", lambda e, g=g, i=i, tbank=tbank: e.tensor_copy(
                            out=lnT.rearrange("p (c n) -> p c n", n=1024)[
                                :, g * 4:(g + 1) * 4, i * 128:(i + 1) * 128],
                            in_=bank_bf(tbank).rearrange("p (c n) -> p c n", n=128)[:, 0:4, :]))
                    state[("tev", tb)] = ev
            state["scr_free_pe"] = C["pe"]
            return C["dv"], C["ac"]

        def w16_tile(dcc):
            if dcc < 8:
                return w16a[:, dcc * 2048:(dcc + 1) * 2048]
            return w16b[:, (dcc - 8) * 2048:(dcc - 8 + 1) * 2048]

        CAST_SEM = {"gpsimd": "po", "vector": "dv", "scalar": "ac"}

        def cast_to(engx, dst, srcap, waits=()):
            for (s, v) in waits:
                w(engx, s, v)
            sem = CAST_SEM[engx]
            if engx == "scalar":
                v = do(engx, sem, lambda e, dst=dst, srcap=srcap: e.copy(
                    out=dst, in_=srcap))
            else:
                v = do(engx, sem, lambda e, dst=dst, srcap=srcap: e.tensor_copy(
                    out=dst, in_=srcap))
            return (sem, v)

        STG = [(wst0, "qS0"), (wst1, "qS1")]

        def stream_w_resident(src, gate_sem=None, gate_val=0, dst_tile=None,
                              cast_engs=("vector", "scalar"), cast_wait=None,
                              ring=None):
            if dst_tile is None:
                dst_tile = w16_tile
            if ring is None:
                ring = STG
            for dcc in range(DC):
                for hhalf in range(2):
                    k = dcc * 2 + hhalf
                    p = k % len(ring)
                    st, qsem = ring[p]
                    if gate_sem is not None and k == 0:
                        w("sync", gate_sem, gate_val)
                    ssem, sval = state.get(("fst", qsem), ("po", 0))
                    w("sync", ssem, sval)
                    qv = do("sync", qsem, lambda e, st=st, dcc=dcc, hh=hhalf, src=src: e.dma_start(
                        out=st, in_=src[dcc * 128:(dcc + 1) * 128, hh * 1024:(hh + 1) * 1024]), 16)
                    engx = cast_engs[k % len(cast_engs)]
                    wl = [(qsem, qv)]
                    if cast_wait is not None:
                        wl.append(cast_wait)
                    last = cast_to(engx, dst_tile(dcc)[:, hhalf * 1024:(hhalf + 1) * 1024],
                                   st, waits=wl)
                    state[("fst", qsem)] = last
            return dict(po=C["po"], dv=C["dv"], ac=C["ac"])

        def wait_ready(owner, r):
            if isinstance(r, dict):
                w(owner, "po", r["po"])
                w(owner, "dv", r["dv"])
                w(owner, "ac", r["ac"])
            else:
                w(owner, "po", r)

        def projection(dst, bias_col, lnT_ready_dv, w_po, extra_pe_wait=0, n_oc=DC,
                       scale_imm=None):
            for oc in range(n_oc):
                for th in range(2):
                    pbk = (2 * oc + th) % 4
                    psem, pval = state.get(("pjev", pbk), ("ac", 0))
                    w("tensor", psem, pval)
                    w("tensor", "dv", lnT_ready_dv)
                    wait_ready("tensor", w_po)
                    w("tensor", "pe", extra_pe_wait)
                    for dcc in range(DC):
                        pe_v = do("tensor", "pe", lambda e, oc=oc, th=th, dcc=dcc, pbk=pbk: e.matmul(
                            out=bank(pbk),
                            lhsT=w16_tile(dcc)[:, oc * 128:(oc + 1) * 128],
                            rhs=lnT[:, dcc * 1024 + th * 512: dcc * 1024 + (th + 1) * 512],
                            start=(dcc == 0), stop=(dcc == DC - 1)))
                    if scale_imm is not None and bias_col is not None:
                        w("vector", "pe", pe_v)
                        evd = do("vector", "dv", lambda e, oc=oc, th=th, pbk=pbk, dst=dst, b=bias_col, s=scale_imm: e.tensor_scalar(
                            out=dst[:, oc * 1024 + th * 512: oc * 1024 + (th + 1) * 512],
                            in0=bank(pbk), scalar1=s, scalar2=b[:, oc:oc + 1],
                            op0=ALU.mult, op1=ALU.add))
                        state[("pjev", pbk)] = ("dv", evd)
                        continue
                    w("scalar", "pe", pe_v)
                    if scale_imm is not None:
                        ev = do("scalar", "ac", lambda e, oc=oc, th=th, pbk=pbk, dst=dst, s=scale_imm: e.activation(
                            out=dst[:, oc * 1024 + th * 512: oc * 1024 + (th + 1) * 512],
                            in_=bank(pbk), func=AF.Copy, scale=s, bias=0.0))
                    elif bias_col is None:
                        ev = do("scalar", "ac", lambda e, oc=oc, th=th, pbk=pbk, dst=dst: e.copy(
                            out=dst[:, oc * 1024 + th * 512: oc * 1024 + (th + 1) * 512],
                            in_=bank(pbk)))
                    else:
                        ev = do("scalar", "ac", lambda e, oc=oc, th=th, pbk=pbk, dst=dst, b=bias_col: e.add(
                            out=dst[:, oc * 1024 + th * 512: oc * 1024 + (th + 1) * 512],
                            in_=bank(pbk),
                            add=b[:, oc:oc + 1]))
                    state[("pjev", pbk)] = ("ac", ev)
            return (C["ac"], C["dv"])

        def emit_pair(par, kh, vh, nkc, qs, av_dst, masks, head_end=None):
            pt = ptb0 if par == 0 else ptb1
            ngr = (nkc + 1) // 2
            last_exp = 0
            first = True
            for g in range(ngr):
                bk = state["sc_rot"] % 4
                state["sc_rot"] += 1
                w("tensor", "ac", state.get(("scb", bk), 0))
                pe_v = None
                n_in = min(2, nkc - 2 * g)
                for half in range(n_in):
                    kc = 2 * g + half
                    pe_v = do("tensor", "pe", lambda e, bk=bk, half=half, kc=kc, kh=kh, qs=qs: e.matmul(
                        out=bank(bk)[:, half * 256:(half + 1) * 256],
                        lhsT=kh[:, kc * 128:(kc + 1) * 128], rhs=qs,
                        start=True, stop=True))
                w("scalar", "pe", pe_v)
                if first:
                    w("scalar", "pe", state[("ptfree", par)])
                    first = False
                a_e = do("scalar", "ac", lambda e, g=g, n_in=n_in, bk=bk, pt=pt: e.activation(
                    out=pt[:, g * 512: g * 512 + n_in * 256],
                    in_=bank(bk)[:, 0:n_in * 256], func=AF.Exp))
                state[("scb", bk)] = a_e
                last_exp = a_e
            mask_dv = 0
            for (kc, m_ap) in masks:
                w("vector", "ac", last_exp)
                mask_dv = do("vector", "dv", lambda e, kc=kc, m_ap=m_ap, pt=pt: e.tensor_mul(
                    out=pt[:, kc * 256:(kc + 1) * 256],
                    in0=pt[:, kc * 256:(kc + 1) * 256], in1=m_ap))
            exp_c, mask_c = last_exp, mask_dv

            def closure():
                w("tensor", "ac", exp_c)
                w("tensor", "dv", mask_c)
                w("tensor", "dv", state.get(("dnfree", par), 0))
                pe_d = None
                for kc in range(nkc):
                    pe_d = do("tensor", "pe", lambda e, kc=kc, par=par, pt=pt: e.matmul(
                        out=bank(6 + par)[:, 0:256], lhsT=onesM,
                        rhs=pt[:, kc * 256:(kc + 1) * 256],
                        start=(kc == 0), stop=(kc == nkc - 1)))
                w("vector", "pe", pe_d)
                v_r = do("vector", "dv", lambda e, par=par: e.reciprocal(
                    out=rrep[par], in_=bank(6 + par)[:, 0:256]))
                state[("dnfree", par)] = v_r
                w("tensor", "dv", state.get(("avfree", par), 0))
                pe_a = None
                for kc in range(nkc):
                    pe_a = do("tensor", "pe", lambda e, kc=kc, par=par, pt=pt, vh=vh: e.matmul(
                        out=bank(4 + par)[:, 0:256],
                        lhsT=vh[:, kc * 128:(kc + 1) * 128],
                        rhs=pt[:, kc * 256:(kc + 1) * 256],
                        start=(kc == 0), stop=(kc == nkc - 1)))
                state[("ptfree", par)] = C["pe"]
                w("vector", "pe", pe_a)
                w("vector", "dv", v_r)
                ev = do("vector", "dv", lambda e, par=par, av_dst=av_dst: e.tensor_mul(
                    out=av[:, av_dst:av_dst + 256], in0=bank(4 + par)[:, 0:256],
                    in1=rrep[par]))
                state[("avfree", par)] = ev
                if head_end is not None:
                    head_end(ev)
            return closure

        def attn_init(qdone_pe):
            for bk in range(4):
                psem, pval = state.get(("pjev", bk), ("ac", 0))
                w("tensor", psem, pval)
            state.setdefault("sc_rot", 0)
            state[("ptfree", 0)] = state[("ptfree", 1)] = qdone_pe
            state[("dnfree", 0)] = state[("dnfree", 1)] = C["dv"]
            state[("avfree", 0)] = state[("avfree", 1)] = C["dv"]


        def ffn_sublayer(x_ready, arena_pe, scratch_po):
            ln_dv, _ = layer_norm(x_ready=x_ready, arena_pe=arena_pe,
                                  scratch_po=scratch_po)
            state[("w16free", 0)] = arena_pe
            state[("w16free", 1)] = arena_pe
            state["ffn_down_pe"] = arena_pe
            state["ffn_dn_ev01"] = x_ready
            state["ffn_tt_dv"] = x_ready
            state[("sgfree", 0)] = x_ready
            state[("sgfree", 1)] = x_ready
            nblk = max(1, n_hc // 8)
            for blk in range(nblk):
                for k in range(8):
                    hc = blk * 8 + k
                    par = hc % 2
                    gu_marks = []
                    for (srcw, stg, stsem, dsts, cengs) in (
                        (dWG, ffw0, "qF0", lambda h=hc: wg16s[h % 2], ("gpsimd", "vector")),
                        (dWU, ffw1, "qF1", lambda h=hc: wu16s[h % 2], ("gpsimd", "scalar")),
                    ):
                        for half in range(2):
                            ssem, sval = state.get(("fst", stsem), ("po", 0))
                            w("sync", ssem, sval)
                            if blk == 0 and k == 0 and half == 0:
                                w("sync", "pe", arena_pe)
                                w("sync", "dv", ln_dv)
                            qv = do("sync", stsem, lambda e, st=stg, srcw=srcw, hc=hc, half=half: e.dma_start(
                                out=st, in_=srcw[hc * 128:(hc + 1) * 128,
                                                 half * 1024:(half + 1) * 1024]), 16)
                            engx = cengs[half]
                            mk = cast_to(engx,
                                         dsts()[:, half * 1024:(half + 1) * 1024], stg,
                                         waits=[(stsem, qv),
                                                ("pe", state.get(("w16free", par), 0))])
                            state[("fst", stsem)] = mk
                            gu_marks.append(mk)
                    wd_marks = []
                    for half in range(2):
                        stsem = "qF2" if half == 0 else "qF3"
                        stg = ffw2 if half == 0 else ffw3
                        ssem, sval = state.get(("fst", stsem), ("po", 0))
                        w("sync", ssem, sval)
                        if blk == 0 and k == 0 and half == 0:
                            w("sync", "pe", arena_pe)
                        qv = do("sync", stsem, lambda e, st=stg, hc=hc, half=half: e.dma_start(
                            out=st, in_=dWD[hc * 128:(hc + 1) * 128,
                                            half * 1024:(half + 1) * 1024]), 16)
                        engx = ("vector", "scalar")[(k + half) % 2]
                        wts = [(stsem, qv), ("pe", state.get("ffn_down_pe", 0))]
                        if blk == 0 and k == 0:
                            wts.append(("dv", x_ready))
                        mk = cast_to(engx,
                                     wd16[:, k * 2048 + half * 1024: k * 2048 + (half + 1) * 1024],
                                     stg, waits=wts)
                        state[("fst", stsem)] = mk
                        wd_marks.append(mk)
                    state[("wd_marks", k)] = list(wd_marks)
                    for (s_, v_) in gu_marks:
                        w("tensor", s_, v_)
                    w("tensor", "dv", ln_dv)
                    w("tensor", "ac", state.get("ffn_silu_ac", 0))
                    w("tensor", "dv", state.get("ffn_dn_ev01", 0))
                    for dcc in range(DC):
                        for th in range(2):
                            pe_g = do("tensor", "pe", lambda e, dcc=dcc, th=th, hc=hc: e.matmul(
                                out=bank(th), lhsT=wg16s[hc % 2][:, dcc * 128:(dcc + 1) * 128],
                                rhs=lnT[:, dcc * 1024 + th * 512: dcc * 1024 + (th + 1) * 512],
                                start=(dcc == 0), stop=(dcc == DC - 1)))
                    w("scalar", "pe", pe_g)
                    w("scalar", "dv", state.get(("sgfree", hc % 2), 0))
                    a_si = do("scalar", "ac", lambda e, hc=hc: e.activation(
                        out=(sg0 if hc % 2 == 0 else sg1)[:], in_=bank(0, 2), func=AF.Silu))
                    state["ffn_silu_ac"] = a_si
                    w("tensor", "dv", state.get("ffn_tt_dv", 0))
                    for dcc in range(DC):
                        for th in range(2):
                            pe_u = do("tensor", "pe", lambda e, dcc=dcc, th=th, hc=hc: e.matmul(
                                out=bank(2 + th), lhsT=wu16s[hc % 2][:, dcc * 128:(dcc + 1) * 128],
                                rhs=lnT[:, dcc * 1024 + th * 512: dcc * 1024 + (th + 1) * 512],
                                start=(dcc == 0), stop=(dcc == DC - 1)))
                    state[("w16free", par)] = C["pe"]
                    w("vector", "pe", pe_u)
                    w("vector", "ac", a_si)
                    w("vector", "pe", state.get("ffn_down_pe", 0))
                    v_tt = do("vector", "dv", lambda e, hc=hc, k=k: e.tensor_mul(
                        out=gated[:, k * 1024:(k + 1) * 1024], in0=bank(2, 2),
                        in1=(sg0 if hc % 2 == 0 else sg1)[:]))
                    state["ffn_tt_dv"] = v_tt
                    state[("sgfree", hc % 2)] = v_tt
                    state[("ffn_gated", k)] = v_tt
                for tt in range(n_tiles):
                    pb = 4 * (tt % 2)
                    w("tensor", "dv", state[("ffn_gated", 7)])
                    if tt == 0:
                        for kk in range(8):
                            for (s_, v_) in state[("wd_marks", kk)]:
                                w("tensor", s_, v_)
                    w("tensor", "ac", state.get("ffn_silu_ac", 0))
                    w("tensor", "dv", state.get(("ffn_dn_ev", tt % 2), 0))
                    for hcc in range(8):
                        for j in range(4):
                            pe_d = do("tensor", "pe", lambda e, hcc=hcc, j=j, tt=tt, pb=pb: e.matmul(
                                out=bank(pb + j),
                                lhsT=gated[:, hcc * 1024 + tt * 128: hcc * 1024 + (tt + 1) * 128],
                                rhs=wd16[:, hcc * 2048 + j * 512: hcc * 2048 + (j + 1) * 512],
                                start=(hcc == 0), stop=(hcc == 7)))
                    w("vector", "pe", pe_d)
                    w("vector", "dv", state.get(("x_guard", tt), 0))
                    ev = do("vector", "dv", lambda e, tt=tt, pb=pb: e.tensor_add(
                        out=x_sb[:, tt * D:(tt + 1) * D],
                        in0=x_sb[:, tt * D:(tt + 1) * D], in1=bank(pb, 4)))
                    state[("ffn_dn_ev", tt % 2)] = ev
                    state[("x_guard", tt)] = ev
                    if tt % 2 == 0:
                        state["ffn_dn_ev01"] = ev
                state["ffn_down_pe"] = C["pe"]
            return C["dv"]
        if 1 in sublayers:
            with scope("s1_ln"):
                ln_dv, ln_ac = layer_norm(double_ln=False, affine=False)

            wq_po = stream_w_resident(
                dWQA, gate_sem=("dv" if cfg.get("zero_arenas") else None),
                gate_val=zero_dv if cfg.get("zero_arenas") else 0,
                cast_engs=("gpsimd",))
            q_ac, _ = projection(qa, None, ln_dv, wq_po, n_oc=n_heads, scale_imm=SCALE)
            state["kv_gate_pe"] = C["pe"]

            kv_po = {}
            def stream_head_kv2(h):
                slot = h % 2
                kh = kh0 if slot == 0 else kh1
                vh = vh0 if slot == 0 else vh1
                for part in range(8):
                    is_v = part >= 4
                    src = dVB if is_v else dKT
                    dst = vh if is_v else kh
                    col0 = (part % 4) * 1024
                    st, qsem = STG[part % len(STG)]
                    ssem, sval = state.get(("fst", qsem), ("po", 0))
                    w("sync", ssem, sval)
                    w("sync", "dv", state.get(("kvdone", slot), 0))
                    qv = do("sync", qsem, lambda e, st=st, src=src, h=h, col0=col0: e.dma_start(
                        out=st, in_=src[h * 128:(h + 1) * 128, col0:col0 + 1024]), 16)
                    w("gpsimd", qsem, qv)
                    w("gpsimd", "pe", state.get("kv_gate_pe", 0))
                    pv = do("gpsimd", "po", lambda e, st=st, dst=dst, col0=col0: e.tensor_copy(
                        out=dst[:, col0:col0 + 1024], in_=st))
                    state[("fst", qsem)] = ("po", pv)
                kv_po[h] = C["po"]

            stream_head_kv2(0)
            if n_heads > 1:
                stream_head_kv2(1)

            ctx_h = scope("s1_heads")
            ctx_h.__enter__()
            attn_init(state["kv_gate_pe"])
            pend = None

            def mk_end_a(h, slot):
                def end(ev):
                    state[("kvdone", slot)] = ev
                    if h + 2 < n_heads:
                        stream_head_kv2(h + 2)
                return end
            pcount = 0
            for h in range(n_heads):
                slot = h % 2
                kh = kh0 if slot == 0 else kh1
                vh = vh0 if slot == 0 else vh1
                for j in range(n_tiles // 2):
                    par = pcount % 2
                    pcount += 1
                    w("tensor", "ac", q_ac)
                    w("tensor", "po", kv_po[h])
                    cl = emit_pair(
                        par, kh, vh, KVLEN // 128,
                        qs=qa[:, h * 1024 + j * 256: h * 1024 + (j + 1) * 256],
                        av_dst=h * 1024 + j * 256, masks=(),
                        head_end=(mk_end_a(h, slot)
                                  if j == n_tiles // 2 - 1 else None))
                    if pend is not None:
                        pend()
                    pend = cl
            if pend is not None:
                pend()
                pend = None

            ctx_h.__exit__(None, None, None)
            av_done = C["dv"]
            state[("tev", 0)] = state[("tev", 1)] = av_done
            mv = None
            for q in range(4):
                w("vector", "dv", av_done)
                mv = do("vector", "dv", lambda e, q=q: e.tensor_copy(
                    out=av_m[:, q * 4096:(q + 1) * 4096], in_=av[:, q * 4096:(q + 1) * 4096]))
            av_mv = mv
            w("sync", "dv", av_mv)
            wo_po = stream_w_resident(dWOA)

            for tt in range(n_tiles):
                pb = 4 * (tt % 2)
                w("tensor", "dv", state.get(("opev", tt % 2), 0))
                w("tensor", "dv", av_mv)
                wait_ready("tensor", wo_po)
                for hc in range(n_heads):
                    for j in range(4):
                        pe_v = do("tensor", "pe", lambda e, hc=hc, j=j, tt=tt, pb=pb: e.matmul(
                            out=bank(pb, 4)[:, j * 512:(j + 1) * 512],
                            lhsT=av_m[:, hc * 1024 + tt * 128: hc * 1024 + (tt + 1) * 128],
                            rhs=w16_tile(hc)[:, j * 512:(j + 1) * 512],
                            start=(hc == 0), stop=(hc == n_heads - 1)))
                w("vector", "pe", pe_v)
                ev = do("vector", "dv", lambda e, tt=tt, pb=pb: e.tensor_add(
                    out=x_sb[:, tt * D:(tt + 1) * D],
                    in0=x_sb[:, tt * D:(tt + 1) * D], in1=bank(pb, 4)))
                state[("opev", tt % 2)] = ev
                state[("x_guard", tt)] = ev
            state["sub_done_dv"] = C["dv"]
            state["sub_done_pe"] = C["pe"]
            state["sub_done_po"] = C["po"]

        if 2 in sublayers:
            with scope("ffn1"):
                ffn_sublayer(x_ready=state.get("sub_done_dv", 0),
                             arena_pe=state.get("sub_done_pe", 0),
                             scratch_po=state.get("sub_done_po", 0))
            state["sub_done_dv"] = C["dv"]
            state["sub_done_pe"] = C["pe"]
            state["sub_done_po"] = C["po"]

        if 3 in sublayers:
            sd, sp, spo = (state.get("sub_done_dv", 0), state.get("sub_done_pe", 0),
                           state.get("sub_done_po", 0))
            ln3_dv, _ = layer_norm(double_ln=True, affine=True, x_ready=sd,
                                   arena_pe=sp, scratch_po=spo)
            wk_po = stream_w_resident(dWKB, gate_sem="pe", gate_val=sp,
                                      cast_engs=("gpsimd", "gpsimd",
                                                 "vector", "scalar"))
            k_ac, _ = projection(kb_own, bk_c, ln3_dv, wk_po)
            kproj_pe = C["pe"]
            for oc in range(DC):
                w("gpsimd", "ac", k_ac)
                do("gpsimd", "qC", lambda e, oc=oc: e.dma_start(
                    out=dKSEND[oc * 128:(oc + 1) * 128, :],
                    in_=kb_own[:, oc * 1024:(oc + 1) * 1024]), 16)
            ksend_q = C["qC"]
            w("gpsimd", "qC", ksend_q)
            if cfg.get("self_pair"):
                do("gpsimd", "cc0", lambda e: e.dma_start(
                    out=dKRECV[0:D, :], in_=dKSEND), 16)
                do("gpsimd", "cc0", lambda e: e.dma_start(
                    out=dKRECV[D:2 * D, :], in_=dKSEND), 16)
                cc0_done = 32
            else:
                do("gpsimd", "cc0", lambda e: e.collective_compute(
                    "AllGather", ALU.bypass,
                    replica_groups=[[0, 1], [2, 3], [4, 5], [6, 7]],
                    ins=[dKSEND.opt()], outs=[dKRECV.opt()]), 1)
                cc0_done = 1
            wv_po = stream_w_resident(dWVB, cast_wait=("pe", kproj_pe))
            w("scalar", "qC", ksend_q)
            v_ac, _ = projection(kb_own, bv_c, ln3_dv, wv_po, extra_pe_wait=kproj_pe)
            vproj_pe = C["pe"]
            vstg = scr[:, 0:4096].bitcast(BF16)
            for i in range(TI):
                for g in range(4):
                    tb = state["tb"]
                    state["tb"] ^= 1
                    tbank = 4 + tb
                    w("tensor", "dv", state.get(("tev", tb), 0))
                    w("tensor", "ac", v_ac)
                    for jj in range(4):
                        oc = g * 4 + jj
                        pe_v = do("tensor", "pe", lambda e, oc=oc, jj=jj, i=i, tbank=tbank: e.transpose(
                            out=bank_bf(tbank)[:, jj * 128:(jj + 1) * 128],
                            in_=kb_own[:, oc * 1024 + i * 128: oc * 1024 + (i + 1) * 128],
                            identity=ident))
                    w("vector", "pe", pe_v)
                    if i == 0 and g == 0:
                        w("vector", "po", C["po"])
                        w("vector", "qC", ksend_q)
                    w("vector", "qC", state.get("vstg_q", 0))
                    ev = do("vector", "dv", lambda e, g=g, tbank=tbank: e.tensor_copy(
                        out=vstg[:, g * 512:(g + 1) * 512],
                        in_=bank_bf(tbank)[:, 0:512]))
                    state[("tev", tb)] = ev
                w("gpsimd", "dv", C["dv"])
                qv = do("gpsimd", "qC", lambda e, i=i: e.dma_start(
                    out=dVSEND[i * 128:(i + 1) * 128, :], in_=vstg[:]), 16)
                state["vstg_q"] = qv
            vsend_q = C["qC"]
            w("gpsimd", "qC", vsend_q)
            if cfg.get("self_pair"):
                do("gpsimd", "cc1", lambda e: e.dma_start(
                    out=dVRECV[0:1024, :], in_=dVSEND), 16)
                do("gpsimd", "cc1", lambda e: e.dma_start(
                    out=dVRECV[1024:2048, :], in_=dVSEND), 16)
                cc1_done = 32
            else:
                do("gpsimd", "cc1", lambda e: e.collective_compute(
                    "AllGather", ALU.bypass,
                    replica_groups=[[0, 1], [2, 3], [4, 5], [6, 7]],
                    ins=[dVSEND.opt()], outs=[dVRECV.opt()]), 1)
                cc1_done = 1
            wq_po2 = stream_w_resident(dWQB, gate_sem="qC", gate_val=vsend_q,
                                       cast_wait=("pe", vproj_pe))
            w("scalar", "qC", vsend_q)
            w("scalar", "dv", bqs_dv)
            q_ac2, q_dv2 = projection(qa, bqs_c, ln3_dv, wq_po2, extra_pe_wait=kproj_pe,
                                      scale_imm=SCALE)
            qproj_pe = C["pe"]
            kb_h = [u8_bf(ar4, 0, 2048), u8_bf(ar4, 4096, 2048)]
            vb_h = [u8_bf(ar4, 8192, 2048), u8_bf(ar4, 16384, 2048)]
            kv_q = {}

            def assemble_head(h):
                slot = h % 2
                qsem = "qA%d" % slot
                w("sync", "cc0", cc0_done)
                w("sync", "cc1", cc1_done)
                w("sync", "pe", qproj_pe)
                w("sync", "dv", state.get(("kvdone_b", slot), 0))
                for j in range(16):
                    r, it = j % 2, j // 2
                    do("sync", qsem, lambda e, h=h, j=j, r=r, it=it, slot=slot: e.dma_start(
                        out=kb_h[slot][:, j * 128:(j + 1) * 128],
                        in_=dKRECV[r * D + h * 128: r * D + (h + 1) * 128,
                                   it * 128:(it + 1) * 128]), 16)
                for j in range(16):
                    r, it = j % 2, j // 2
                    do("sync", qsem, lambda e, h=h, j=j, r=r, it=it, slot=slot: e.dma_start(
                        out=vb_h[slot][:, j * 128:(j + 1) * 128],
                        in_=dVRECV[r * 1024 + it * 128: r * 1024 + (it + 1) * 128,
                                   h * 128:(h + 1) * 128]), 16)
                kv_q[h] = C[qsem]

            assemble_head(0)
            if n_heads > 1:
                assemble_head(1)
            ctx_h3 = scope("s3_heads")
            ctx_h3.__enter__()
            attn_init(qproj_pe)
            pend = None
            m_c = (m_c0, m_c1, m_c2, m_c3)

            def mk_end_b(h, slot):
                def end(ev):
                    state[("kvdone_b", slot)] = ev
                    if h + 2 < n_heads:
                        assemble_head(h + 2)
                return end
            pcount = 0
            for h in range(n_heads):
                slot = h % 2
                for j in range(n_tiles // 2):
                    par = pcount % 2
                    pcount += 1
                    nkc = 4 * j + 4
                    w("tensor", "ac", q_ac2)
                    w("tensor", "dv", q_dv2)
                    w("tensor", "qA%d" % slot, kv_q[h])
                    masks = [(4 * j + t, m_c[t]) for t in range(4)]
                    cl = emit_pair(
                        par, kb_h[slot], vb_h[slot], nkc,
                        qs=qa[:, h * 1024 + j * 256: h * 1024 + (j + 1) * 256],
                        av_dst=h * 1024 + j * 256, masks=masks,
                        head_end=(mk_end_b(h, slot)
                                  if j == n_tiles // 2 - 1 else None))
                    if pend is not None:
                        pend()
                    pend = cl
            if pend is not None:
                pend()
                pend = None
            ctx_h3.__exit__(None, None, None)
            av_done = C["dv"]
            state[("tev", 0)] = state[("tev", 1)] = av_done
            mv = None
            for qq in range(4):
                w("vector", "dv", av_done)
                mv = do("vector", "dv", lambda e, qq=qq: e.tensor_copy(
                    out=av_m[:, qq * 4096:(qq + 1) * 4096],
                    in_=av[:, qq * 4096:(qq + 1) * 4096]))
            av_mv = mv
            w("sync", "dv", av_mv)
            wo_po2 = stream_w_resident(dWOB)
            bo_row2 = u8_bf(ar2, 0, 2048)
            bo32 = u8_f32(ar2, 8192, 2048)
            headloop_pe = C["pe"]
            w("sync", "pe", headloop_pe)
            bo_q = do("sync", "qX", lambda e: e.dma_start(out=bo32[0:1, :], in_=dBO), 16)
            w("gpsimd", "qX", bo_q)
            w("gpsimd", "pe", headloop_pe)
            bo_po = do("gpsimd", "po", lambda e: e.tensor_copy(
                out=bo_row2[0:1, :], in_=bo32[0:1, :]))
            for tt in range(n_tiles):
                pb = 4 * (tt % 2)
                w("tensor", "dv", state.get(("opev", tt % 2), 0))
                w("tensor", "dv", av_mv)
                wait_ready("tensor", wo_po2)
                for hc in range(n_heads):
                    for j in range(4):
                        pe_v = do("tensor", "pe", lambda e, hc=hc, j=j, tt=tt, pb=pb: e.matmul(
                            out=bank(pb + j),
                            lhsT=av_m[:, hc * 1024 + tt * 128: hc * 1024 + (tt + 1) * 128],
                            rhs=w16_tile(hc)[:, j * 512:(j + 1) * 512],
                            start=(hc == 0), stop=False))
                w("tensor", "po", bo_po)
                w("tensor", "qX", C["qX"])
                for j in range(4):
                    pe_v = do("tensor", "pe", lambda e, j=j, pb=pb: e.matmul(
                        out=bank(pb + j), lhsT=ones1[0:1, :],
                        rhs=bo_row2[0:1, j * 512:(j + 1) * 512],
                        start=False, stop=True))
                w("vector", "pe", pe_v)
                w("vector", "dv", state.get(("x_guard", tt), 0))
                ev = do("vector", "dv", lambda e, tt=tt, pb=pb: e.tensor_add(
                    out=x_sb[:, tt * D:(tt + 1) * D],
                    in0=x_sb[:, tt * D:(tt + 1) * D], in1=bank(pb, 4)))
                state[("opev", tt % 2)] = ev
                state[("x_guard", tt)] = ev
            state["sub_done_dv"] = C["dv"]
            state["sub_done_pe"] = C["pe"]
            state["sub_done_po"] = C["po"]

        if 4 in sublayers:
            with scope("ffn2"):
                ffn_sublayer(x_ready=state.get("sub_done_dv", 0),
                             arena_pe=state.get("sub_done_pe", 0),
                             scratch_po=state.get("sub_done_po", 0))
            state["sub_done_dv"] = C["dv"]
            state["sub_done_pe"] = C["pe"]
            state["sub_done_po"] = C["po"]

        xfin = C["dv"]
        for i in range(TI):
            w("sync", "dv", xfin)
            do("sync", "qY", lambda e, i=i: e.dma_start(
                out=dY[i * 128:(i + 1) * 128, :], in_=x_sb[:, i * D:(i + 1) * D]), 16)
        w("sync", "qY", C["qY"])

    def mk(name):
        def f(eng):
            emit(name, eng)
        return f

    block.sync(mk("sync"))
    block.tensor(mk("tensor"))
    block.vector(mk("vector"))
    block.scalar(mk("scalar"))
    block.gpsimd(mk("gpsimd"))

    stack.close()
    return nc


def prep_inputs(inputs, pc_list=None):
    q = np.asarray(inputs["L0_attn_a_query"], dtype=np.float32)
    kvk = np.asarray(inputs["L0_attn_a_kv_k"], dtype=np.float32).reshape(KVLEN, D)
    kvv = np.asarray(inputs["L0_attn_a_kv_v"], dtype=np.float32).reshape(KVLEN, D)
    kvkT = np.ascontiguousarray(kvk.T)
    kvv_b = np.ascontiguousarray(
        kvv.reshape(32, 128, 16, 128).transpose(2, 1, 0, 3).reshape(NH * 128, KVLEN))
    wqaT = np.ascontiguousarray(np.asarray(inputs["L0_attn_a_Wq"], np.float32).T)
    woaT = np.ascontiguousarray(np.asarray(inputs["L0_attn_a_Wo"], np.float32).T)
    wqb = np.ascontiguousarray(np.asarray(inputs["L0_attn_b_Wq"], np.float32))
    wkb = np.ascontiguousarray(np.asarray(inputs["L0_attn_b_Wk"], np.float32))
    wvb = np.ascontiguousarray(np.asarray(inputs["L0_attn_b_Wv"], np.float32))
    wob = np.ascontiguousarray(np.asarray(inputs["L0_attn_b_Wo"], np.float32))
    bq = np.ascontiguousarray(np.asarray(inputs["L0_attn_b_bq"], np.float32).reshape(16, 128).T)
    bk = np.ascontiguousarray(np.asarray(inputs["L0_attn_b_bk"], np.float32).reshape(16, 128).T)
    bv = np.ascontiguousarray(np.asarray(inputs["L0_attn_b_bv"], np.float32).reshape(16, 128).T)
    bo = np.asarray(inputs["L0_attn_b_bo"], np.float32).reshape(1, D)
    lng = np.ascontiguousarray(np.asarray(inputs["L0_attn_b_ln_g"], np.float32).reshape(16, 128).T)
    lnb = np.ascontiguousarray(np.asarray(inputs["L0_attn_b_ln_b"], np.float32).reshape(16, 128).T)
    wg = np.asarray(inputs["L0_ffn_w_gate"], np.float32)
    wu = np.asarray(inputs["L0_ffn_w_up"], np.float32)
    wd = np.asarray(inputs["L0_ffn_w_down"], np.float32)
    wgT_r = np.ascontiguousarray(
        wg.reshape(64, 128, 16, 128).transpose(0, 3, 2, 1).reshape(FH, D))
    wuT_r = np.ascontiguousarray(
        wu.reshape(64, 128, 16, 128).transpose(0, 3, 2, 1).reshape(FH, D))
    wdT_r = np.ascontiguousarray(wd.T)
    ident = np.eye(128, dtype=BF)
    ones1 = np.ones((1, 128), dtype=BF)
    onesM = np.ones((128, 128), dtype=BF)
    triu = (np.arange(128)[None, :] >= np.arange(128)[:, None]).astype(BF)
    one = np.ones((128, 128), dtype=BF)
    zer = np.zeros((128, 128), dtype=BF)

    def cat(a, b):
        return np.ascontiguousarray(np.concatenate([a, b], axis=1))

    in_maps = []
    for c in range(8):
        b, pc = c // 2, c % 2
        x0 = np.ascontiguousarray(
            q[b].reshape(16, 128, D)[pc::2].reshape(TI * 128, D))
        if pc == 0:
            mm = [cat(triu, one), cat(zer, one), cat(zer, triu), cat(zer, zer)]
        else:
            mm = [cat(one, one), cat(triu, one), cat(zer, one), cat(zer, triu)]
        in_maps.append(dict(
            x0=x0, kvkT=kvkT, kvv_b=kvv_b, wqaT=wqaT, woaT=woaT,
            wqb=wqb, wkb=wkb, wvb=wvb, wob=wob,
            bqb=bq, bkb=bk, bvb=bv, bob=bo, lng=lng, lnb=lnb,
            wgT_r=wgT_r, wuT_r=wuT_r, wdT_r=wdT_r,
            mulm0=mm[0], mulm1=mm[1], mulm2=mm[2], mulm3=mm[3],
            onesM=onesM, ident=ident, ones1=ones1,
        ))
    return in_maps


def kernel(**inputs):
    cfg = default_cfg()
    cfg["sublayers"] = (1, 2, 3, 4)
    nc = build_program(cfg)
    in_maps = prep_inputs(inputs)
    res = run_bass_kernel_spmd(nc, in_maps, core_ids=list(range(8)))
    out = np.zeros((4, 2048, D), dtype=np.float32)
    for c in range(8):
        b, pc = c // 2, c % 2
        y = res.results[c]["y"].reshape(TI, 128, D)
        out[b].reshape(16, 128, D)[pc::2] = y
    return out


# version 14
# speedup vs baseline: 1.5881x; 1.1171x over previous
import numpy as np
import ml_dtypes
from contextlib import ExitStack

import concourse.bass as bass
import concourse.mybir as mybir
from concourse.bass_utils import run_bass_kernel_spmd

F32 = mybir.dt.float32
BF16 = mybir.dt.bfloat16
AX = mybir.AxisListType.X
ALU = mybir.AluOpType
AF = mybir.ActivationFunctionType
BF = ml_dtypes.bfloat16

D = 2048
TI = 8
DC = 16
NH = 16
DH = 128
KVLEN = 4096
FH = 8192
NHC = 64
SCALE = 1.0 / np.sqrt(128.0)
EPS_LN = 1e-5
NEG = -1.0e30

ENGS = ("sync", "tensor", "vector", "scalar", "gpsimd")


def default_cfg():
    return dict(
        n_heads=NH,
        n_tiles=TI,
        n_hc=NHC,
        sublayers=(1,),
        zero_arenas=False,
    )


class Em:

    def __init__(self, me, eng, sems):
        self.me = me
        self.eng = eng
        self.S = sems
        self.C = {k: 0 for k in sems}

    def w(self, owner, sem, val):
        if val > 0 and self.me == owner:
            self.eng.wait_ge(self.S[sem], val)

    def do(self, owner, sem, f, n=1):
        if sem is not None:
            self.C[sem] += n
        if self.me == owner:
            inst = f(self.eng)
            if sem is not None:
                inst.then_inc(self.S[sem], n)
        return self.C[sem] if sem is not None else None

    def do0(self, owner, f):
        if self.me == owner:
            f(self.eng)


def build_program(cfg):
    nc = bass.Bass("TRN2", target_bir_lowering=False, debug=False)

    dX = nc.dram_tensor("x0", [TI * 128, D], F32, kind="ExternalInput").ap()
    dKT = nc.dram_tensor("kvkT", [D, KVLEN], F32, kind="ExternalInput").ap()
    dVB = nc.dram_tensor("kvv_b", [NH * 128, KVLEN], F32, kind="ExternalInput").ap()
    dWQA = nc.dram_tensor("wqaT", [D, D], F32, kind="ExternalInput").ap()
    dWOA = nc.dram_tensor("woaT", [D, D], F32, kind="ExternalInput").ap()
    dWQB = nc.dram_tensor("wqb", [D, D], F32, kind="ExternalInput").ap()
    dWKB = nc.dram_tensor("wkb", [D, D], F32, kind="ExternalInput").ap()
    dWVB = nc.dram_tensor("wvb", [D, D], F32, kind="ExternalInput").ap()
    dWOB = nc.dram_tensor("wob", [D, D], F32, kind="ExternalInput").ap()
    dBQ = nc.dram_tensor("bqb", [128, 16], F32, kind="ExternalInput").ap()
    dBK = nc.dram_tensor("bkb", [128, 16], F32, kind="ExternalInput").ap()
    dBV = nc.dram_tensor("bvb", [128, 16], F32, kind="ExternalInput").ap()
    dBO = nc.dram_tensor("bob", [1, D], F32, kind="ExternalInput").ap()
    dLNG = nc.dram_tensor("lng", [128, 16], F32, kind="ExternalInput").ap()
    dLNB = nc.dram_tensor("lnb", [128, 16], F32, kind="ExternalInput").ap()
    dWG = nc.dram_tensor("wgT_r", [FH, D], F32, kind="ExternalInput").ap()
    dWU = nc.dram_tensor("wuT_r", [FH, D], F32, kind="ExternalInput").ap()
    dWD = nc.dram_tensor("wdT_r", [FH, D], F32, kind="ExternalInput").ap()
    dM = [nc.dram_tensor("mulm%d" % t, [128, 256], BF16, kind="ExternalInput").ap()
          for t in range(4)]
    dONESM = nc.dram_tensor("onesM", [128, 128], BF16, kind="ExternalInput").ap()
    dID = nc.dram_tensor("ident", [128, 128], BF16, kind="ExternalInput").ap()
    dONES = nc.dram_tensor("ones1", [1, 128], BF16, kind="ExternalInput").ap()
    dY = nc.dram_tensor("y", [TI * 128, D], F32, kind="ExternalOutput").ap()

    dKSEND = nc.dram_tensor("ksend", [D, 1024], BF16).ap()
    dKRECV = nc.dram_tensor("krecv", [2 * D, 1024], BF16).ap()
    dVSEND = nc.dram_tensor("vsend", [1024, D], BF16).ap()
    dVRECV = nc.dram_tensor("vrecv", [2 * 1024, D], BF16).ap()
    dWG16 = nc.dram_tensor("wg16", [FH, D], BF16).ap()
    dWU16 = nc.dram_tensor("wu16", [FH, D], BF16).ap()
    dWD16 = nc.dram_tensor("wd16", [FH, D], BF16).ap()

    stack = ExitStack()
    e = stack.enter_context

    x_sb = e(nc.sbuf_tensor("x_sb", [128, TI * D], F32))
    ar1 = e(nc.sbuf_tensor("ar1", [128, 32768], mybir.dt.uint8))
    ar2 = e(nc.sbuf_tensor("ar2", [128, 32768], mybir.dt.uint8))
    ar3 = e(nc.sbuf_tensor("ar3", [128, 32768], mybir.dt.uint8))
    ar4 = e(nc.sbuf_tensor("ar4", [128, 32768], mybir.dt.uint8))
    scr = e(nc.sbuf_tensor("scr", [128, 10240], mybir.dt.uint8))
    cst = e(nc.sbuf_tensor("cst", [128, 2048], mybir.dt.uint8))
    sta = e(nc.sbuf_tensor("sta", [128, 2048], mybir.dt.uint8))

    def u8_bf(t, off, n):
        return t[:, off:off + 2 * n].bitcast(BF16)

    def u8_f32(t, off, n):
        return t[:, off:off + 4 * n].bitcast(F32)

    lnT = u8_bf(ar1, 0, 16384)
    pt = u8_bf(ar1, 0, 8192)
    ptb0 = u8_bf(ar1, 0, 8192)
    ptb1 = u8_bf(ar1, 16384, 8192)
    av_m = u8_bf(ar1, 0, 16384)
    bo_row = u8_bf(ar1, 32768 - 4096, 2048)

    qa = u8_bf(ar2, 0, 16384)
    gated = u8_bf(ar2, 0, 8192)
    sg0 = u8_bf(ar2, 16384, 1024)
    sg1 = u8_bf(ar2, 18432, 1024)
    ffw0 = u8_f32(ar2, 20480, 1024)
    ffw1 = u8_f32(ar2, 24576, 1024)
    kb_own = u8_bf(ar2, 0, 16384)

    av = u8_bf(ar3, 0, 16384)
    wd16 = u8_bf(ar3, 0, 16384)
    w16a = u8_bf(ar3, 0, 16384)

    w16b = u8_bf(ar4, 0, 16384)
    kh0 = u8_bf(ar4, 0, 4096)
    kh1 = u8_bf(ar4, 8192, 4096)
    vh0 = u8_bf(ar4, 16384, 4096)
    vh1 = u8_bf(ar4, 24576, 4096)
    wg16s = [u8_bf(ar4, 0, 2048), u8_bf(ar4, 4096, 2048)]
    wu16s = [u8_bf(ar4, 8192, 2048), u8_bf(ar4, 12288, 2048)]
    ffw2 = u8_f32(ar4, 16384, 1024)
    ffw3 = u8_f32(ar4, 20480, 1024)
    wo16s = [u8_bf(ar4, 0, 2048), u8_bf(ar4, 4096, 2048)]

    wst0 = u8_f32(scr, 0, 1024)
    wst1 = u8_f32(scr, 4096, 1024)
    ln_tm0 = u8_bf(ar2, 0, 2048)
    ln_tm1 = u8_bf(ar2, 4096, 2048)
    sqscr = u8_f32(scr, 8192, 512)

    m_c0 = u8_bf(cst, 0, 256)
    m_c1 = u8_bf(cst, 512, 256)
    ident = u8_bf(cst, 1024, 128)
    ones1 = u8_bf(cst, 1280, 128)
    bq_c = u8_f32(cst, 1536, 16)
    bk_c = u8_f32(cst, 1600, 16)
    bv_c = u8_f32(cst, 1664, 16)
    lng_c = u8_f32(cst, 1728, 16)
    lnb_c = u8_f32(cst, 1792, 16)
    eps_c = u8_f32(cst, 1856, 1)
    bqs_c = u8_f32(sta, 1680, 16)

    ssum = u8_f32(sta, 0, 16)
    sqac = u8_f32(sta, 64, 16)
    mu_s = u8_f32(sta, 128, 16)
    var_s = u8_f32(sta, 192, 16)
    rstd_s = u8_f32(sta, 256, 16)
    t0_s = u8_f32(sta, 320, 16)
    t1_s = u8_f32(sta, 384, 16)
    m_c2 = u8_bf(sta, 448, 256)
    m_c3 = u8_bf(sta, 960, 256)
    onesM = u8_bf(sta, 1744, 128)
    rrep = [u8_f32(scr, 8192, 256),
            u8_f32(scr, 9216, 256)]

    ps = e(nc.psum_tensor("ps", [128, 4096], F32))

    def bank(k, n=1):
        return ps[:, 512 * k:512 * (k + n)]

    def bank_bf(k, n=1):
        return ps[:, 512 * k:512 * (k + n)].bitcast(BF16)

    sems = {}
    for name in ("qX", "qS0", "qS1", "qA0", "qA1", "qF0", "qF1", "qF2", "qF3",
                 "qW", "qC", "qK", "qY", "cc0", "cc1", "pe", "dv", "ac", "po"):
        sems[name] = e(nc.semaphore(name))

    block = e(nc.Block())

    n_heads = cfg["n_heads"]
    n_tiles = cfg["n_tiles"]
    n_hc = cfg["n_hc"]
    sublayers = cfg["sublayers"]

    def emit(me, eng):
        E = Em(me, eng, sems)
        w, do, do0, C = E.w, E.do, E.do0, E.C
        def scope(name):
            return nc.named_scope(name)

        x_loaded = []
        do("sync", "qX", lambda e: e.dma_start(
            out=x_sb[:, 0:D], in_=dX[0:128, :]), 16)
        x_loaded.append(("qX", C["qX"]))
        for i in range(1, TI):
            do("sync", "qY", lambda e, i=i: e.dma_start(
                out=x_sb[:, i * D:(i + 1) * D], in_=dX[i * 128:(i + 1) * 128, :]), 16)
            x_loaded.append(("qY", C["qY"]))
        for (dst, src) in ((m_c0, dM[0]), (m_c1, dM[1]), (m_c2, dM[2]),
                           (m_c3, dM[3]), (onesM, dONESM), (ident, dID),
                           (bq_c, dBQ), (bk_c, dBK), (bv_c, dBV),
                           (lng_c, dLNG), (lnb_c, dLNB)):
            do("sync", "qK", lambda e, d=dst, s=src: e.dma_start(out=d, in_=s), 16)
        do("sync", "qK", lambda e: e.dma_start(out=ones1[0:1, :], in_=dONES), 16)
        consts_ok = C["qK"]
        if cfg.get("zero_arenas"):
            for t in (ar1, ar2, ar3, ar4):
                do("vector", "dv", lambda e, t=t: e.memset(t[:, :], 0))
        zero_dv = C["dv"]
        eps_dv = do("vector", "dv", lambda e: e.memset(eps_c, EPS_LN))
        w("vector", "qK", consts_ok)
        bqs_dv = do("vector", "dv", lambda e: e.tensor_scalar_mul(
            out=bqs_c[:], in0=bq_c[:], scalar1=SCALE))

        state = dict(tb=0, avb=0)

        def layer_norm(double_ln=False, affine=False, x_ready=0, arena_pe=0,
                       scratch_po=0):
            for i in range(n_tiles):
                sl = i % 2
                ln_tm = ln_tm0 if sl == 0 else ln_tm1
                xi = x_sb[:, i * D:(i + 1) * D]
                xg_i = state.get(("x_guard", i), 0)
                w("vector", x_loaded[i][0], x_loaded[i][1])
                w("vector", "dv", xg_i)
                v_sum = do("vector", "dv", lambda e, xi=xi, i=i: e.reduce_sum(
                    out=ssum[:, i:i + 1], in_=xi, axis=AX))
                w("scalar", x_loaded[i][0], x_loaded[i][1])
                w("scalar", "dv", xg_i)
                a_sq0 = do("scalar", "ac", lambda e, xi=xi, i=i: e.activation(
                    out=sqscr[:], in_=xi[:, 0:512], func=AF.Square,
                    accum_out=sqac[:, 2 * (i % 8):2 * (i % 8) + 1]))
                w("scalar", "ac", a_sq0)
                a_sq1 = do("scalar", "ac", lambda e, xi=xi, i=i: e.activation(
                    out=sqscr[:], in_=xi[:, 512:1024], func=AF.Square,
                    accum_out=t0_s[:, i:i + 1]))
                w("scalar", "ac", a_sq1)
                a_sq2 = do("scalar", "ac", lambda e, xi=xi, i=i: e.activation(
                    out=sqscr[:], in_=xi[:, 1024:1536], func=AF.Square,
                    accum_out=t1_s[:, i:i + 1]))
                w("scalar", "ac", a_sq2)
                a_sq3 = do("scalar", "ac", lambda e, xi=xi, i=i: e.activation(
                    out=sqscr[:], in_=xi[:, 1536:2048], func=AF.Square,
                    accum_out=sqac[:, 2 * (i % 8) + 1:2 * (i % 8) + 2]))

                def stats_chain(i, v_sum, a_sq3, pfx_mu, pfx_var, pfx_rstd):
                    w("vector", "dv", v_sum)
                    v_mu = do("vector", "dv", lambda e, i=i: e.tensor_scalar_mul(
                        out=mu_s[:, i:i + 1], in0=ssum[:, i:i + 1], scalar1=1.0 / D))
                    w("vector", "ac", a_sq3)
                    v_s2 = do("vector", "dv", lambda e, i=i: e.tensor_add(
                        out=sqac[:, 2 * (i % 8):2 * (i % 8) + 1],
                        in0=sqac[:, 2 * (i % 8):2 * (i % 8) + 1],
                        in1=t0_s[:, i:i + 1]))
                    w("vector", "dv", v_s2)
                    v_s2b = do("vector", "dv", lambda e, i=i: e.tensor_add(
                        out=t1_s[:, i:i + 1], in0=t1_s[:, i:i + 1],
                        in1=sqac[:, 2 * (i % 8) + 1:2 * (i % 8) + 2]))
                    w("vector", "dv", v_s2b)
                    v_s2c = do("vector", "dv", lambda e, i=i: e.tensor_add(
                        out=sqac[:, 2 * (i % 8):2 * (i % 8) + 1],
                        in0=sqac[:, 2 * (i % 8):2 * (i % 8) + 1],
                        in1=t1_s[:, i:i + 1]))
                    w("vector", "dv", max(v_mu, v_s2c))
                    v_mu2 = do("vector", "dv", lambda e, i=i: e.tensor_mul(
                        out=t0_s[:, i:i + 1], in0=mu_s[:, i:i + 1], in1=mu_s[:, i:i + 1]))
                    w("vector", "dv", v_mu2)
                    v_var = do("vector", "dv", lambda e, i=i: e.scalar_tensor_tensor(
                        out=var_s[:, i:i + 1], in0=sqac[:, 2 * (i % 8):2 * (i % 8) + 1],
                        scalar=1.0 / D, in1=t0_s[:, i:i + 1],
                        op0=ALU.mult, op1=ALU.subtract))
                    w("scalar", "dv", v_var)
                    w("scalar", "dv", eps_dv)
                    a_ln = do("scalar", "ac", lambda e, i=i: e.activation(
                        out=t1_s[:, i:i + 1], in_=var_s[:, i:i + 1], func=AF.Ln,
                        bias=eps_c[:, 0:1]))
                    w("scalar", "ac", a_ln)
                    a_rs = do("scalar", "ac", lambda e, i=i: e.activation(
                        out=rstd_s[:, i:i + 1], in_=t1_s[:, i:i + 1], func=AF.Exp,
                        scale=-0.5))
                    return a_rs

                a_rs = stats_chain(i, v_sum, a_sq3, mu_s, var_s, rstd_s)

                if not double_ln:
                    w("vector", "ac", a_rs)
                    if i < 2:
                        w("vector", "pe", arena_pe)
                    v_ln = do("vector", "dv", lambda e, xi=xi, i=i, lt=ln_tm: e.tensor_scalar(
                        out=lt[:], in0=xi, scalar1=mu_s[:, i:i + 1],
                        scalar2=rstd_s[:, i:i + 1], op0=ALU.subtract, op1=ALU.mult))
                else:
                    y32 = ar2[:, 8192:16384].bitcast(F32)
                    w("vector", "ac", a_rs)
                    if i < 2:
                        w("vector", "pe", arena_pe)
                    v_y = do("vector", "dv", lambda e, xi=xi, i=i, y=y32: e.tensor_scalar(
                        out=y[:], in0=xi, scalar1=mu_s[:, i:i + 1],
                        scalar2=rstd_s[:, i:i + 1], op0=ALU.subtract, op1=ALU.mult))
                    w("vector", "dv", v_y)
                    v_sum2 = do("vector", "dv", lambda e, y=y32, i=i: e.reduce_sum(
                        out=ssum[:, 8 + (i % 8):9 + (i % 8)], in_=y, axis=AX))
                    w("scalar", "dv", v_y)
                    a_q0 = do("scalar", "ac", lambda e, y=y32, i=i: e.activation(
                        out=sqscr[:], in_=y[:, 0:512], func=AF.Square,
                        accum_out=sqac[:, 2 * (i % 8):2 * (i % 8) + 1]))
                    w("scalar", "ac", a_q0)
                    a_q1 = do("scalar", "ac", lambda e, y=y32, i=i: e.activation(
                        out=sqscr[:], in_=y[:, 512:1024], func=AF.Square,
                        accum_out=t0_s[:, i:i + 1]))
                    w("scalar", "ac", a_q1)
                    a_q2 = do("scalar", "ac", lambda e, y=y32, i=i: e.activation(
                        out=sqscr[:], in_=y[:, 1024:1536], func=AF.Square,
                        accum_out=t1_s[:, i:i + 1]))
                    w("scalar", "ac", a_q2)
                    a_q3 = do("scalar", "ac", lambda e, y=y32, i=i: e.activation(
                        out=sqscr[:], in_=y[:, 1536:2048], func=AF.Square,
                        accum_out=sqac[:, 2 * (i % 8) + 1:2 * (i % 8) + 2]))
                    w("vector", "dv", v_sum2)
                    v_mu = do("vector", "dv", lambda e, i=i: e.tensor_scalar_mul(
                        out=mu_s[:, 8 + (i % 8):9 + (i % 8)],
                        in0=ssum[:, 8 + (i % 8):9 + (i % 8)], scalar1=1.0 / D))
                    w("vector", "ac", a_q3)
                    v_a = do("vector", "dv", lambda e, i=i: e.tensor_add(
                        out=sqac[:, 2 * (i % 8):2 * (i % 8) + 1],
                        in0=sqac[:, 2 * (i % 8):2 * (i % 8) + 1], in1=t0_s[:, i:i + 1]))
                    w("vector", "dv", v_a)
                    v_b = do("vector", "dv", lambda e, i=i: e.tensor_add(
                        out=t1_s[:, i:i + 1], in0=t1_s[:, i:i + 1],
                        in1=sqac[:, 2 * (i % 8) + 1:2 * (i % 8) + 2]))
                    w("vector", "dv", v_b)
                    v_c = do("vector", "dv", lambda e, i=i: e.tensor_add(
                        out=sqac[:, 2 * (i % 8):2 * (i % 8) + 1],
                        in0=sqac[:, 2 * (i % 8):2 * (i % 8) + 1], in1=t1_s[:, i:i + 1]))
                    w("vector", "dv", max(v_mu, v_c))
                    v_mu2 = do("vector", "dv", lambda e, i=i: e.tensor_mul(
                        out=t0_s[:, i:i + 1], in0=mu_s[:, 8 + (i % 8):9 + (i % 8)],
                        in1=mu_s[:, 8 + (i % 8):9 + (i % 8)]))
                    w("vector", "dv", v_mu2)
                    v_var = do("vector", "dv", lambda e, i=i: e.scalar_tensor_tensor(
                        out=var_s[:, 8 + (i % 8):9 + (i % 8)],
                        in0=sqac[:, 2 * (i % 8):2 * (i % 8) + 1],
                        scalar=1.0 / D, in1=t0_s[:, i:i + 1],
                        op0=ALU.mult, op1=ALU.subtract))
                    w("scalar", "dv", v_var)
                    w("scalar", "dv", eps_dv)
                    a_l2 = do("scalar", "ac", lambda e, i=i: e.activation(
                        out=t1_s[:, i:i + 1], in_=var_s[:, 8 + (i % 8):9 + (i % 8)],
                        func=AF.Ln, bias=eps_c[:, 0:1]))
                    w("scalar", "ac", a_l2)
                    a_rs2 = do("scalar", "ac", lambda e, i=i: e.activation(
                        out=rstd_s[:, 8 + (i % 8):9 + (i % 8)], in_=t1_s[:, i:i + 1],
                        func=AF.Exp, scale=-0.5))
                    w("vector", "ac", a_rs2)
                    v_ln = do("vector", "dv", lambda e, y=y32, i=i, lt=ln_tm: e.tensor_scalar(
                        out=lt[:], in0=y, scalar1=mu_s[:, 8 + (i % 8):9 + (i % 8)],
                        scalar2=rstd_s[:, 8 + (i % 8):9 + (i % 8)],
                        op0=ALU.subtract, op1=ALU.mult))

                w("tensor", "qK", consts_ok)
                w("tensor", "dv", zero_dv)
                w("tensor", "pe", arena_pe)
                for g in range(4):
                    tb = state["tb"]
                    state["tb"] ^= 1
                    tbank = 4 + tb
                    w("tensor", "dv", state.get(("tev", tb), 0))
                    w("tensor", "dv", v_ln)
                    for j in range(4):
                        dc = g * 4 + j
                        pe_v = do("tensor", "pe", lambda e, lt=ln_tm, dc=dc, j=j, tbank=tbank: e.transpose(
                            out=bank_bf(tbank)[:, j * 128:(j + 1) * 128],
                            in_=lt[:, dc * 128:(dc + 1) * 128], identity=ident))
                    w("vector", "pe", pe_v)
                    if affine:
                        for j in range(4):
                            dc = g * 4 + j
                            ev = do("vector", "dv", lambda e, dc=dc, j=j, i=i, tbank=tbank: e.tensor_scalar(
                                out=lnT[:, dc * 1024 + i * 128: dc * 1024 + (i + 1) * 128],
                                in0=bank_bf(tbank)[:, j * 128:(j + 1) * 128],
                                scalar1=lng_c[:, dc:dc + 1], scalar2=lnb_c[:, dc:dc + 1],
                                op0=ALU.mult, op1=ALU.add))
                    else:
                        ev = do("vector", "dv", lambda e, g=g, i=i, tbank=tbank: e.tensor_copy(
                            out=lnT.rearrange("p (c n) -> p c n", n=1024)[
                                :, g * 4:(g + 1) * 4, i * 128:(i + 1) * 128],
                            in_=bank_bf(tbank).rearrange("p (c n) -> p c n", n=128)[:, 0:4, :]))
                    state[("tev", tb)] = ev
            state["scr_free_pe"] = C["pe"]
            return C["dv"], C["ac"]

        def w16_tile(dcc):
            if dcc < 8:
                return w16a[:, dcc * 2048:(dcc + 1) * 2048]
            return w16b[:, (dcc - 8) * 2048:(dcc - 8 + 1) * 2048]

        CAST_SEM = {"gpsimd": "po", "vector": "dv", "scalar": "ac"}

        def cast_to(engx, dst, srcap, waits=()):
            for (s, v) in waits:
                w(engx, s, v)
            sem = CAST_SEM[engx]
            if engx == "scalar":
                v = do(engx, sem, lambda e, dst=dst, srcap=srcap: e.copy(
                    out=dst, in_=srcap))
            else:
                v = do(engx, sem, lambda e, dst=dst, srcap=srcap: e.tensor_copy(
                    out=dst, in_=srcap))
            return (sem, v)

        STG = [(wst0, "qS0"), (wst1, "qS1")]

        def stream_w_resident(src, gate_sem=None, gate_val=0, dst_tile=None,
                              cast_engs=("vector", "scalar"), cast_wait=None,
                              ring=None):
            if dst_tile is None:
                dst_tile = w16_tile
            if ring is None:
                ring = STG
            for dcc in range(DC):
                for hhalf in range(2):
                    k = dcc * 2 + hhalf
                    p = k % len(ring)
                    st, qsem = ring[p]
                    if gate_sem is not None and k == 0:
                        w("sync", gate_sem, gate_val)
                    ssem, sval = state.get(("fst", qsem), ("po", 0))
                    w("sync", ssem, sval)
                    qv = do("sync", qsem, lambda e, st=st, dcc=dcc, hh=hhalf, src=src: e.dma_start(
                        out=st, in_=src[dcc * 128:(dcc + 1) * 128, hh * 1024:(hh + 1) * 1024]), 16)
                    engx = cast_engs[k % len(cast_engs)]
                    wl = [(qsem, qv)]
                    if cast_wait is not None:
                        wl.append(cast_wait)
                    last = cast_to(engx, dst_tile(dcc)[:, hhalf * 1024:(hhalf + 1) * 1024],
                                   st, waits=wl)
                    state[("fst", qsem)] = last
            return dict(po=C["po"], dv=C["dv"], ac=C["ac"])

        def wait_ready(owner, r):
            if isinstance(r, dict):
                w(owner, "po", r["po"])
                w(owner, "dv", r["dv"])
                w(owner, "ac", r["ac"])
            else:
                w(owner, "po", r)

        def projection(dst, bias_col, lnT_ready_dv, w_po, extra_pe_wait=0, n_oc=DC,
                       scale_imm=None):
            for oc in range(n_oc):
                for th in range(2):
                    pbk = (2 * oc + th) % 4
                    psem, pval = state.get(("pjev", pbk), ("ac", 0))
                    w("tensor", psem, pval)
                    w("tensor", "dv", lnT_ready_dv)
                    wait_ready("tensor", w_po)
                    w("tensor", "pe", extra_pe_wait)
                    for dcc in range(DC):
                        pe_v = do("tensor", "pe", lambda e, oc=oc, th=th, dcc=dcc, pbk=pbk: e.matmul(
                            out=bank(pbk),
                            lhsT=w16_tile(dcc)[:, oc * 128:(oc + 1) * 128],
                            rhs=lnT[:, dcc * 1024 + th * 512: dcc * 1024 + (th + 1) * 512],
                            start=(dcc == 0), stop=(dcc == DC - 1)))
                    if scale_imm is not None and bias_col is not None:
                        w("vector", "pe", pe_v)
                        evd = do("vector", "dv", lambda e, oc=oc, th=th, pbk=pbk, dst=dst, b=bias_col, s=scale_imm: e.tensor_scalar(
                            out=dst[:, oc * 1024 + th * 512: oc * 1024 + (th + 1) * 512],
                            in0=bank(pbk), scalar1=s, scalar2=b[:, oc:oc + 1],
                            op0=ALU.mult, op1=ALU.add))
                        state[("pjev", pbk)] = ("dv", evd)
                        continue
                    w("scalar", "pe", pe_v)
                    if scale_imm is not None:
                        ev = do("scalar", "ac", lambda e, oc=oc, th=th, pbk=pbk, dst=dst, s=scale_imm: e.activation(
                            out=dst[:, oc * 1024 + th * 512: oc * 1024 + (th + 1) * 512],
                            in_=bank(pbk), func=AF.Copy, scale=s, bias=0.0))
                    elif bias_col is None:
                        ev = do("scalar", "ac", lambda e, oc=oc, th=th, pbk=pbk, dst=dst: e.copy(
                            out=dst[:, oc * 1024 + th * 512: oc * 1024 + (th + 1) * 512],
                            in_=bank(pbk)))
                    else:
                        ev = do("scalar", "ac", lambda e, oc=oc, th=th, pbk=pbk, dst=dst, b=bias_col: e.add(
                            out=dst[:, oc * 1024 + th * 512: oc * 1024 + (th + 1) * 512],
                            in_=bank(pbk),
                            add=b[:, oc:oc + 1]))
                    state[("pjev", pbk)] = ("ac", ev)
            return (C["ac"], C["dv"])

        def emit_pair(par, kh, vh, nkc, qs, av_dst, masks, head_end=None):
            pt = ptb0 if par == 0 else ptb1
            ngr = (nkc + 1) // 2
            last_exp = 0
            first = True
            for g in range(ngr):
                bk = state["sc_rot"] % 4
                state["sc_rot"] += 1
                w("tensor", "ac", state.get(("scb", bk), 0))
                pe_v = None
                n_in = min(2, nkc - 2 * g)
                for half in range(n_in):
                    kc = 2 * g + half
                    pe_v = do("tensor", "pe", lambda e, bk=bk, half=half, kc=kc, kh=kh, qs=qs: e.matmul(
                        out=bank(bk)[:, half * 256:(half + 1) * 256],
                        lhsT=kh[:, kc * 128:(kc + 1) * 128], rhs=qs,
                        start=True, stop=True))
                w("scalar", "pe", pe_v)
                if first:
                    w("scalar", "pe", state[("ptfree", par)])
                    first = False
                a_e = do("scalar", "ac", lambda e, g=g, n_in=n_in, bk=bk, pt=pt: e.activation(
                    out=pt[:, g * 512: g * 512 + n_in * 256],
                    in_=bank(bk)[:, 0:n_in * 256], func=AF.Exp))
                state[("scb", bk)] = a_e
                last_exp = a_e
            mask_dv = 0
            for (kc, m_ap) in masks:
                w("vector", "ac", last_exp)
                mask_dv = do("vector", "dv", lambda e, kc=kc, m_ap=m_ap, pt=pt: e.tensor_mul(
                    out=pt[:, kc * 256:(kc + 1) * 256],
                    in0=pt[:, kc * 256:(kc + 1) * 256], in1=m_ap))
            exp_c, mask_c = last_exp, mask_dv

            def closure():
                w("tensor", "ac", exp_c)
                w("tensor", "dv", mask_c)
                w("tensor", "dv", state.get(("dnfree", par), 0))
                pe_d = None
                for kc in range(nkc):
                    pe_d = do("tensor", "pe", lambda e, kc=kc, par=par, pt=pt: e.matmul(
                        out=bank(6 + par)[:, 0:256], lhsT=onesM,
                        rhs=pt[:, kc * 256:(kc + 1) * 256],
                        start=(kc == 0), stop=(kc == nkc - 1)))
                w("vector", "pe", pe_d)
                v_r = do("vector", "dv", lambda e, par=par: e.reciprocal(
                    out=rrep[par], in_=bank(6 + par)[:, 0:256]))
                state[("dnfree", par)] = v_r
                w("tensor", "dv", state.get(("avfree", par), 0))
                pe_a = None
                for kc in range(nkc):
                    pe_a = do("tensor", "pe", lambda e, kc=kc, par=par, pt=pt, vh=vh: e.matmul(
                        out=bank(4 + par)[:, 0:256],
                        lhsT=vh[:, kc * 128:(kc + 1) * 128],
                        rhs=pt[:, kc * 256:(kc + 1) * 256],
                        start=(kc == 0), stop=(kc == nkc - 1)))
                state[("ptfree", par)] = C["pe"]
                w("vector", "pe", pe_a)
                w("vector", "dv", v_r)
                ev = do("vector", "dv", lambda e, par=par, av_dst=av_dst: e.tensor_mul(
                    out=av[:, av_dst:av_dst + 256], in0=bank(4 + par)[:, 0:256],
                    in1=rrep[par]))
                state[("avfree", par)] = ev
                if head_end is not None:
                    head_end(ev)
            return closure

        def attn_init(qdone_pe):
            for bk in range(4):
                psem, pval = state.get(("pjev", bk), ("ac", 0))
                w("tensor", psem, pval)
            state.setdefault("sc_rot", 0)
            state[("ptfree", 0)] = state[("ptfree", 1)] = qdone_pe
            state[("dnfree", 0)] = state[("dnfree", 1)] = C["dv"]
            state[("avfree", 0)] = state[("avfree", 1)] = C["dv"]


        def ffn_sublayer(x_ready, arena_pe, scratch_po):
            ln_dv, _ = layer_norm(x_ready=x_ready, arena_pe=arena_pe,
                                  scratch_po=scratch_po)
            state[("w16free", 0)] = arena_pe
            state[("w16free", 1)] = arena_pe
            state["ffn_down_pe"] = arena_pe
            state["ffn_dn_ev01"] = x_ready
            state["ffn_tt_dv"] = x_ready
            state[("sgfree", 0)] = x_ready
            state[("sgfree", 1)] = x_ready
            nblk = max(1, n_hc // 8)
            for blk in range(nblk):
                for k in range(8):
                    hc = blk * 8 + k
                    par = hc % 2
                    gu_marks = []
                    wd_marks = []
                    for (srcw, stg, stsem, half, engx, is_wd) in (
                        (dWG, ffw0, "qF0", 0, "gpsimd", 0),
                        (dWG, ffw1, "qF1", 1, "vector", 0),
                        (dWU, ffw2, "qF2", 0, "gpsimd", 0),
                        (dWU, ffw3, "qF3", 1, "scalar", 0),
                        (dWD, wst0, "qS0", 0, "vector", 1),
                        (dWD, wst1, "qS1", 1, "scalar", 1),
                    ):
                        ssem, sval = state.get(("fst", stsem), ("po", 0))
                        w("sync", ssem, sval)
                        if blk == 0 and k == 0 and half == 0 and srcw is dWG:
                            w("sync", "pe", arena_pe)
                            w("sync", "dv", ln_dv)
                        qv = do("sync", stsem, lambda e, st=stg, srcw=srcw, hc=hc, half=half: e.dma_start(
                            out=st, in_=srcw[hc * 128:(hc + 1) * 128,
                                             half * 1024:(half + 1) * 1024]), 16)
                        if is_wd:
                            dst = wd16[:, k * 2048 + half * 1024:
                                       k * 2048 + (half + 1) * 1024]
                            wts = [(stsem, qv), ("pe", state.get("ffn_down_pe", 0))]
                            if blk == 0 and k == 0:
                                wts.append(("dv", x_ready))
                        else:
                            wsrc = wg16s if srcw is dWG else wu16s
                            dst = wsrc[hc % 2][:, half * 1024:(half + 1) * 1024]
                            wts = [(stsem, qv),
                                   ("pe", state.get(("w16free", par), 0))]
                        mk = cast_to(engx, dst, stg, waits=wts)
                        state[("fst", stsem)] = mk
                        (wd_marks if is_wd else gu_marks).append(mk)
                    state[("wd_marks", k)] = list(wd_marks)
                    for (s_, v_) in gu_marks:
                        w("tensor", s_, v_)
                    w("tensor", "dv", ln_dv)
                    w("tensor", "ac", state.get("ffn_silu_ac", 0))
                    w("tensor", "dv", state.get("ffn_dn_ev01", 0))
                    for dcc in range(DC):
                        for th in range(2):
                            pe_g = do("tensor", "pe", lambda e, dcc=dcc, th=th, hc=hc: e.matmul(
                                out=bank(th), lhsT=wg16s[hc % 2][:, dcc * 128:(dcc + 1) * 128],
                                rhs=lnT[:, dcc * 1024 + th * 512: dcc * 1024 + (th + 1) * 512],
                                start=(dcc == 0), stop=(dcc == DC - 1)))
                    w("scalar", "pe", pe_g)
                    w("scalar", "dv", state.get(("sgfree", hc % 2), 0))
                    a_si = do("scalar", "ac", lambda e, hc=hc: e.activation(
                        out=(sg0 if hc % 2 == 0 else sg1)[:], in_=bank(0, 2), func=AF.Silu))
                    state["ffn_silu_ac"] = a_si
                    w("tensor", "dv", state.get("ffn_tt_dv", 0))
                    for dcc in range(DC):
                        for th in range(2):
                            pe_u = do("tensor", "pe", lambda e, dcc=dcc, th=th, hc=hc: e.matmul(
                                out=bank(2 + th), lhsT=wu16s[hc % 2][:, dcc * 128:(dcc + 1) * 128],
                                rhs=lnT[:, dcc * 1024 + th * 512: dcc * 1024 + (th + 1) * 512],
                                start=(dcc == 0), stop=(dcc == DC - 1)))
                    state[("w16free", par)] = C["pe"]
                    w("vector", "pe", pe_u)
                    w("vector", "ac", a_si)
                    w("vector", "pe", state.get("ffn_down_pe", 0))
                    v_tt = do("vector", "dv", lambda e, hc=hc, k=k: e.tensor_mul(
                        out=gated[:, k * 1024:(k + 1) * 1024], in0=bank(2, 2),
                        in1=(sg0 if hc % 2 == 0 else sg1)[:]))
                    state["ffn_tt_dv"] = v_tt
                    state[("sgfree", hc % 2)] = v_tt
                    state[("ffn_gated", k)] = v_tt
                for tt in range(n_tiles):
                    pb = 4 * (tt % 2)
                    w("tensor", "dv", state[("ffn_gated", 7)])
                    if tt == 0:
                        for kk in range(8):
                            for (s_, v_) in state[("wd_marks", kk)]:
                                w("tensor", s_, v_)
                    w("tensor", "ac", state.get("ffn_silu_ac", 0))
                    w("tensor", "dv", state.get(("ffn_dn_ev", tt % 2), 0))
                    for hcc in range(8):
                        for j in range(4):
                            pe_d = do("tensor", "pe", lambda e, hcc=hcc, j=j, tt=tt, pb=pb: e.matmul(
                                out=bank(pb + j),
                                lhsT=gated[:, hcc * 1024 + tt * 128: hcc * 1024 + (tt + 1) * 128],
                                rhs=wd16[:, hcc * 2048 + j * 512: hcc * 2048 + (j + 1) * 512],
                                start=(hcc == 0), stop=(hcc == 7)))
                    w("vector", "pe", pe_d)
                    w("vector", "dv", state.get(("x_guard", tt), 0))
                    ev = do("vector", "dv", lambda e, tt=tt, pb=pb: e.tensor_add(
                        out=x_sb[:, tt * D:(tt + 1) * D],
                        in0=x_sb[:, tt * D:(tt + 1) * D], in1=bank(pb, 4)))
                    state[("ffn_dn_ev", tt % 2)] = ev
                    state[("x_guard", tt)] = ev
                    if tt % 2 == 0:
                        state["ffn_dn_ev01"] = ev
                state["ffn_down_pe"] = C["pe"]
            return C["dv"]
        if 1 in sublayers:
            with scope("s1_ln"):
                ln_dv, ln_ac = layer_norm(double_ln=False, affine=False)

            wq_po = stream_w_resident(
                dWQA, gate_sem=("dv" if cfg.get("zero_arenas") else None),
                gate_val=zero_dv if cfg.get("zero_arenas") else 0,
                cast_engs=("gpsimd",))
            q_ac, _ = projection(qa, None, ln_dv, wq_po, n_oc=n_heads, scale_imm=SCALE)
            state["kv_gate_pe"] = C["pe"]

            kv_po = {}
            def stream_head_kv2(h):
                slot = h % 2
                kh = kh0 if slot == 0 else kh1
                vh = vh0 if slot == 0 else vh1
                for part in range(8):
                    is_v = part >= 4
                    src = dVB if is_v else dKT
                    dst = vh if is_v else kh
                    col0 = (part % 4) * 1024
                    st, qsem = STG[part % len(STG)]
                    ssem, sval = state.get(("fst", qsem), ("po", 0))
                    w("sync", ssem, sval)
                    w("sync", "dv", state.get(("kvdone", slot), 0))
                    qv = do("sync", qsem, lambda e, st=st, src=src, h=h, col0=col0: e.dma_start(
                        out=st, in_=src[h * 128:(h + 1) * 128, col0:col0 + 1024]), 16)
                    w("gpsimd", qsem, qv)
                    w("gpsimd", "pe", state.get("kv_gate_pe", 0))
                    pv = do("gpsimd", "po", lambda e, st=st, dst=dst, col0=col0: e.tensor_copy(
                        out=dst[:, col0:col0 + 1024], in_=st))
                    state[("fst", qsem)] = ("po", pv)
                kv_po[h] = C["po"]

            stream_head_kv2(0)
            if n_heads > 1:
                stream_head_kv2(1)

            ctx_h = scope("s1_heads")
            ctx_h.__enter__()
            attn_init(state["kv_gate_pe"])
            pend = None

            def mk_end_a(h, slot):
                def end(ev):
                    state[("kvdone", slot)] = ev
                    if h + 2 < n_heads:
                        stream_head_kv2(h + 2)
                return end
            pcount = 0
            for h in range(n_heads):
                slot = h % 2
                kh = kh0 if slot == 0 else kh1
                vh = vh0 if slot == 0 else vh1
                for j in range(n_tiles // 2):
                    par = pcount % 2
                    pcount += 1
                    w("tensor", "ac", q_ac)
                    w("tensor", "po", kv_po[h])
                    cl = emit_pair(
                        par, kh, vh, KVLEN // 128,
                        qs=qa[:, h * 1024 + j * 256: h * 1024 + (j + 1) * 256],
                        av_dst=h * 1024 + j * 256, masks=(),
                        head_end=(mk_end_a(h, slot)
                                  if j == n_tiles // 2 - 1 else None))
                    if pend is not None:
                        pend()
                    pend = cl
            if pend is not None:
                pend()
                pend = None

            ctx_h.__exit__(None, None, None)
            av_done = C["dv"]
            state[("tev", 0)] = state[("tev", 1)] = av_done
            mv = None
            for q in range(4):
                w("vector", "dv", av_done)
                mv = do("vector", "dv", lambda e, q=q: e.tensor_copy(
                    out=av_m[:, q * 4096:(q + 1) * 4096], in_=av[:, q * 4096:(q + 1) * 4096]))
            av_mv = mv
            w("sync", "dv", av_mv)
            wo_po = stream_w_resident(dWOA)

            for tt in range(n_tiles):
                pb = 4 * (tt % 2)
                w("tensor", "dv", state.get(("opev", tt % 2), 0))
                w("tensor", "dv", av_mv)
                wait_ready("tensor", wo_po)
                for hc in range(n_heads):
                    for j in range(4):
                        pe_v = do("tensor", "pe", lambda e, hc=hc, j=j, tt=tt, pb=pb: e.matmul(
                            out=bank(pb, 4)[:, j * 512:(j + 1) * 512],
                            lhsT=av_m[:, hc * 1024 + tt * 128: hc * 1024 + (tt + 1) * 128],
                            rhs=w16_tile(hc)[:, j * 512:(j + 1) * 512],
                            start=(hc == 0), stop=(hc == n_heads - 1)))
                w("vector", "pe", pe_v)
                ev = do("vector", "dv", lambda e, tt=tt, pb=pb: e.tensor_add(
                    out=x_sb[:, tt * D:(tt + 1) * D],
                    in0=x_sb[:, tt * D:(tt + 1) * D], in1=bank(pb, 4)))
                state[("opev", tt % 2)] = ev
                state[("x_guard", tt)] = ev
            state["sub_done_dv"] = C["dv"]
            state["sub_done_pe"] = C["pe"]
            state["sub_done_po"] = C["po"]

        if 2 in sublayers:
            with scope("ffn1"):
                ffn_sublayer(x_ready=state.get("sub_done_dv", 0),
                             arena_pe=state.get("sub_done_pe", 0),
                             scratch_po=state.get("sub_done_po", 0))
            state["sub_done_dv"] = C["dv"]
            state["sub_done_pe"] = C["pe"]
            state["sub_done_po"] = C["po"]

        if 3 in sublayers:
            sd, sp, spo = (state.get("sub_done_dv", 0), state.get("sub_done_pe", 0),
                           state.get("sub_done_po", 0))
            ln3_dv, _ = layer_norm(double_ln=True, affine=True, x_ready=sd,
                                   arena_pe=sp, scratch_po=spo)
            wk_po = stream_w_resident(dWKB, gate_sem="pe", gate_val=sp,
                                      cast_engs=("gpsimd", "gpsimd",
                                                 "vector", "scalar"))
            k_ac, _ = projection(kb_own, bk_c, ln3_dv, wk_po)
            kproj_pe = C["pe"]
            for oc in range(DC):
                w("gpsimd", "ac", k_ac)
                do("gpsimd", "qC", lambda e, oc=oc: e.dma_start(
                    out=dKSEND[oc * 128:(oc + 1) * 128, :],
                    in_=kb_own[:, oc * 1024:(oc + 1) * 1024]), 16)
            ksend_q = C["qC"]
            w("gpsimd", "qC", ksend_q)
            if cfg.get("self_pair"):
                do("gpsimd", "cc0", lambda e: e.dma_start(
                    out=dKRECV[0:D, :], in_=dKSEND), 16)
                do("gpsimd", "cc0", lambda e: e.dma_start(
                    out=dKRECV[D:2 * D, :], in_=dKSEND), 16)
                cc0_done = 32
            else:
                do("gpsimd", "cc0", lambda e: e.collective_compute(
                    "AllGather", ALU.bypass,
                    replica_groups=[[0, 1], [2, 3], [4, 5], [6, 7]],
                    ins=[dKSEND.opt()], outs=[dKRECV.opt()]), 1)
                cc0_done = 1
            wv_po = stream_w_resident(dWVB, cast_wait=("pe", kproj_pe))
            w("scalar", "qC", ksend_q)
            v_ac, _ = projection(kb_own, bv_c, ln3_dv, wv_po, extra_pe_wait=kproj_pe)
            vproj_pe = C["pe"]
            vstg = scr[:, 0:4096].bitcast(BF16)
            for i in range(TI):
                for g in range(4):
                    tb = state["tb"]
                    state["tb"] ^= 1
                    tbank = 4 + tb
                    w("tensor", "dv", state.get(("tev", tb), 0))
                    w("tensor", "ac", v_ac)
                    for jj in range(4):
                        oc = g * 4 + jj
                        pe_v = do("tensor", "pe", lambda e, oc=oc, jj=jj, i=i, tbank=tbank: e.transpose(
                            out=bank_bf(tbank)[:, jj * 128:(jj + 1) * 128],
                            in_=kb_own[:, oc * 1024 + i * 128: oc * 1024 + (i + 1) * 128],
                            identity=ident))
                    w("vector", "pe", pe_v)
                    if i == 0 and g == 0:
                        w("vector", "po", C["po"])
                        w("vector", "qC", ksend_q)
                    w("vector", "qC", state.get("vstg_q", 0))
                    ev = do("vector", "dv", lambda e, g=g, tbank=tbank: e.tensor_copy(
                        out=vstg[:, g * 512:(g + 1) * 512],
                        in_=bank_bf(tbank)[:, 0:512]))
                    state[("tev", tb)] = ev
                w("gpsimd", "dv", C["dv"])
                qv = do("gpsimd", "qC", lambda e, i=i: e.dma_start(
                    out=dVSEND[i * 128:(i + 1) * 128, :], in_=vstg[:]), 16)
                state["vstg_q"] = qv
            vsend_q = C["qC"]
            w("gpsimd", "qC", vsend_q)
            if cfg.get("self_pair"):
                do("gpsimd", "cc1", lambda e: e.dma_start(
                    out=dVRECV[0:1024, :], in_=dVSEND), 16)
                do("gpsimd", "cc1", lambda e: e.dma_start(
                    out=dVRECV[1024:2048, :], in_=dVSEND), 16)
                cc1_done = 32
            else:
                do("gpsimd", "cc1", lambda e: e.collective_compute(
                    "AllGather", ALU.bypass,
                    replica_groups=[[0, 1], [2, 3], [4, 5], [6, 7]],
                    ins=[dVSEND.opt()], outs=[dVRECV.opt()]), 1)
                cc1_done = 1
            wq_po2 = stream_w_resident(dWQB, gate_sem="qC", gate_val=vsend_q,
                                       cast_wait=("pe", vproj_pe))
            w("scalar", "qC", vsend_q)
            w("scalar", "dv", bqs_dv)
            q_ac2, q_dv2 = projection(qa, bqs_c, ln3_dv, wq_po2, extra_pe_wait=kproj_pe,
                                      scale_imm=SCALE)
            qproj_pe = C["pe"]
            kb_h = [u8_bf(ar4, 0, 2048), u8_bf(ar4, 4096, 2048)]
            vb_h = [u8_bf(ar4, 8192, 2048), u8_bf(ar4, 16384, 2048)]
            kv_q = {}

            def assemble_head(h):
                slot = h % 2
                qsem = "qA%d" % slot
                w("sync", "cc0", cc0_done)
                w("sync", "cc1", cc1_done)
                w("sync", "pe", qproj_pe)
                w("sync", "dv", state.get(("kvdone_b", slot), 0))
                for r in range(2):
                    do("sync", qsem, lambda e, h=h, r=r, slot=slot: e.dma_start(
                        out=kb_h[slot].rearrange("p (j n) -> p j n", n=128)[:, r::2, :],
                        in_=dKRECV[r * D + h * 128: r * D + (h + 1) * 128, :]
                        .rearrange("p (it n) -> p it n", n=128)), 16)
                for r in range(2):
                    do("sync", qsem, lambda e, h=h, r=r, slot=slot: e.dma_start(
                        out=vb_h[slot].rearrange("p (j n) -> p j n", n=128)[:, r::2, :],
                        in_=dVRECV[r * 1024:(r + 1) * 1024, h * 128:(h + 1) * 128]
                        .rearrange("(it p) n -> p it n", p=128)), 16)
                kv_q[h] = C[qsem]

            assemble_head(0)
            if n_heads > 1:
                assemble_head(1)
            ctx_h3 = scope("s3_heads")
            ctx_h3.__enter__()
            attn_init(qproj_pe)
            pend = None
            m_c = (m_c0, m_c1, m_c2, m_c3)

            def mk_end_b(h, slot):
                def end(ev):
                    state[("kvdone_b", slot)] = ev
                    if h + 2 < n_heads:
                        assemble_head(h + 2)
                return end
            pcount = 0
            for h in range(n_heads):
                slot = h % 2
                for j in range(n_tiles // 2):
                    par = pcount % 2
                    pcount += 1
                    nkc = 4 * j + 4
                    w("tensor", "ac", q_ac2)
                    w("tensor", "dv", q_dv2)
                    w("tensor", "qA%d" % slot, kv_q[h])
                    masks = [(4 * j + t, m_c[t]) for t in range(4)]
                    cl = emit_pair(
                        par, kb_h[slot], vb_h[slot], nkc,
                        qs=qa[:, h * 1024 + j * 256: h * 1024 + (j + 1) * 256],
                        av_dst=h * 1024 + j * 256, masks=masks,
                        head_end=(mk_end_b(h, slot)
                                  if j == n_tiles // 2 - 1 else None))
                    if pend is not None:
                        pend()
                    pend = cl
            if pend is not None:
                pend()
                pend = None
            ctx_h3.__exit__(None, None, None)
            av_done = C["dv"]
            state[("tev", 0)] = state[("tev", 1)] = av_done
            mv = None
            for qq in range(4):
                w("vector", "dv", av_done)
                mv = do("vector", "dv", lambda e, qq=qq: e.tensor_copy(
                    out=av_m[:, qq * 4096:(qq + 1) * 4096],
                    in_=av[:, qq * 4096:(qq + 1) * 4096]))
            av_mv = mv
            w("sync", "dv", av_mv)
            wo_po2 = stream_w_resident(dWOB)
            bo_row2 = u8_bf(ar2, 0, 2048)
            bo32 = u8_f32(ar2, 8192, 2048)
            headloop_pe = C["pe"]
            w("sync", "pe", headloop_pe)
            bo_q = do("sync", "qX", lambda e: e.dma_start(out=bo32[0:1, :], in_=dBO), 16)
            w("gpsimd", "qX", bo_q)
            w("gpsimd", "pe", headloop_pe)
            bo_po = do("gpsimd", "po", lambda e: e.tensor_copy(
                out=bo_row2[0:1, :], in_=bo32[0:1, :]))
            for tt in range(n_tiles):
                pb = 4 * (tt % 2)
                w("tensor", "dv", state.get(("opev", tt % 2), 0))
                w("tensor", "dv", av_mv)
                wait_ready("tensor", wo_po2)
                for hc in range(n_heads):
                    for j in range(4):
                        pe_v = do("tensor", "pe", lambda e, hc=hc, j=j, tt=tt, pb=pb: e.matmul(
                            out=bank(pb + j),
                            lhsT=av_m[:, hc * 1024 + tt * 128: hc * 1024 + (tt + 1) * 128],
                            rhs=w16_tile(hc)[:, j * 512:(j + 1) * 512],
                            start=(hc == 0), stop=False))
                w("tensor", "po", bo_po)
                w("tensor", "qX", C["qX"])
                for j in range(4):
                    pe_v = do("tensor", "pe", lambda e, j=j, pb=pb: e.matmul(
                        out=bank(pb + j), lhsT=ones1[0:1, :],
                        rhs=bo_row2[0:1, j * 512:(j + 1) * 512],
                        start=False, stop=True))
                w("vector", "pe", pe_v)
                w("vector", "dv", state.get(("x_guard", tt), 0))
                ev = do("vector", "dv", lambda e, tt=tt, pb=pb: e.tensor_add(
                    out=x_sb[:, tt * D:(tt + 1) * D],
                    in0=x_sb[:, tt * D:(tt + 1) * D], in1=bank(pb, 4)))
                state[("opev", tt % 2)] = ev
                state[("x_guard", tt)] = ev
            state["sub_done_dv"] = C["dv"]
            state["sub_done_pe"] = C["pe"]
            state["sub_done_po"] = C["po"]

        if 4 in sublayers:
            with scope("ffn2"):
                ffn_sublayer(x_ready=state.get("sub_done_dv", 0),
                             arena_pe=state.get("sub_done_pe", 0),
                             scratch_po=state.get("sub_done_po", 0))
            state["sub_done_dv"] = C["dv"]
            state["sub_done_pe"] = C["pe"]
            state["sub_done_po"] = C["po"]

        xfin = C["dv"]
        for i in range(TI):
            w("sync", "dv", xfin)
            do("sync", "qY", lambda e, i=i: e.dma_start(
                out=dY[i * 128:(i + 1) * 128, :], in_=x_sb[:, i * D:(i + 1) * D]), 16)
        w("sync", "qY", C["qY"])

    def mk(name):
        def f(eng):
            emit(name, eng)
        return f

    block.sync(mk("sync"))
    block.tensor(mk("tensor"))
    block.vector(mk("vector"))
    block.scalar(mk("scalar"))
    block.gpsimd(mk("gpsimd"))

    stack.close()
    return nc


def prep_inputs(inputs, pc_list=None):
    q = np.asarray(inputs["L0_attn_a_query"], dtype=np.float32)
    kvk = np.asarray(inputs["L0_attn_a_kv_k"], dtype=np.float32).reshape(KVLEN, D)
    kvv = np.asarray(inputs["L0_attn_a_kv_v"], dtype=np.float32).reshape(KVLEN, D)
    kvkT = np.ascontiguousarray(kvk.T)
    kvv_b = np.ascontiguousarray(
        kvv.reshape(32, 128, 16, 128).transpose(2, 1, 0, 3).reshape(NH * 128, KVLEN))
    wqaT = np.ascontiguousarray(np.asarray(inputs["L0_attn_a_Wq"], np.float32).T)
    woaT = np.ascontiguousarray(np.asarray(inputs["L0_attn_a_Wo"], np.float32).T)
    wqb = np.ascontiguousarray(np.asarray(inputs["L0_attn_b_Wq"], np.float32))
    wkb = np.ascontiguousarray(np.asarray(inputs["L0_attn_b_Wk"], np.float32))
    wvb = np.ascontiguousarray(np.asarray(inputs["L0_attn_b_Wv"], np.float32))
    wob = np.ascontiguousarray(np.asarray(inputs["L0_attn_b_Wo"], np.float32))
    bq = np.ascontiguousarray(np.asarray(inputs["L0_attn_b_bq"], np.float32).reshape(16, 128).T)
    bk = np.ascontiguousarray(np.asarray(inputs["L0_attn_b_bk"], np.float32).reshape(16, 128).T)
    bv = np.ascontiguousarray(np.asarray(inputs["L0_attn_b_bv"], np.float32).reshape(16, 128).T)
    bo = np.asarray(inputs["L0_attn_b_bo"], np.float32).reshape(1, D)
    lng = np.ascontiguousarray(np.asarray(inputs["L0_attn_b_ln_g"], np.float32).reshape(16, 128).T)
    lnb = np.ascontiguousarray(np.asarray(inputs["L0_attn_b_ln_b"], np.float32).reshape(16, 128).T)
    wg = np.asarray(inputs["L0_ffn_w_gate"], np.float32)
    wu = np.asarray(inputs["L0_ffn_w_up"], np.float32)
    wd = np.asarray(inputs["L0_ffn_w_down"], np.float32)
    wgT_r = np.ascontiguousarray(
        wg.reshape(64, 128, 16, 128).transpose(0, 3, 2, 1).reshape(FH, D))
    wuT_r = np.ascontiguousarray(
        wu.reshape(64, 128, 16, 128).transpose(0, 3, 2, 1).reshape(FH, D))
    wdT_r = np.ascontiguousarray(wd.T)
    ident = np.eye(128, dtype=BF)
    ones1 = np.ones((1, 128), dtype=BF)
    onesM = np.ones((128, 128), dtype=BF)
    triu = (np.arange(128)[None, :] >= np.arange(128)[:, None]).astype(BF)
    one = np.ones((128, 128), dtype=BF)
    zer = np.zeros((128, 128), dtype=BF)

    def cat(a, b):
        return np.ascontiguousarray(np.concatenate([a, b], axis=1))

    in_maps = []
    for c in range(8):
        b, pc = c // 2, c % 2
        x0 = np.ascontiguousarray(
            q[b].reshape(16, 128, D)[pc::2].reshape(TI * 128, D))
        if pc == 0:
            mm = [cat(triu, one), cat(zer, one), cat(zer, triu), cat(zer, zer)]
        else:
            mm = [cat(one, one), cat(triu, one), cat(zer, one), cat(zer, triu)]
        in_maps.append(dict(
            x0=x0, kvkT=kvkT, kvv_b=kvv_b, wqaT=wqaT, woaT=woaT,
            wqb=wqb, wkb=wkb, wvb=wvb, wob=wob,
            bqb=bq, bkb=bk, bvb=bv, bob=bo, lng=lng, lnb=lnb,
            wgT_r=wgT_r, wuT_r=wuT_r, wdT_r=wdT_r,
            mulm0=mm[0], mulm1=mm[1], mulm2=mm[2], mulm3=mm[3],
            onesM=onesM, ident=ident, ones1=ones1,
        ))
    return in_maps


def kernel(**inputs):
    cfg = default_cfg()
    cfg["sublayers"] = (1, 2, 3, 4)
    nc = build_program(cfg)
    in_maps = prep_inputs(inputs)
    res = run_bass_kernel_spmd(nc, in_maps, core_ids=list(range(8)))
    out = np.zeros((4, 2048, D), dtype=np.float32)
    for c in range(8):
        b, pc = c // 2, c % 2
        y = res.results[c]["y"].reshape(TI, 128, D)
        out[b].reshape(16, 128, D)[pc::2] = y
    return out
